# version 02
# baseline (speedup 1.0000x reference)
import math
import numpy as np
from contextlib import ExitStack

import concourse.bass as bass
import concourse.mybir as mybir
from concourse.bass_utils import run_bass_kernel_spmd
from concourse.bass_core_rust import IndirectOffsetOnAxis
import ml_dtypes

F32 = mybir.dt.float32
BF16 = mybir.dt.bfloat16
I32 = mybir.dt.int32
AF = mybir.ActivationFunctionType
ALU = mybir.AluOpType
AX = mybir.AxisListType

N_CORES = 8
B_ALL, T, D = 32, 2048, 1024
B_LOC = B_ALL // N_CORES
TOK = B_LOC * T
NTOK_ALL = B_ALL * T
DFF = 4096
H0 = 64
NH, DH = 16, 64
NE = 8
FEXP = 2048
CAP = int(math.ceil(2 * NTOK_ALL / NE * 1.25))
LN_EPS = 1e-5
RMS_EPS = 1e-8

def ceil_div(a, b):
    return (a + b - 1) // b


ENGINES = ("sync", "tensor", "vector", "scalar", "gpsimd")
COMPUTE_ENGINES = {"tensor", "vector", "scalar"}


class Prog:

    def __init__(self, nc, es, streams=()):
        self.nc = nc
        self.es = es
        self.ops = {e: [] for e in ENGINES}
        self.keys = list(ENGINES) + ["dma_" + s for s in streams]
        self.count = {k: 0 for k in self.keys}
        self.sem = {k: es.enter_context(nc.semaphore(f"prog_{k}")) for k in self.keys}
        self.writer = {}
        self.readers = {}
        self.base_waits = []

    def emit(self, eng, fn, reads=(), writes=(), waits=(), dma=False, stream=None):
        inc = 16 if dma else 1
        if dma:
            assert stream is not None, "DMA emits must name a stream"
            key = "dma_" + stream
        else:
            key = eng
        deps = list(waits) + self.base_waits
        for r in reads:
            w = self.writer.get(r)
            if w is not None:
                deps.append(w)
        for r in writes:
            w = self.writer.get(r)
            if w is not None:
                deps.append(w)
            deps.extend(self.readers.get(r, ()))
        best = {}
        for (de, dv) in deps:
            if best.get(de, -1) < dv:
                best[de] = dv
        if key in best and best[key] > self.count[key]:
            raise RuntimeError(f"self-dep in future on {key}")
        for k in list(best):
            if k.startswith("dma_") and best[k] > 0:
                best[k] = self.count[k]
        toklist = sorted(best.items())
        self.ops[eng].append((toklist, fn, inc, key))
        self.count[key] += inc
        tok = (key, self.count[key])
        for r in writes:
            self.writer[r] = tok
            self.readers[r] = []
        for r in reads:
            self.readers.setdefault(r, []).append(tok)
        return tok

    def barrier_token(self):
        return [(k, self.count[k]) for k in self.keys if self.count[k] > 0]

    def finalize(self, blk):
        semmap = self.sem
        for e in ENGINES:
            ops = self.ops[e]
            if not ops:
                continue

            def make_body(ops):
                def body(engobj):
                    for waits, fn, inc, key in ops:
                        for (we, wv) in waits:
                            engobj.wait_ge(semmap[we], wv)
                        fn(engobj).then_inc(semmap[key], inc)
                return body
            getattr(blk, e)(make_body(ops))



def build_launch1():
    nc = bass.Bass("TRN2", target_bir_lowering=False, debug=False)
    x_d = nc.dram_tensor("x", [TOK, D], F32, kind="ExternalInput").ap()
    wq_d = nc.dram_tensor("wq", [D, H0], F32, kind="ExternalInput").ap()
    wk_d = nc.dram_tensor("wk", [D, H0], F32, kind="ExternalInput").ap()
    wv_d = nc.dram_tensor("wv", [D, H0], F32, kind="ExternalInput").ap()
    wo_d = nc.dram_tensor("wo", [H0, D], F32, kind="ExternalInput").ap()
    wup_d = nc.dram_tensor("wup", [DFF, D], F32, kind="ExternalInput").ap()
    wdn_d = nc.dram_tensor("wdn", [D, DFF], F32, kind="ExternalInput").ap()
    wg_d = nc.dram_tensor("wg", [D, NE], F32, kind="ExternalInput").ap()

    x1_d = nc.dram_tensor("x1", [TOK, D], F32, kind="ExternalOutput").ap()
    xn1_d = nc.dram_tensor("xn1", [TOK, D], BF16, kind="ExternalOutput").ap()
    lgt_d = nc.dram_tensor("lgt", [TOK, NE], F32, kind="ExternalOutput").ap()

    w1t_d = nc.dram_tensor("w1t", [DFF // 128, D // 128, 128, 128], BF16).ap()
    w2t_d = nc.dram_tensor("w2t", [D // 128, DFF // 128, 128, 128], BF16).ap()

    es = ExitStack()
    P = Prog(nc, es, streams=("wsmall", "wraw", "wtst", "xin", "w1", "w2", "ost"))

    def sb(name, shape, dtype):
        return es.enter_context(nc.sbuf_tensor(name, shape, dtype))

    def psum(name, shape, dtype):
        return es.enter_context(nc.psum_tensor(name, shape, dtype))

    ones_f = sb("ones_f", [128, 128], F32)
    ones_b = sb("ones_b", [128, 128], BF16)
    ident_b = sb("ident_b", [128, 128], BF16)
    ident_f = sb("ident_f", [128, 128], F32)
    ultri_f = sb("ultri_f", [128, 128], F32)
    onecol_b = sb("onecol_b", [128, 1], BF16)

    qw_s = sb("qw_s", [128, 8 * H0], BF16)
    kw_s = sb("kw_s", [128, 8 * H0], BF16)
    vw_s = sb("vw_s", [128, 8 * H0], BF16)
    ow_s = sb("ow_s", [H0, D], BF16)
    gw_s = sb("gw_s", [128, 8 * NE], F32)


    xrow = sb("xrow", [128, 2 * D], F32)
    xsq = sb("xsq", [128, D], F32)
    s_a = sb("s_a", [128, 1], F32)
    s_b = sb("s_b", [128, 1], F32)
    mu = sb("mu", [128, 1], F32)
    var = sb("var", [128, 1], F32)
    rstd = sb("rstd", [128, 1], F32)
    xn_tok = sb("xn_tok", [128, 2 * D], BF16)
    xnT = sb("xnT", [128, 8 * 512], BF16)
    xnT3 = xnT[:].rearrange("p (c t) -> p c t", c=8)

    qT_s = sb("qT_s", [H0, T], BF16)
    kT_s = sb("kT_s", [H0, T], BF16)
    vT = sb("vT", [H0, T], BF16)
    oT = sb("oT", [H0, T], BF16)
    phi_a = sb("phi_a", [128, 512], F32)
    phi_b = sb("phi_b", [128, 512], F32)
    ktm = sb("ktm", [128, 2 * H0], BF16)
    vtm = sb("vtm", [128, 2 * H0], BF16)
    ktm_f = sb("ktm_f", [128, 128], F32)
    wtm_b = sb("wtm_b", [128, 128], BF16)
    rd_row = sb("rd_row", [1, 128], F32)
    qs_b = sb("qs_b", [H0, 128], BF16)
    st_f = sb("st_f", [H0, 4 * (H0 + 1)], F32)
    st_b = sb("st_b", [H0, H0 + 1], BF16)

    w1sb = sb("w1sb", [128, 2 * 8 * 128], BF16)
    w2sb = sb("w2sb", [128, 2 * 16 * 128], BF16)
    xnormT = sb("xnormT", [128, 8 * 512], BF16)
    hbuf = sb("hbuf", [128, 32 * 512], BF16)
    hsq = sb("hsq", [128, 2 * 512], BF16)
    t1s = sb("t1s", [128, 512], F32)
    t2s = sb("t2s", [128, 512], F32)
    sum_fm = sb("sum_fm", [128, 8 * 512], F32)
    x1tok = sb("x1tok", [128, D], F32)
    xn1tok_f = sb("xn1tok_f", [128, D], F32)
    xn1tok_b = sb("xn1tok_b", [128, D], BF16)
    xn1T_f = sb("xn1T_f", [128, 128], F32)
    lgt_row = sb("lgt_row", [8, 128], F32)
    lgt_f = sb("lgt_f", [128, NE], F32)

    esA = ExitStack()

    def sbA(name, shape, dtype):
        return esA.enter_context(nc.sbuf_tensor(name, shape, dtype))

    wld = sbA("wld", [128, DFF], F32)
    wabs = sbA("wabs", [128, 1], F32)
    wsgn = sbA("wsgn", [128, DFF], F32)
    wtb = sbA("wtb", [128, DFF], BF16)
    alpha_acc = sbA("alpha_acc", [1, 2], F32)
    tau_col = sbA("tau_col", [128, 1], F32)
    tr_ev = sbA("tr_ev", [128, 4 * 128], BF16)

    PA = psum("PA", [128, 512], F32)
    PB = psum("PB", [128, 512], F32)
    PC = psum("PC", [128, 512], F32)
    PD = psum("PD", [128, 1024], F32)
    PEb = psum("PEb", [128, 1024], BF16)
    PX = psum("PX", [128, 512], F32)
    PY = psum("PY", [128, 512], F32)

    blk = es.enter_context(nc.Block())

    P.emit("vector", lambda v: v.memset(ones_f[:], 1.0), writes=["ones_f"])
    P.emit("vector", lambda v: v.memset(ones_b[:], 1.0), writes=["ones_b"])
    P.emit("vector", lambda v: v.memset(onecol_b[:], 1.0), writes=["onecol_b"])
    P.emit("gpsimd", lambda g: g.affine_select(
        ident_b[:], ones_b[:], [[-1, 128]], ALU.is_equal, 0.0, channel_multiplier=1),
        reads=["ones_b"], writes=["ident_b"])
    P.emit("gpsimd", lambda g: g.affine_select(
        ident_f[:], ones_f[:], [[-1, 128]], ALU.is_equal, 0.0, channel_multiplier=1),
        reads=["ones_f"], writes=["ident_f"])
    P.emit("gpsimd", lambda g: g.affine_select(
        ultri_f[:], ones_f[:], [[1, 128]], ALU.is_ge, 0.0, channel_multiplier=-1),
        reads=["ones_f"], writes=["ultri_f"])

    wq3 = wq_d.rearrange("(c p) h -> p c h", p=128)
    wk3 = wk_d.rearrange("(c p) h -> p c h", p=128)
    wv3 = wv_d.rearrange("(c p) h -> p c h", p=128)
    wg3 = wg_d.rearrange("(c p) e -> p c e", p=128)
    P.emit("gpsimd", lambda g: g.dma_start(
        out=qw_s[:].rearrange("p (c h) -> p c h", c=8), in_=wq3), writes=["qw_s"], dma=True, stream="wsmall")
    P.emit("gpsimd", lambda g: g.dma_start(
        out=kw_s[:].rearrange("p (c h) -> p c h", c=8), in_=wk3), writes=["kw_s"], dma=True, stream="wsmall")
    P.emit("gpsimd", lambda g: g.dma_start(
        out=vw_s[:].rearrange("p (c h) -> p c h", c=8), in_=wv3), writes=["vw_s"], dma=True, stream="wsmall")
    P.emit("gpsimd", lambda g: g.dma_start(out=ow_s[:], in_=wo_d[:, :]), writes=["ow_s"], dma=True, stream="wsmall")
    P.emit("gpsimd", lambda g: g.dma_start(
        out=gw_s[:].rearrange("p (c e) -> p c e", c=8), in_=wg3), writes=["gw_s"], dma=True, stream="wsmall")

    def tern_alpha(w_d, nrows, ncols, slot):
        row_tiles = nrows // 128
        for r in range(row_tiles):
            P.emit("sync", lambda s, r=r: s.dma_start(
                out=wld[:, 0:ncols], in_=w_d[r * 128:(r + 1) * 128, :]),
                writes=["wld"], dma=True, stream="wraw")
            P.emit("vector", lambda v: v.tensor_reduce(
                out=wabs[:], in_=wld[:, 0:ncols], axis=AX.X, op=ALU.add,
                apply_absolute_value=True), reads=["wld"], writes=["wabs"])
            P.emit("tensor", lambda t, r=r: t.matmul(
                PY[0:1, slot:slot + 1], ones_f[:, 0:1], wabs[:],
                start=(r == 0), stop=(r == row_tiles - 1)),
                reads=["wabs", "ones_f"], writes=["PY"])
        P.emit("vector", lambda v: v.tensor_copy(
            out=alpha_acc[0:1, slot:slot + 1], in_=PY[0:1, slot:slot + 1]),
            reads=["PY"], writes=["alpha_acc"])
        P.emit("tensor", lambda t: t.matmul(
            PY[0:128, 256:257], ones_f[0:1, :], alpha_acc[0:1, slot:slot + 1],
            start=True, stop=True),
            reads=["alpha_acc"], writes=["PY"])
        P.emit("vector", lambda v: v.tensor_scalar_mul(
            out=tau_col[:], in0=PY[0:128, 256:257], scalar1=0.5 / (nrows * ncols)),
            reads=["PY"], writes=["tau_col"])

    def tern_pass2(w_d, nrows, ncols, wt_out_d):
        row_tiles = nrows // 128
        col_chunks = ncols // 128
        for rb in range(row_tiles):
            P.emit("sync", lambda s, rb=rb: s.dma_start(
                out=wld[:, 0:ncols], in_=w_d[rb * 128:(rb + 1) * 128, :]),
                writes=["wld"], dma=True, stream="wraw")
            P.emit("scalar", lambda sc: sc.activation(
                out=wsgn[:, 0:ncols], in_=wld[:, 0:ncols], func=AF.Sign),
                reads=["wld"], writes=["wsgn"])
            P.emit("scalar", lambda sc: sc.activation(
                out=wld[:, 0:ncols], in_=wld[:, 0:ncols], func=AF.Abs),
                reads=["wld"], writes=["wld"])
            P.emit("vector", lambda v: v.tensor_scalar(
                out=wld[:, 0:ncols], in0=wld[:, 0:ncols], scalar1=tau_col[:, 0:1],
                scalar2=None, op0=ALU.is_gt),
                reads=["wld", "tau_col"], writes=["wld"])
            P.emit("vector", lambda v: v.tensor_tensor(
                out=wtb[:, 0:ncols], in0=wsgn[:, 0:ncols], in1=wld[:, 0:ncols],
                op=ALU.mult), reads=["wsgn", "wld"], writes=["wtb"])
            for cb in range(col_chunks):
                half = cb % 8
                tslot = cb % 4
                P.emit("tensor", lambda t, cb=cb, half=half: t.transpose(
                    PEb[:, half * 128:(half + 1) * 128],
                    wtb[:, cb * 128:(cb + 1) * 128], ident_b[:]),
                    reads=["wtb", "ident_b"], writes=["PEb"])
                P.emit("vector", lambda v, half=half, tslot=tslot: v.tensor_copy(
                    out=tr_ev[:, tslot * 128:(tslot + 1) * 128],
                    in_=PEb[:, half * 128:(half + 1) * 128]),
                    reads=["PEb"], writes=[f"trev{tslot}"])
                P.emit("gpsimd", lambda g, rb=rb, cb=cb, tslot=tslot: g.dma_start(
                    out=wt_out_d[rb, cb, :, :],
                    in_=tr_ev[:, tslot * 128:(tslot + 1) * 128]),
                    reads=[f"trev{tslot}"], dma=True, stream="wtst")

    tern_alpha(wup_d, DFF, D, 0)
    tern_pass2(wup_d, DFF, D, w1t_d)
    tern_alpha(wdn_d, D, DFF, 1)
    tern_pass2(wdn_d, D, DFF, w2t_d)
    esA.close()
    P.base_waits = P.barrier_token()

    NT = T // 128
    NG = T // 512

    def layernorm_tile(src_res, src_ap, dst_ap, dst_res, out_dtype_pair=None):
        P.emit("vector", lambda v: v.tensor_reduce(
            out=s_a[:], in_=src_ap, axis=AX.X, op=ALU.add),
            reads=[src_res], writes=["s_a"])
        P.emit("scalar", lambda sc: sc.activation(
            out=xsq[:], in_=src_ap, func=AF.Square), reads=[src_res], writes=["xsq"])
        P.emit("vector", lambda v: v.tensor_reduce(
            out=s_b[:], in_=xsq[:], axis=AX.X, op=ALU.add),
            reads=["xsq"], writes=["s_b"])
        P.emit("vector", lambda v: v.tensor_scalar_mul(
            out=mu[:], in0=s_a[:], scalar1=1.0 / D), reads=["s_a"], writes=["mu"])
        P.emit("vector", lambda v: v.tensor_scalar_mul(
            out=var[:], in0=s_b[:], scalar1=1.0 / D), reads=["s_b"], writes=["var"])
        P.emit("vector", lambda v: v.tensor_tensor(
            out=s_b[:], in0=mu[:], in1=mu[:], op=ALU.mult),
            reads=["mu"], writes=["s_b"])
        P.emit("vector", lambda v: v.tensor_tensor(
            out=var[:], in0=var[:], in1=s_b[:], op=ALU.subtract),
            reads=["var", "s_b"], writes=["var"])
        P.emit("vector", lambda v: v.tensor_scalar_add(
            out=var[:], in0=var[:], scalar1=LN_EPS), reads=["var"], writes=["var"])
        P.emit("scalar", lambda sc: sc.activation(
            out=rstd[:], in_=var[:], func=AF.Sqrt),
            reads=["var"], writes=["rstd"])
        P.emit("vector", lambda v: v.reciprocal(out=rstd[:], in_=rstd[:]),
            reads=["rstd"], writes=["rstd"])
        P.emit("vector", lambda v: v.tensor_scalar(
            out=dst_ap, in0=src_ap, scalar1=mu[:, 0:1], scalar2=rstd[:, 0:1],
            op0=ALU.subtract, op1=ALU.mult),
            reads=[src_res, "mu", "rstd"], writes=[dst_res])

    for b in range(B_LOC):
        row0 = b * T

        P.emit("vector", lambda v, b=b: v.memset(
            st_f[:, b * (H0 + 1):(b + 1) * (H0 + 1)], 0.0), writes=["st_f"])
        P.emit("vector", lambda v: v.memset(st_b[:], 0.0), writes=["st_b"])
        stf = st_f[:, b * (H0 + 1):(b + 1) * (H0 + 1)]

        for g in range(NG):
            tslc = slice(g * 512, (g + 1) * 512)
            def emit_xload(nl):
                n = g * 4 + nl
                slot = nl % 2
                xr = xrow[:, slot * D:(slot + 1) * D]
                P.emit("sync", lambda s, n=n, xr=xr, row0=row0: s.dma_start(
                    out=xr, in_=x_d[row0 + n * 128:row0 + (n + 1) * 128, :]),
                    writes=[f"xrow{slot}"], dma=True, stream="xin")
            emit_xload(0)
            for nl in range(4):
                n = g * 4 + nl
                slot = nl % 2
                xr = xrow[:, slot * D:(slot + 1) * D]
                xnt = xn_tok[:, slot * D:(slot + 1) * D]
                if nl + 1 < 4:
                    emit_xload(nl + 1)
                layernorm_tile(f"xrow{slot}", xr, xnt, f"xn{slot}")
                for c in range(8):
                    P.emit("tensor", lambda t, c=c, xnt=xnt, slot=slot: t.transpose(
                        PEb[:, c * 128:(c + 1) * 128],
                        xnt[:, c * 128:(c + 1) * 128], ident_b[:]),
                        reads=[f"xn{slot}", "ident_b", "xnT"], writes=["PEb"])
                P.emit("vector", lambda v, nl=nl: v.tensor_copy(
                    out=xnT3[:, :, nl * 128:(nl + 1) * 128],
                    in_=PEb[:].rearrange("p (c t) -> p c t", c=8)),
                    reads=["PEb"], writes=["xnT"])

            for c in range(8):
                P.emit("tensor", lambda t, c=c: t.matmul(
                    PA[0:64, 0:512], qw_s[:, c * H0:(c + 1) * H0],
                    xnT3[:, c, :], start=(c == 0), stop=(c == 7)),
                    reads=["qw_s", "xnT"], writes=["PA"])
            for c in range(8):
                P.emit("tensor", lambda t, c=c: t.matmul(
                    PB[0:64, 0:512], kw_s[:, c * H0:(c + 1) * H0],
                    xnT3[:, c, :], start=(c == 0), stop=(c == 7)),
                    reads=["kw_s", "xnT"], writes=["PB"])
            for c in range(8):
                P.emit("tensor", lambda t, c=c: t.matmul(
                    PC[0:64, 0:512], vw_s[:, c * H0:(c + 1) * H0],
                    xnT3[:, c, :], start=(c == 0), stop=(c == 7)),
                    reads=["vw_s", "xnT"], writes=["PC"])
            for (pp, pres, dst, dres) in (
                    (PA, "PA", qT_s, "qT"), (PB, "PB", kT_s, "kT")):
                P.emit("vector", lambda v, pp=pp: v.tensor_scalar_min(
                    out=phi_a[0:64, :], in0=pp[0:64, 0:512], scalar1=0.0),
                    reads=[pres], writes=["phi_a"])
                P.emit("scalar", lambda sc: sc.activation(
                    out=phi_a[0:64, :], in_=phi_a[0:64, :], func=AF.Exp),
                    reads=["phi_a"], writes=["phi_a"])
                P.emit("vector", lambda v, pp=pp: v.tensor_scalar_max(
                    out=phi_b[0:64, :], in0=pp[0:64, 0:512], scalar1=0.0),
                    reads=[pres], writes=["phi_b"])
                P.emit("vector", lambda v, tslc=tslc, dst=dst: v.tensor_tensor(
                    out=dst[:, tslc], in0=phi_a[0:64, :], in1=phi_b[0:64, :],
                    op=ALU.add),
                    reads=["phi_a", "phi_b"], writes=[dres])
            P.emit("vector", lambda v, tslc=tslc: v.tensor_copy(
                out=vT[:, tslc], in_=PC[0:64, 0:512]),
                reads=["PC"], writes=["vT"])

            for i in range(4 * g, 4 * g + 4):
                t0 = i * 128
                cs = slice(t0, t0 + 128)
                sl = i % 2
                P.emit("tensor", lambda t, cs=cs, sl=sl: t.transpose(
                    PEb[:, 0:64], kT_s[:, cs], ident_b[0:64, 0:64]),
                    reads=["kT", "ident_b", "ktm" + str(sl)], writes=["PEb"])
                P.emit("vector", lambda v, sl=sl: v.tensor_copy(
                    out=ktm[:, sl * H0:(sl + 1) * H0], in_=PEb[0:128, 0:64]),
                    reads=["PEb"], writes=["ktm" + str(sl)])
                P.emit("tensor", lambda t, cs=cs, sl=sl: t.transpose(
                    PEb[:, 64:128], vT[0:64, cs], ident_b[0:64, 0:64]),
                    reads=["vT", "ident_b", "vtm" + str(sl)], writes=["PEb"])
                P.emit("vector", lambda v, sl=sl: v.tensor_copy(
                    out=vtm[:, sl * H0:(sl + 1) * H0], in_=PEb[0:128, 64:128]),
                    reads=["PEb"], writes=["vtm" + str(sl)])
                ktm_s = ktm[:, sl * H0:(sl + 1) * H0]
                vtm_s = vtm[:, sl * H0:(sl + 1) * H0]
                P.emit("tensor", lambda t, cs=cs: t.matmul(
                    PA[:, 0:128], kT_s[:, cs], qT_s[:, cs], start=True, stop=True),
                    reads=["qT", "kT"], writes=["PA"])
                P.emit("vector", lambda v: v.tensor_tensor(
                    out=ktm_f[:], in0=PA[:, 0:128], in1=ultri_f[:], op=ALU.mult),
                    reads=["PA", "ultri_f"], writes=["ktm_f"])
                P.emit("tensor", lambda t: t.matmul(
                    PC[0:1, 0:128], ones_f[:, 0:1], ktm_f[:], start=True, stop=False),
                    reads=["ktm_f", "ones_f"], writes=["PC"])
                P.emit("tensor", lambda t, cs=cs: t.matmul(
                    PC[0:1, 0:128], st_b[:, H0:H0 + 1], qT_s[:, cs],
                    start=False, stop=True),
                    reads=["st_b", "qT"], writes=["PC"])
                P.emit("vector", lambda v: v.tensor_scalar_max(
                    out=rd_row[:], in0=PC[0:1, 0:128], scalar1=1e-6),
                    reads=["PC"], writes=["rd_row"])
                P.emit("vector", lambda v: v.reciprocal(out=rd_row[:], in_=rd_row[:]),
                    reads=["rd_row"], writes=["rd_row"])
                P.emit("tensor", lambda t: t.matmul(
                    PX[0:64, 0:128], ones_f[0:1, 0:64], rd_row[:], start=True, stop=True),
                    reads=["rd_row", "ones_f"], writes=["PX"])
                P.emit("vector", lambda v, cs=cs: v.tensor_tensor(
                    out=qs_b[:], in0=qT_s[:, cs], in1=PX[0:64, 0:128], op=ALU.mult),
                    reads=["qT", "PX"], writes=["qs_b"])
                P.emit("tensor", lambda t, cs=cs: t.matmul(
                    PA[:, 128:256], vT[0:64, cs], qs_b[:], start=True, stop=True),
                    reads=["vT", "qs_b"], writes=["PA"])
                P.emit("vector", lambda v: v.tensor_tensor(
                    out=wtm_b[:], in0=PA[:, 128:256], in1=ultri_f[:], op=ALU.mult),
                    reads=["PA", "ultri_f"], writes=["wtm_b"])
                P.emit("tensor", lambda t: t.matmul(
                    PB[0:64, 0:128], st_b[:, 0:H0], qs_b[:], start=True, stop=False),
                    reads=["st_b", "qs_b"], writes=["PB"])
                P.emit("tensor", lambda t, ktm_s=ktm_s, sl=sl: t.matmul(
                    PB[0:64, 0:128], ktm_s, wtm_b[:], start=False, stop=True),
                    reads=["wtm_b", "ktm" + str(sl)], writes=["PB"])
                P.emit("vector", lambda v, cs=cs: v.tensor_copy(
                    out=oT[:, cs], in_=PB[0:64, 0:128]), reads=["PB"], writes=["oT"])
                P.emit("tensor", lambda t, vtm_s=vtm_s, ktm_s=ktm_s, sl=sl: t.matmul(
                    PX[0:64, 128:192], vtm_s, ktm_s, start=True, stop=True),
                    reads=["ktm" + str(sl), "vtm" + str(sl)], writes=["PX"])
                P.emit("tensor", lambda t, ktm_s=ktm_s, sl=sl: t.matmul(
                    PX[0:64, 192:193], ktm_s, onecol_b[:], start=True, stop=True),
                    reads=["onecol_b", "ktm" + str(sl)], writes=["PX"])
                P.emit("vector", lambda v, stf=stf: v.tensor_tensor(
                    out=stf, in0=stf, in1=PX[0:64, 128:193], op=ALU.add),
                    reads=["PX", "st_f"], writes=["st_f"])
                P.emit("vector", lambda v, stf=stf: v.tensor_copy(out=st_b[:], in_=stf),
                    reads=["st_f"], writes=["st_b"])

            for c in range(8):
                hs = c % 2
                hq = hsq[:, hs * 512:(hs + 1) * 512]
                P.emit("scalar", lambda sc, c=c, hq=hq: sc.activation(
                    out=hq, in_=xnT3[:, c, :], func=AF.Square),
                    reads=["xnT"], writes=[f"hsq{hs}"])
                P.emit("tensor", lambda t, c=c, hq=hq: t.matmul(
                    PX[:, 0:512], ones_b[:], hq, start=(c == 0), stop=(c == 7)),
                    reads=[f"hsq{hs}", "ones_b"], writes=["PX"])
            P.emit("scalar", lambda sc: sc.activation(
                out=t1s[:], in_=PX[:, 0:512], func=AF.Sqrt, scale=1.0 / D),
                reads=["PX"], writes=["t1s"])
            P.emit("vector", lambda v: v.tensor_scalar_add(
                out=t1s[:], in0=t1s[:], scalar1=RMS_EPS), reads=["t1s"], writes=["t1s"])
            P.emit("vector", lambda v: v.reciprocal(out=t1s[:], in_=t1s[:]),
                reads=["t1s"], writes=["t1s"])
            for c in range(8):
                P.emit("vector", lambda v, c=c: v.tensor_tensor(
                    out=xnormT[:, c * 512:(c + 1) * 512], in0=xnT3[:, c, :],
                    in1=t1s[:], op=ALU.mult),
                    reads=["xnT", "t1s"], writes=["xnormT"])

            def emit_w1load(fb):
                wsl = fb % 2
                P.emit("sync", lambda s, fb=fb, wsl=wsl: s.dma_start(
                    out=w1sb[:, wsl * 1024:(wsl + 1) * 1024].rearrange(
                        "p (c f) -> p c f", c=8),
                    in_=w1t_d[fb].rearrange("c p f -> p c f")),
                    writes=[f"w1sb{wsl}"], dma=True, stream="w1")
            emit_w1load(0)
            for fb in range(32):
                wsl = fb % 2
                if fb + 1 < 32:
                    emit_w1load(fb + 1)
                ppy = PA if fb % 2 == 0 else PB
                ppy_res = "PA" if fb % 2 == 0 else "PB"
                for c in range(8):
                    P.emit("tensor", lambda t, fb=fb, c=c, wsl=wsl, ppy=ppy: t.matmul(
                        ppy[:, 0:512],
                        w1sb[:, wsl * 1024 + c * 128:wsl * 1024 + (c + 1) * 128],
                        xnormT[:, c * 512:(c + 1) * 512],
                        start=(c == 0), stop=(c == 7)),
                        reads=[f"w1sb{wsl}", "xnormT"], writes=[ppy_res])
                P.emit("scalar", lambda sc, fb=fb, ppy=ppy: sc.activation(
                    out=hbuf[:, fb * 512:(fb + 1) * 512], in_=ppy[:, 0:512],
                    func=AF.Gelu), reads=[ppy_res], writes=[f"hb{fb}"])
                hs = fb % 2
                hq = hsq[:, hs * 512:(hs + 1) * 512]
                P.emit("scalar", lambda sc, fb=fb, hq=hq: sc.activation(
                    out=hq, in_=hbuf[:, fb * 512:(fb + 1) * 512], func=AF.Square),
                    reads=[f"hb{fb}"], writes=[f"hsq{hs}"])
                P.emit("tensor", lambda t, fb=fb, hq=hq: t.matmul(
                    PY[:, 0:512], ones_b[:], hq, start=(fb == 0), stop=(fb == 31)),
                    reads=[f"hsq{hs}", "ones_b"], writes=["PY"])
            P.emit("scalar", lambda sc: sc.activation(
                out=t2s[:], in_=PY[:, 0:512], func=AF.Sqrt, scale=1.0 / DFF),
                reads=["PY"], writes=["t2s"])
            P.emit("vector", lambda v: v.tensor_scalar_add(
                out=t2s[:], in0=t2s[:], scalar1=RMS_EPS), reads=["t2s"], writes=["t2s"])
            P.emit("vector", lambda v: v.reciprocal(out=t2s[:], in_=t2s[:]),
                reads=["t2s"], writes=["t2s"])

            def emit_w2load(dc, hh):
                wsl = hh
                P.emit("sync", lambda s, dc=dc, hh=hh, wsl=wsl: s.dma_start(
                    out=w2sb[:, wsl * 2048:(wsl + 1) * 2048].rearrange(
                        "p (c f) -> p c f", c=16),
                    in_=w2t_d[dc, hh * 16:(hh + 1) * 16].rearrange(
                        "c p f -> p c f")),
                    writes=[f"w2sb{wsl}"], dma=True, stream="w2")
            emit_w2load(0, 0)
            for dc in range(8):
                emit_w2load(dc, 1)
                ppz = PA if dc % 2 == 0 else PB
                ppz_res = "PA" if dc % 2 == 0 else "PB"
                for fc in range(32):
                    hh = fc // 16
                    wsl = hh
                    fcl = fc % 16
                    P.emit("tensor", lambda t, fc=fc, wsl=wsl, fcl=fcl, ppz=ppz: t.matmul(
                        ppz[:, 0:512],
                        w2sb[:, wsl * 2048 + fcl * 128:wsl * 2048 + (fcl + 1) * 128],
                        hbuf[:, fc * 512:(fc + 1) * 512],
                        start=(fc == 0), stop=(fc == 31)),
                        reads=[f"w2sb{wsl}", f"hb{fc}"], writes=[ppz_res])
                if dc + 1 < 8:
                    emit_w2load(dc + 1, 0)
                P.emit("tensor", lambda t, dc=dc, tslc=tslc: t.matmul(
                    PC[:, 0:512], ow_s[0:64, dc * 128:(dc + 1) * 128], oT[:, tslc],
                    start=True, stop=True), reads=["ow_s", "oT"], writes=["PC"])
                P.emit("vector", lambda v, dc=dc, ppz=ppz: v.tensor_tensor(
                    out=sum_fm[:, dc * 512:(dc + 1) * 512], in0=ppz[:, 0:512],
                    in1=t2s[:], op=ALU.mult),
                    reads=[ppz_res, "t2s"], writes=[f"sfm{dc}"])
                P.emit("vector", lambda v, dc=dc: v.tensor_tensor(
                    out=sum_fm[:, dc * 512:(dc + 1) * 512],
                    in0=sum_fm[:, dc * 512:(dc + 1) * 512],
                    in1=PC[:, 0:512], op=ALU.add),
                    reads=[f"sfm{dc}", "PC"], writes=[f"sfm{dc}"])

            for ts in range(4):
                rows = slice(row0 + g * 512 + ts * 128, row0 + g * 512 + ts * 128 + 128)
                for dc in range(8):
                    P.emit("tensor", lambda t, dc=dc, ts=ts: t.transpose(
                        PD[:, dc * 128:(dc + 1) * 128],
                        sum_fm[:, dc * 512 + ts * 128:dc * 512 + (ts + 1) * 128],
                        ident_f[:]),
                        reads=[f"sfm{dc}", "ident_f", "x1tok"], writes=["PD"])
                P.emit("sync", lambda s, rows=rows: s.dma_start(
                    out=xrow[:, 0:D], in_=x_d[rows, :]), writes=["xrow0"], dma=True, stream="xin")
                P.emit("vector", lambda v: v.tensor_tensor(
                    out=x1tok[:], in0=xrow[:, 0:D], in1=PD[:, 0:1024], op=ALU.add),
                    reads=["xrow0", "PD"], writes=["x1tok"])
                P.emit("gpsimd", lambda g_, rows=rows: g_.dma_start(
                    out=x1_d[rows, :], in_=x1tok[:]), reads=["x1tok"], dma=True, stream="ost")
                layernorm_tile("x1tok", x1tok[:], xn1tok_f[:], "xn1f")
                P.emit("vector", lambda v: v.tensor_copy(
                    out=xn1tok_b[:], in_=xn1tok_f[:]), reads=["xn1f"], writes=["xn1b"])
                P.emit("gpsimd", lambda g_, rows=rows: g_.dma_start(
                    out=xn1_d[rows, :], in_=xn1tok_b[:]), reads=["xn1b"], dma=True, stream="ost")
                for c in range(8):
                    P.emit("tensor", lambda t, c=c: t.transpose(
                        PX[:, 0:128], xn1tok_f[:, c * 128:(c + 1) * 128], ident_f[:]),
                        reads=["xn1f", "ident_f", "xn1T"], writes=["PX"])
                    P.emit("vector", lambda v: v.tensor_copy(
                        out=xn1T_f[:], in_=PX[:, 0:128]), reads=["PX"], writes=["xn1T"])
                    P.emit("tensor", lambda t, c=c: t.matmul(
                        PY[0:8, 0:128], gw_s[:, c * NE:(c + 1) * NE], xn1T_f[:],
                        start=(c == 0), stop=(c == 7)),
                        reads=["gw_s", "xn1T"], writes=["PY"])
                P.emit("vector", lambda v: v.tensor_copy(
                    out=lgt_row[:], in_=PY[0:8, 0:128]), reads=["PY"], writes=["lgt_row"])
                P.emit("tensor", lambda t: t.transpose(
                    PX[:, 128:136], lgt_row[:], ident_f[0:8, 0:8]),
                    reads=["lgt_row", "ident_f", "lgt_f"], writes=["PX"])
                P.emit("vector", lambda v: v.tensor_copy(
                    out=lgt_f[:], in_=PX[0:128, 128:136]), reads=["PX"], writes=["lgt_f"])
                P.emit("gpsimd", lambda g_, rows=rows: g_.dma_start(
                    out=lgt_d[rows, :], in_=lgt_f[:]), reads=["lgt_f"], dma=True, stream="ost")

    toks = P.barrier_token()
    P.emit("vector", lambda v: v.engine_nop(), waits=toks)

    P.finalize(blk)
    es.close()
    return nc





def build_launch2(core_id_unused=None):
    nc = bass.Bass("TRN2", target_bir_lowering=False, debug=False)
    x1_d = nc.dram_tensor("x1", [TOK, D], F32, kind="ExternalInput").ap()
    xn1_d = nc.dram_tensor("xn1", [TOK, D], BF16, kind="ExternalInput").ap()
    lgt_d = nc.dram_tensor("lgt", [NTOK_ALL, NE], F32, kind="ExternalInput").ap()
    w1_d = nc.dram_tensor("w1", [D, 3 * D], F32, kind="ExternalInput").ap()
    b1_d = nc.dram_tensor("b1", [3 * D], F32, kind="ExternalInput").ap()
    wp_d = nc.dram_tensor("wp", [D, D], F32, kind="ExternalInput").ap()
    bp_d = nc.dram_tensor("bp", [D], F32, kind="ExternalInput").ap()
    wi_d = nc.dram_tensor("wi", [NE, D, 2 * FEXP], F32, kind="ExternalInput").ap()
    wo_d = nc.dram_tensor("wo", [NE, FEXP, D], F32, kind="ExternalInput").ap()
    out_d = nc.dram_tensor("out", [TOK, D], F32, kind="ExternalOutput").ap()

    NPOWN = TOK // 512
    coef_d = nc.dram_tensor("coefsc", [NE, NPOWN, 512], F32).ap()
    moe_d = nc.dram_tensor("moesc", [8, 128, TOK], F32).ap()

    es = ExitStack()
    P = Prog(nc, es, streams=(
        "wsmall", "wsmall2", "lgt", "xt", "mz", "wi", "wo2", "macc", "coef",
        "coef2", "xin", "ost", "moin"))

    def sb(name, shape, dtype):
        return es.enter_context(nc.sbuf_tensor(name, shape, dtype))

    def psum(name, shape, dtype):
        return es.enter_context(nc.psum_tensor(name, shape, dtype))

    ones_f = sb("ones_f", [128, 128], F32)
    ones_b = sb("ones_b", [128, 128], BF16)
    ident_b = sb("ident_b", [128, 128], BF16)
    ident_f = sb("ident_f", [128, 128], F32)
    ltm_f = sb("ltm_f", [128, 128], F32)
    cmask = sb("cmask", [128, 128], F32)

    P.emit("vector", lambda v: v.memset(ones_f[:], 1.0), writes=["ones_f"])
    P.emit("vector", lambda v: v.memset(ones_b[:], 1.0), writes=["ones_b"])
    P.emit("gpsimd", lambda g: g.affine_select(
        ident_b[:], ones_b[:], [[-1, 128]], ALU.is_equal, 0.0, channel_multiplier=1),
        reads=["ones_b"], writes=["ident_b"])
    P.emit("gpsimd", lambda g: g.affine_select(
        ident_f[:], ones_f[:], [[-1, 128]], ALU.is_equal, 0.0, channel_multiplier=1),
        reads=["ones_f"], writes=["ident_f"])
    P.emit("gpsimd", lambda g: g.affine_select(
        ltm_f[:], ones_f[:], [[1, 128]], ALU.is_gt, 0.0, channel_multiplier=-1),
        reads=["ones_f"], writes=["ltm_f"])
    P.emit("vector", lambda v: v.memset(cmask[:], 0.0), writes=["cmask"])
    P.emit("gpsimd", lambda g: g.affine_select(
        cmask[:], cmask[:], [[-1, 128]], ALU.is_ge, -1e6, channel_multiplier=1),
        reads=["cmask"], writes=["cmask"])

    NA = NTOK_ALL // 128
    esR = ExitStack()

    def sbR(name, shape, dtype):
        return esR.enter_context(nc.sbuf_tensor(name, shape, dtype))

    lgt_sb = sbR("lgt_sb", [128, NA * NE], F32)
    l3 = lgt_sb[:].rearrange("p (a e) -> p a e", e=NE)
    l3e = lgt_sb[:].rearrange("p (a e) -> p a e", e=NE).transpose([0, 2, 1])
    iota_e = sbR("iota_e", [128, NA * NE], F32)
    i3 = iota_e[:].rearrange("p (a e) -> p a e", e=NE)
    m1 = sbR("m1", [128, NA], F32)
    m2 = sbR("m2", [128, NA], F32)
    idx1 = sbR("idx1", [128, NA], F32)
    idx2 = sbR("idx2", [128, NA], F32)
    e2g = sbR("e2g", [128, NA], F32)
    oh1 = sbR("oh1", [128, NA * NE], F32)
    oh2 = sbR("oh2", [128, NA * NE], F32)
    o13 = oh1[:].rearrange("p (a e) -> p a e", e=NE)
    o23 = oh2[:].rearrange("p (a e) -> p a e", e=NE)
    scr = sbR("scr", [128, NA * NE], F32)
    s3 = scr[:].rearrange("p (a e) -> p a e", e=NE)
    s3e = scr[:].rearrange("p (a e) -> p a e", e=NE).transpose([0, 2, 1])
    scan1 = sbR("scan1", [128, NA * NE], F32)
    c13 = scan1[:].rearrange("p (a e) -> p a e", e=NE)
    c13e = scan1[:].rearrange("p (a e) -> p a e", e=NE).transpose([0, 2, 1])
    scan2 = sbR("scan2", [128, NA * NE], F32)
    c23 = scan2[:].rearrange("p (a e) -> p a e", e=NE)
    c23e = scan2[:].rearrange("p (a e) -> p a e", e=NE).transpose([0, 2, 1])
    cnt1 = sbR("cnt1", [128, NE], F32)
    cnt2 = sbR("cnt2", [128, NE], F32)
    base1 = sbR("base1", [128, NE], F32)
    base2 = sbR("base2", [128, NE], F32)
    tot1r = sbR("tot1r", [1, NE], F32)
    loc1 = sbR("loc1", [128, NA], F32)
    loc2 = sbR("loc2", [128, NA], F32)
    keep1 = sbR("keep1", [128, NA], F32)
    keep2 = sbR("keep2", [128, NA], F32)
    gden = sbR("gden", [128, NA], F32)
    coef = sbR("coef", [128, NA * NE], F32)
    cf3 = coef[:].rearrange("p (a e) -> p a e", e=NE)
    cf3e = coef[:].rearrange("p (a e) -> p a e", e=NE).transpose([0, 2, 1])

    esP1 = ExitStack()
    PR1 = esP1.enter_context(nc.psum_tensor("PR1", [128, 512], F32))
    PR2 = esP1.enter_context(nc.psum_tensor("PR2", [128, 512], F32))

    lgt_pm = lgt_d.rearrange("(p a) e -> p a e", p=128)
    P.emit("sync", lambda s: s.dma_start(out=l3, in_=lgt_pm),
           writes=["lgt_sb"], dma=True, stream="lgt")
    P.emit("gpsimd", lambda g: g.iota(
        iota_e[:], [[0, NA], [1, NE]], channel_multiplier=0,
        allow_small_or_imprecise_dtypes=True), writes=["iota_e"])

    BIG = 1.0e9
    P.emit("vector", lambda v: v.reduce_max(out=m1[:], in_=l3, axis=AX.X),
           reads=["lgt_sb"], writes=["m1"])
    P.emit("vector", lambda v: v.tensor_tensor(
        out=scr[:], in0=l3, in1=m1[:].rearrange("p a -> p a 1" if False else "p (a o) -> p a o", o=1).broadcast_to([128, NA, NE]),
        op=ALU.is_ge), reads=["lgt_sb", "m1"], writes=["scr"])
    P.emit("vector", lambda v: v.tensor_scalar(
        out=scr[:], in0=scr[:], scalar1=-BIG, scalar2=BIG,
        op0=ALU.mult, op1=ALU.add), reads=["scr"], writes=["scr"])
    P.emit("vector", lambda v: v.tensor_tensor(
        out=scr[:], in0=scr[:], in1=iota_e[:], op=ALU.add),
        reads=["scr", "iota_e"], writes=["scr"])
    P.emit("vector", lambda v: v.tensor_reduce(
        out=idx1[:], in_=s3, axis=AX.X, op=ALU.min),
        reads=["scr"], writes=["idx1"])
    P.emit("vector", lambda v: v.tensor_tensor(
        out=oh1[:], in0=iota_e[:],
        in1=idx1[:].rearrange("p (a o) -> p a o", o=1).broadcast_to([128, NA, NE]),
        op=ALU.is_equal), reads=["iota_e", "idx1"], writes=["oh1"])
    P.emit("vector", lambda v: v.tensor_scalar(
        out=scr[:], in0=oh1[:], scalar1=-BIG, scalar2=None, op0=ALU.mult),
        reads=["oh1"], writes=["scr"])
    P.emit("vector", lambda v: v.tensor_tensor(
        out=scr[:], in0=scr[:], in1=lgt_sb[:], op=ALU.add),
        reads=["scr", "lgt_sb"], writes=["scr"])
    P.emit("vector", lambda v: v.reduce_max(out=m2[:], in_=s3, axis=AX.X),
        reads=["scr"], writes=["m2"])
    P.emit("vector", lambda v: v.tensor_tensor(
        out=scr[:], in0=s3,
        in1=m2[:].rearrange("p (a o) -> p a o", o=1).broadcast_to([128, NA, NE]),
        op=ALU.is_ge), reads=["scr", "m2"], writes=["scr"])
    P.emit("vector", lambda v: v.tensor_scalar(
        out=scr[:], in0=scr[:], scalar1=-BIG, scalar2=BIG,
        op0=ALU.mult, op1=ALU.add), reads=["scr"], writes=["scr"])
    P.emit("vector", lambda v: v.tensor_tensor(
        out=scr[:], in0=scr[:], in1=iota_e[:], op=ALU.add),
        reads=["scr", "iota_e"], writes=["scr"])
    P.emit("vector", lambda v: v.tensor_reduce(
        out=idx2[:], in_=s3, axis=AX.X, op=ALU.min),
        reads=["scr"], writes=["idx2"])
    P.emit("vector", lambda v: v.tensor_tensor(
        out=oh2[:], in0=iota_e[:],
        in1=idx2[:].rearrange("p (a o) -> p a o", o=1).broadcast_to([128, NA, NE]),
        op=ALU.is_equal), reads=["iota_e", "idx2"], writes=["oh2"])
    P.emit("vector", lambda v: v.tensor_tensor(
        out=e2g[:], in0=m2[:], in1=m1[:], op=ALU.subtract),
        reads=["m2", "m1"], writes=["e2g"])
    P.emit("scalar", lambda sc: sc.activation(
        out=e2g[:], in_=e2g[:], func=AF.Exp), reads=["e2g"], writes=["e2g"])
    for e in range(NE):
        P.emit("vector", lambda v, e=e: v.tensor_tensor_scan(
            out=c13[:, :, e], data0=o13[:, :, e], data1=o13[:, :, e],
            initial=0.0, op0=ALU.mult, op1=ALU.add),
            reads=["oh1"], writes=["scan1"])
        P.emit("vector", lambda v, e=e: v.tensor_tensor_scan(
            out=c23[:, :, e], data0=o23[:, :, e], data1=o23[:, :, e],
            initial=0.0, op0=ALU.mult, op1=ALU.add),
            reads=["oh2"], writes=["scan2"])
    P.emit("vector", lambda v: v.reduce_sum(out=cnt1[:], in_=o13.transpose([0, 2, 1]),
        axis=AX.X), reads=["oh1"], writes=["cnt1"])
    P.emit("vector", lambda v: v.reduce_sum(out=cnt2[:], in_=o23.transpose([0, 2, 1]),
        axis=AX.X), reads=["oh2"], writes=["cnt2"])
    P.emit("tensor", lambda t: t.matmul(
        PR1[0:128, 0:NE], ltm_f[:], cnt1[:], start=True, stop=True),
        reads=["ltm_f", "cnt1"], writes=["PR1"])
    P.emit("vector", lambda v: v.tensor_copy(out=base1[:], in_=PR1[0:128, 0:NE]),
        reads=["PR1"], writes=["base1"])
    P.emit("tensor", lambda t: t.matmul(
        PR1[0:1, 32:32 + NE], ones_f[:, 0:1], cnt1[:], start=True, stop=True),
        reads=["ones_f", "cnt1"], writes=["PR1"])
    P.emit("vector", lambda v: v.tensor_copy(out=tot1r[:], in_=PR1[0:1, 32:32 + NE]),
        reads=["PR1"], writes=["tot1r"])
    P.emit("tensor", lambda t: t.matmul(
        PR2[0:128, 0:NE], ltm_f[:], cnt2[:], start=True, stop=False),
        reads=["ltm_f", "cnt2"], writes=["PR2"])
    P.emit("tensor", lambda t: t.matmul(
        PR2[0:128, 0:NE], ones_f[0:1, :], tot1r[:], start=False, stop=True),
        reads=["ones_f", "tot1r"], writes=["PR2"])
    P.emit("vector", lambda v: v.tensor_copy(out=base2[:], in_=PR2[0:128, 0:NE]),
        reads=["PR2"], writes=["base2"])
    for (o3, c3, basex, locx, kres) in (
            (o13, c13, base1, loc1, ("oh1", "scan1", "base1", "loc1")),
            (o23, c23, base2, loc2, ("oh2", "scan2", "base2", "loc2"))):
        ohr, scr_r, baser, locr = kres
        P.emit("vector", lambda v, c3=c3, basex=basex, o3=o3: v.tensor_tensor(
            out=s3, in0=c3,
            in1=basex[:].rearrange("p (o e) -> p o e", o=1).broadcast_to([128, NA, NE]),
            op=ALU.add), reads=[scr_r, baser], writes=["scr"])
        P.emit("vector", lambda v, o3=o3: v.tensor_tensor(
            out=scr[:], in0=scr[:], in1=(oh1[:] if o3 is o13 else oh2[:]),
            op=ALU.mult), reads=["scr", ohr], writes=["scr"])
        P.emit("vector", lambda v, locx=locx: v.reduce_sum(
            out=locx[:], in_=s3, axis=AX.X), reads=["scr"], writes=[locr])
    P.emit("vector", lambda v: v.tensor_scalar(
        out=keep1[:], in0=loc1[:], scalar1=float(CAP), scalar2=None, op0=ALU.is_le),
        reads=["loc1"], writes=["keep1"])
    P.emit("vector", lambda v: v.tensor_scalar(
        out=keep2[:], in0=loc2[:], scalar1=float(CAP), scalar2=None, op0=ALU.is_le),
        reads=["loc2"], writes=["keep2"])
    P.emit("vector", lambda v: v.tensor_tensor(
        out=e2g[:], in0=e2g[:], in1=keep2[:], op=ALU.mult),
        reads=["e2g", "keep2"], writes=["e2g"])
    P.emit("vector", lambda v: v.tensor_tensor(
        out=gden[:], in0=keep1[:], in1=e2g[:], op=ALU.add),
        reads=["keep1", "e2g"], writes=["gden"])
    P.emit("vector", lambda v: v.tensor_scalar_max(
        out=gden[:], in0=gden[:], scalar1=1e-9), reads=["gden"], writes=["gden"])
    P.emit("vector", lambda v: v.reciprocal(out=gden[:], in_=gden[:]),
        reads=["gden"], writes=["gden"])
    P.emit("vector", lambda v: v.tensor_tensor(
        out=keep1[:], in0=keep1[:], in1=gden[:], op=ALU.mult),
        reads=["keep1", "gden"], writes=["keep1"])
    P.emit("vector", lambda v: v.tensor_tensor(
        out=e2g[:], in0=e2g[:], in1=gden[:], op=ALU.mult),
        reads=["e2g", "gden"], writes=["e2g"])
    P.emit("vector", lambda v: v.tensor_tensor(
        out=coef[:], in0=oh1[:],
        in1=keep1[:].rearrange("p (a o) -> p a o", o=1).broadcast_to([128, NA, NE]),
        op=ALU.mult), reads=["oh1", "keep1"], writes=["coef"])
    P.emit("vector", lambda v: v.tensor_tensor(
        out=scr[:], in0=oh2[:],
        in1=e2g[:].rearrange("p (a o) -> p a o", o=1).broadcast_to([128, NA, NE]),
        op=ALU.mult), reads=["oh2", "e2g"], writes=["scr"])
    P.emit("vector", lambda v: v.tensor_tensor(
        out=coef[:], in0=coef[:], in1=scr[:], op=ALU.add),
        reads=["coef", "scr"], writes=["coef"])
    sel_d = nc.dram_tensor("sel", [128, NPOWN], F32, kind="ExternalInput").ap()
    sel_sb = sbR("sel_sb", [128, NPOWN], F32)
    owncoef = sbR("owncoef", [NPOWN, NA * NE], F32)
    P.emit("sync", lambda s: s.dma_start(out=sel_sb[:], in_=sel_d[:, :]),
           writes=["sel_sb"], dma=True, stream="wsmall2")
    for j in range(NA * NE // 512):
        P.emit("tensor", lambda t, j=j: t.matmul(
            PR1[0:NPOWN, 0:512], sel_sb[:], coef[:, j * 512:(j + 1) * 512],
            start=True, stop=True), reads=["sel_sb", "coef"], writes=["PR1"])
        P.emit("vector", lambda v, j=j: v.tensor_copy(
            out=owncoef[:, j * 512:(j + 1) * 512], in_=PR1[0:NPOWN, 0:512]),
            reads=["PR1"], writes=["owncoef"])
    oc3 = owncoef[:].rearrange("p (a e) -> p a e", e=NE)
    ocrow = sbR("ocrow", [NPOWN, 512], F32)
    for e in range(NE):
        P.emit("vector", lambda v, e=e: v.tensor_copy(
            out=ocrow[:], in_=oc3[:, :, e]), reads=["owncoef"], writes=["ocrow"])
        P.emit("gpsimd", lambda g, e=e: g.dma_start(
            out=coef_d[e, :, :], in_=ocrow[:]),
            reads=["ocrow"], dma=True, stream="coef")

    P.base_waits = P.barrier_token()
    esP1.close()
    esR.close()

    NTB = TOK // 512
    esM = ExitStack()

    def sbM(name, shape, dtype):
        return esM.enter_context(nc.sbuf_tensor(name, shape, dtype))

    zero_sb = sbM("zero_sb", [128, 2048], F32)
    P.emit("vector", lambda v: v.memset(zero_sb[:], 0.0), writes=["zero_sb"])
    for db in range(8):
        for j in range(TOK // 2048):
            P.emit("gpsimd", lambda g, db=db, j=j: g.dma_start(
                out=moe_d[db, :, j * 2048:(j + 1) * 2048], in_=zero_sb[:]),
                reads=["zero_sb"], writes=["moe_dram"], dma=True, stream="mz")

    wi_all = sbM("wi_all", [128, 8 * 4096], BF16)
    actbuf = sbM("actbuf", [128, 16 * 512], BF16)
    ga_sb = sbM("ga_sb", [128, 512], BF16)
    xnT1m = sbM("xnT1m", [128, 8 * 512], BF16)
    xm3 = xnT1m[:].rearrange("p (c t) -> p c t", c=8)
    wo_all = sbM("wo_all", [128, 16 * 1024], BF16)
    coefrow = sbM("coefrow", [1, TOK], F32)
    coefbc = sbM("coefbc", [128, 512], F32)
    mev = sbM("mev", [128, 2 * 512], F32)

    esP2 = ExitStack()
    PM1 = esP2.enter_context(nc.psum_tensor("PM1", [128, 512], F32))
    PM2 = esP2.enter_context(nc.psum_tensor("PM2", [128, 512], F32))
    PDB = esP2.enter_context(nc.psum_tensor("PDB", [128, 4 * 512], F32))


    for e in range(NE):
        P.emit("sync", lambda s, e=e: s.dma_start(
            out=coefrow[0:1, :].rearrange("o (p a) -> o p a", a=512),
            in_=coef_d[e].rearrange("(o p) a -> o p a", o=1)),
            writes=["coefrow"], dma=True, stream="coef2")
        for hc in range(8):
            P.emit("gpsimd", lambda g, e=e, hc=hc: g.dma_start(
                out=wi_all[:, hc * 4096:(hc + 1) * 4096],
                in_=wi_d[e, hc * 128:(hc + 1) * 128, :]),
                writes=["wi_all"], dma=True, stream="wi")
        for fc in range(16):
            P.emit("gpsimd", lambda g, e=e, fc=fc: g.dma_start(
                out=wo_all[:, fc * 1024:(fc + 1) * 1024],
                in_=wo_d[e, fc * 128:(fc + 1) * 128, :]),
                writes=["wo_all"], dma=True, stream="wo2")
        for tb in range(NTB):
            for c in range(8):
                P.emit("sync", lambda s, tb=tb, c=c: s.dma_start(
                    out=xm3[:, c, :],
                    in_=xn1_d[tb * 512:(tb + 1) * 512, c * 128:(c + 1) * 128],
                    transpose=True), writes=["xnT1m"], dma=True, stream="xt")
            P.emit("tensor", lambda t, tb=tb: t.matmul(
                PM1[:, 0:512], ones_f[0:1, :], coefrow[0:1, tb * 512:(tb + 1) * 512],
                start=True, stop=True), reads=["ones_f", "coefrow"], writes=["PM1"])
            P.emit("vector", lambda v: v.tensor_copy(out=coefbc[:], in_=PM1[:, 0:512]),
                reads=["PM1"], writes=["coefbc"])
            for fc in range(16):
                for hc in range(8):
                    P.emit("tensor", lambda t, fc=fc, hc=hc: t.matmul(
                        PM1[:, 0:512],
                        wi_all[:, hc * 4096 + fc * 128:hc * 4096 + (fc + 1) * 128],
                        xm3[:, hc, :], start=(hc == 0), stop=(hc == 7)),
                        reads=["wi_all", "xnT1m"], writes=["PM1"])
                for hc in range(8):
                    P.emit("tensor", lambda t, fc=fc, hc=hc: t.matmul(
                        PM2[:, 0:512],
                        wi_all[:, hc * 4096 + (fc + 16) * 128:hc * 4096 + (fc + 17) * 128],
                        xm3[:, hc, :], start=(hc == 0), stop=(hc == 7)),
                        reads=["wi_all", "xnT1m"], writes=["PM2"])
                P.emit("scalar", lambda sc: sc.activation(
                    out=ga_sb[:], in_=PM1[:, 0:512], func=AF.Gelu),
                    reads=["PM1"], writes=["ga_sb"])
                P.emit("vector", lambda v, fc=fc: v.tensor_tensor(
                    out=actbuf[:, fc * 512:(fc + 1) * 512], in0=ga_sb[:],
                    in1=PM2[:, 0:512], op=ALU.mult),
                    reads=["ga_sb", "PM2"], writes=["actbuf"])
            for half in range(2):
                for fc in range(16):
                    for dbl in range(4):
                        db = half * 4 + dbl
                        P.emit("tensor", lambda t, fc=fc, db=db, dbl=dbl: t.matmul(
                            PDB[:, dbl * 512:(dbl + 1) * 512],
                            wo_all[:, fc * 1024 + db * 128:fc * 1024 + (db + 1) * 128],
                            actbuf[:, fc * 512:(fc + 1) * 512],
                            start=(fc == 0), stop=(fc == 15)),
                            reads=["wo_all", "actbuf"], writes=[f"PDB{dbl}"])
                for dbl in range(4):
                    db = half * 4 + dbl
                    sl = dbl % 2
                    P.emit("vector", lambda v, dbl=dbl, sl=sl: v.tensor_tensor(
                        out=mev[:, sl * 512:(sl + 1) * 512],
                        in0=PDB[:, dbl * 512:(dbl + 1) * 512], in1=coefbc[:],
                        op=ALU.mult),
                        reads=[f"PDB{dbl}", "coefbc"], writes=[f"mev{sl}"])
                    P.emit("gpsimd", lambda g, db=db, tb=tb, sl=sl: g.dma_start(
                        out=moe_d[db, :, tb * 512:(tb + 1) * 512],
                        in_=mev[:, sl * 512:(sl + 1) * 512], accum_op=ALU.add),
                        reads=[f"mev{sl}"], writes=["moe_dram"], dma=True,
                        stream="macc")

    P.base_waits = P.barrier_token()
    esP2.close()
    esM.close()

    w1_sb = sb("w1_sb", [128, 8 * 3072], BF16)
    wp_sb = sb("wp_sb", [128, 8 * 1024], BF16)
    bcol = sb("bcol", [128, 24], F32)
    bpcol = sb("bpcol", [128, 8], F32)
    brow = sb("brow", [1, 3072], F32)
    xnT1a = sb("xnT1a", [128, 8 * T], BF16)
    xa3 = xnT1a[:].rearrange("p (c t) -> p c t", c=8)
    qT2_sb = sb("qT2_sb", [H0, T], BF16)
    kT2_sb = sb("kT2_sb", [H0, T], BF16)
    vtm_sb = sb("vtm_sb", [128, 16 * 64], BF16)
    pbuf = sb("pbuf", [128, T], BF16)
    pT_sb = sb("pT_sb", [128, 16 * 512], BF16)
    avT_sb = sb("avT_sb", [128, 8 * T], BF16)
    lsum = sb("lsum", [128, 4], F32)
    negm = sb("negm", [128, 1], F32)
    rlrow = sb("rlrow", [1, 512], F32)
    rlbc = sb("rlbc", [H0, 512], F32)
    sum_fm2 = sb("sum_fm2", [128, 8 * 512], F32)
    moe_sb = sb("moe_sb", [128, 2 * 512], F32)
    xrow2 = sb("xrow2", [128, D], F32)
    outtok = sb("outtok", [128, D], F32)

    for c in range(8):
        P.emit("gpsimd", lambda g, c=c: g.dma_start(
            out=w1_sb[:, c * 3072:(c + 1) * 3072],
            in_=w1_d[c * 128:(c + 1) * 128, :]), writes=["w1_sb"], dma=True,
            stream="wi")
        P.emit("gpsimd", lambda g, c=c: g.dma_start(
            out=wp_sb[:, c * 1024:(c + 1) * 1024],
            in_=wp_d[c * 128:(c + 1) * 128, :]), writes=["wp_sb"], dma=True,
            stream="wi")
    P.emit("sync", lambda s: s.dma_start(
        out=brow[0:1, :], in_=b1_d[:].rearrange("(o f) -> o f", o=1)),
        writes=["brow"], dma=True, stream="coef2")
    for cb in range(24):
        P.emit("tensor", lambda t, cb=cb: t.transpose(
            PQK[0:128, 0:1], brow[0:1, cb * 128:(cb + 1) * 128], ident_f[0:1, 0:1]),
            reads=["brow", "ident_f"], writes=["PQK"])
        P.emit("vector", lambda v, cb=cb: v.tensor_copy(
            out=bcol[:, cb:cb + 1], in_=PQK[0:128, 0:1]),
            reads=["PQK"], writes=["bcol"])
    P.emit("sync", lambda s: s.dma_start(
        out=brow[0:1, 0:1024], in_=bp_d[:].rearrange("(o f) -> o f", o=1)),
        writes=["brow"], dma=True, stream="coef2")
    for cb in range(8):
        P.emit("tensor", lambda t, cb=cb: t.transpose(
            PQK[0:128, 0:1], brow[0:1, cb * 128:(cb + 1) * 128], ident_f[0:1, 0:1]),
            reads=["brow", "ident_f"], writes=["PQK"])
        P.emit("vector", lambda v, cb=cb: v.tensor_copy(
            out=bpcol[:, cb:cb + 1], in_=PQK[0:128, 0:1]),
            reads=["PQK"], writes=["bpcol"])

    PS = psum("PS", [128, 2048], F32)
    PEb2 = psum("PEb2", [128, 1024], BF16)
    PAV = psum("PAV", [128, 512], F32)
    PQK = psum("PQK", [128, 512], F32)

    for b in range(B_LOC):
        row0 = b * T
        for c in range(8):
            P.emit("sync", lambda s, b=b, c=c: s.dma_start(
                out=xa3[:, c, :],
                in_=xn1_d[row0:row0 + T, c * 128:(c + 1) * 128],
                transpose=True), writes=["xnT1a"], dma=True, stream="xt")
        for hh in range(NH):
            qcol = hh * 192
            kcol = hh * 192 + 64
            vcol = hh * 192 + 128
            for (col, dstbuf, nm) in ((qcol, qT2_sb, "qk_q"), (kcol, kT2_sb, "qk_k")):
                cb = col // 128
                off = col % 128
                for tg in range(4):
                    for c in range(8):
                        P.emit("tensor", lambda t, col=col, tg=tg, c=c: t.matmul(
                            PQK[0:64, 0:512],
                            w1_sb[:, c * 3072 + col:c * 3072 + col + 64],
                            xa3[:, c, tg * 512:(tg + 1) * 512],
                            start=(c == 0), stop=(c == 7)),
                            reads=["w1_sb", "xnT1a"], writes=["PQK"])
                    P.emit("scalar", lambda sc, dstbuf=dstbuf, tg=tg, cb=cb, off=off: sc.activation(
                        out=dstbuf[:, tg * 512:(tg + 1) * 512],
                        in_=PQK[0:64, 0:512], func=AF.Identity,
                        bias=bcol[off:off + 64, cb:cb + 1]),
                        reads=["PQK", "bcol"], writes=[nm])
            for tg in range(4):
                for c in range(8):
                    P.emit("tensor", lambda t, vcol=vcol, tg=tg, c=c: t.matmul(
                        PQK[0:64, 0:512],
                        w1_sb[:, c * 3072 + vcol:c * 3072 + vcol + 64],
                        xa3[:, c, tg * 512:(tg + 1) * 512],
                        start=(c == 0), stop=(c == 7)),
                        reads=["w1_sb", "xnT1a"], writes=["PQK"])
                vcb = (vcol) // 128
                voff = vcol % 128
                P.emit("scalar", lambda sc, tg=tg, vcb=vcb, voff=voff: sc.activation(
                    out=pbuf[0:64, tg * 512:(tg + 1) * 512], in_=PQK[0:64, 0:512],
                    func=AF.Identity, bias=bcol[voff:voff + 64, vcb:vcb + 1]),
                    reads=["PQK", "bcol"], writes=["pbuf"])
            for tt in range(16):
                P.emit("tensor", lambda t, tt=tt: t.transpose(
                    PEb2[:, 0:64], pbuf[0:64, tt * 128:(tt + 1) * 128],
                    ident_b[0:64, 0:64]),
                    reads=["pbuf", "ident_b", "vtm"], writes=["PEb2"])
                P.emit("vector", lambda v, tt=tt: v.tensor_copy(
                    out=vtm_sb[:, tt * 64:(tt + 1) * 64], in_=PEb2[0:128, 0:64]),
                    reads=["PEb2"], writes=["vtm"])
            for qi in range(16):
                kmax = (qi + 1) * 128
                nkb = ceil_div(kmax, 512)
                for kb in range(nkb):
                    klen = min(512, kmax - kb * 512)
                    P.emit("tensor", lambda t, qi=qi, kb=kb, klen=klen: t.matmul(
                        PS[:, kb * 512:kb * 512 + klen],
                        qT2_sb[:, qi * 128:(qi + 1) * 128],
                        kT2_sb[:, kb * 512:kb * 512 + klen],
                        start=True, stop=True),
                        reads=["qk_q", "qk_k"], writes=["PS"])
                P.emit("vector", lambda v, qi=qi: v.tensor_tensor(
                    out=PS[:, qi * 128:(qi + 1) * 128],
                    in0=PS[:, qi * 128:(qi + 1) * 128], in1=cmask[:], op=ALU.add),
                    reads=["PS", "cmask"], writes=["PS"])
                P.emit("vector", lambda v, qi=qi, kmax=kmax: v.reduce_max(
                    out=negm[:], in_=PS[:, 0:kmax], axis=AX.X),
                    reads=["PS"], writes=["negm"])
                P.emit("vector", lambda v: v.tensor_scalar_mul(
                    out=negm[:], in0=negm[:], scalar1=-0.125),
                    reads=["negm"], writes=["negm"])
                sl4 = qi % 4
                P.emit("vector", lambda v, sl4=sl4: v.memset(
                    lsum[:, sl4:sl4 + 1], 0.0), writes=["lsum"])
                P.emit("scalar", lambda sc, kmax=kmax, sl4=sl4: sc.activation(
                    out=pbuf[:, 0:kmax], in_=PS[:, 0:kmax], func=AF.Exp,
                    bias=negm[:, 0:1], scale=0.125, accum_out=lsum[:, sl4:sl4 + 1]),
                    reads=["PS", "negm", "lsum"], writes=["pbuf", "lsum"])
                for kt in range(qi + 1, (qi // 4) * 4 + 4):
                    P.emit("vector", lambda v, kt=kt, qi=qi: v.memset(
                        pT_sb[:, kt * 512 + (qi % 4) * 128:kt * 512 + (qi % 4) * 128 + 128],
                        0.0), writes=["pT"])
                for kt in range(qi + 1):
                    P.emit("tensor", lambda t, kt=kt: t.transpose(
                        PEb2[:, 0:128], pbuf[:, kt * 128:(kt + 1) * 128], ident_b[:]),
                        reads=["pbuf", "ident_b", "pT"], writes=["PEb2"])
                    P.emit("vector", lambda v, kt=kt, qi=qi: v.tensor_copy(
                        out=pT_sb[:, kt * 512 + (qi % 4) * 128:kt * 512 + (qi % 4) * 128 + 128],
                        in_=PEb2[:, 0:128]), reads=["PEb2"], writes=["pT"])
                if qi % 4 == 3:
                    qg = qi // 4
                    nkt = (qg + 1) * 4
                    for kt in range(nkt):
                        P.emit("tensor", lambda t, kt=kt, nkt=nkt: t.matmul(
                            PAV[0:64, 0:512], vtm_sb[:, kt * 64:(kt + 1) * 64],
                            pT_sb[:, kt * 512:(kt + 1) * 512],
                            start=(kt == 0), stop=(kt == nkt - 1)),
                            reads=["vtm", "pT"], writes=["PAV"])
                    P.emit("vector", lambda v: v.reciprocal(
                        out=lsum[:, 0:4], in_=lsum[:, 0:4]),
                        reads=["lsum"], writes=["lsum"])
                    for j in range(4):
                        P.emit("tensor", lambda t, j=j: t.transpose(
                            PQK[0:1, 0:128], lsum[:, j:j + 1], ident_f[:]),
                            reads=["lsum", "ident_f"], writes=["PQK"])
                        P.emit("vector", lambda v, j=j: v.tensor_copy(
                            out=rlrow[0:1, j * 128:(j + 1) * 128], in_=PQK[0:1, 0:128]),
                            reads=["PQK"], writes=["rlrow"])
                    P.emit("tensor", lambda t: t.matmul(
                        PQK[0:64, 0:512], ones_f[0:1, 0:64], rlrow[0:1, :],
                        start=True, stop=True),
                        reads=["ones_f", "rlrow"], writes=["PQK"])
                    P.emit("vector", lambda v: v.tensor_copy(
                        out=rlbc[:], in_=PQK[0:64, 0:512]),
                        reads=["PQK"], writes=["rlbc"])
                    hslot = hh % 2
                    P.emit("vector", lambda v, hh=hh, qg=qg, hslot=hslot: v.tensor_tensor(
                        out=avT_sb[hslot * 64:hslot * 64 + 64,
                                   (hh // 2) * T + qg * 512:(hh // 2) * T + (qg + 1) * 512],
                        in0=PAV[0:64, 0:512], in1=rlbc[:], op=ALU.mult),
                        reads=["PAV", "rlbc"], writes=["avT"])
        for qg in range(4):
            tslc = slice(qg * 512, (qg + 1) * 512)
            def emit_moeload(db):
                msl = db % 2
                P.emit("sync", lambda s, db=db, b=b, qg=qg, msl=msl: s.dma_start(
                    out=moe_sb[:, msl * 512:(msl + 1) * 512],
                    in_=moe_d[db, :, b * T + qg * 512:b * T + (qg + 1) * 512]),
                    writes=[f"moe_sb{msl}"], dma=True, stream="moin")
            emit_moeload(0)
            for db in range(8):
                if db + 1 < 8:
                    emit_moeload(db + 1)
                for hh in range(NH):
                    P.emit("tensor", lambda t, db=db, hh=hh, qg=qg: t.matmul(
                        PQK[:, 0:512],
                        wp_sb[(hh % 2) * 64:(hh % 2) * 64 + 64,
                              (hh // 4) * 2048 + 0:0 + 0] if False else
                        wp_sb[64 * (hh % 2):64 * (hh % 2) + 64,
                              (hh // 2) * 1024 + db * 128:(hh // 2) * 1024 + (db + 1) * 128],
                        avT_sb[(hh % 2) * 64:(hh % 2) * 64 + 64,
                               (hh // 2) * T + qg * 512:(hh // 2) * T + (qg + 1) * 512],
                        start=(hh == 0), stop=(hh == NH - 1)),
                        reads=["wp_sb", "avT"], writes=["PQK"])
                msl = db % 2
                P.emit("vector", lambda v, db=db: v.tensor_scalar(
                    out=sum_fm2[:, db * 512:(db + 1) * 512], in0=PQK[:, 0:512],
                    scalar1=bpcol[:, db:db + 1], scalar2=None, op0=ALU.add),
                    reads=["PQK", "bpcol"], writes=[f"sf{db}"])
                P.emit("vector", lambda v, db=db, msl=msl: v.tensor_tensor(
                    out=sum_fm2[:, db * 512:(db + 1) * 512],
                    in0=sum_fm2[:, db * 512:(db + 1) * 512],
                    in1=moe_sb[:, msl * 512:(msl + 1) * 512], op=ALU.add),
                    reads=[f"sf{db}", f"moe_sb{msl}"], writes=[f"sf{db}"])
            for ts in range(4):
                rows = slice(row0 + qg * 512 + ts * 128, row0 + qg * 512 + ts * 128 + 128)
                for dc in range(8):
                    P.emit("tensor", lambda t, dc=dc, ts=ts: t.transpose(
                        PS[:, dc * 128:(dc + 1) * 128],
                        sum_fm2[:, dc * 512 + ts * 128:dc * 512 + (ts + 1) * 128],
                        ident_f[:]),
                        reads=[f"sf{dc}", "ident_f", "outtok"], writes=["PS"])
                P.emit("sync", lambda s, rows=rows: s.dma_start(
                    out=xrow2[:], in_=x1_d[rows, :]), writes=["xrow2"], dma=True,
                    stream="xin")
                P.emit("vector", lambda v: v.tensor_tensor(
                    out=outtok[:], in0=xrow2[:], in1=PS[:, 0:1024], op=ALU.add),
                    reads=["xrow2", "PS"], writes=["outtok"])
                P.emit("gpsimd", lambda g, rows=rows: g.dma_start(
                    out=out_d[rows, :], in_=outtok[:]), reads=["outtok"], dma=True,
                    stream="ost")

    toks = P.barrier_token()
    P.emit("vector", lambda v: v.engine_nop(), waits=toks)
    blk = es.enter_context(nc.Block())
    P.finalize(blk)
    es.close()
    return nc

_NC1 = None
_NC2 = None


def kernel(L0_attn_x, L0_attn_W_q, L0_attn_W_k, L0_attn_W_v, L0_attn_W_o,
           L0_ffn_w_up, L0_ffn_w_down, L1_attn_W1, L1_attn_b1, L1_attn_Wp,
           L1_attn_bp, L1_ffn_Wg, L1_ffn_W_in, L1_ffn_W_out):
    global _NC1, _NC2
    x = np.ascontiguousarray(L0_attn_x.reshape(NTOK_ALL, D))
    if _NC1 is None:
        _NC1 = build_launch1()
    in1 = []
    for c in range(N_CORES):
        in1.append({
            "x": x[c * TOK:(c + 1) * TOK],
            "wq": L0_attn_W_q, "wk": L0_attn_W_k, "wv": L0_attn_W_v,
            "wo": L0_attn_W_o, "wup": L0_ffn_w_up, "wdn": L0_ffn_w_down,
            "wg": L1_ffn_Wg,
        })
    r1 = run_bass_kernel_spmd(_NC1, in1, core_ids=list(range(N_CORES)))
    lgt_full = np.concatenate([r["lgt"] for r in r1.results], 0)
    if _NC2 is None:
        _NC2 = build_launch2()
    in2 = []
    for c in range(N_CORES):
        npown = TOK // 512
        sel = np.zeros((128, npown), np.float32)
        for j in range(npown):
            sel[c * npown + j, j] = 1.0
        in2.append({
            "x1": r1.results[c]["x1"], "xn1": r1.results[c]["xn1"],
            "lgt": lgt_full, "sel": sel,
            "w1": L1_attn_W1, "b1": L1_attn_b1, "wp": L1_attn_Wp,
            "bp": L1_attn_bp, "wi": L1_ffn_W_in, "wo": L1_ffn_W_out,
        })
    r2 = run_bass_kernel_spmd(_NC2, in2, core_ids=list(range(N_CORES)))
    out = np.concatenate([r["out"] for r in r2.results], 0)
    return out.reshape(B_ALL, T, D)


if __name__ == "__main__":
    nc = build_launch1()
    print("launch1 built ok; instructions:",
          sum(len(f.instructions) for f in []) if False else "n/a")


# version 04
# speedup vs baseline: 1.4372x; 1.4372x over previous
import math
import numpy as np
from contextlib import ExitStack

import concourse.bass as bass
import concourse.mybir as mybir
from concourse.bass_utils import run_bass_kernel_spmd
from concourse.bass_core_rust import IndirectOffsetOnAxis
import ml_dtypes

F32 = mybir.dt.float32
BF16 = mybir.dt.bfloat16
I32 = mybir.dt.int32
AF = mybir.ActivationFunctionType
ALU = mybir.AluOpType
AX = mybir.AxisListType

N_CORES = 8
B_ALL, T, D = 32, 2048, 1024
B_LOC = B_ALL // N_CORES
TOK = B_LOC * T
NTOK_ALL = B_ALL * T
DFF = 4096
H0 = 64
NH, DH = 16, 64
NE = 8
FEXP = 2048
CAP = int(math.ceil(2 * NTOK_ALL / NE * 1.25))
LN_EPS = 1e-5
RMS_EPS = 1e-8

def ceil_div(a, b):
    return (a + b - 1) // b


ENGINES = ("sync", "tensor", "vector", "scalar", "gpsimd")
COMPUTE_ENGINES = {"tensor", "vector", "scalar"}


class Prog:

    def __init__(self, nc, es, streams=()):
        self.nc = nc
        self.es = es
        self.ops = {e: [] for e in ENGINES}
        self.keys = list(ENGINES) + ["dma_" + s for s in streams]
        self.count = {k: 0 for k in self.keys}
        self.sem = {k: es.enter_context(nc.semaphore(f"prog_{k}")) for k in self.keys}
        self.writer = {}
        self.readers = {}
        self.base_waits = []
        self.guaranteed = {}

    def emit(self, eng, fn, reads=(), writes=(), waits=(), dma=False, stream=None):
        inc = 16 if dma else 1
        if dma:
            assert stream is not None, "DMA emits must name a stream"
            key = "dma_" + stream
        else:
            key = eng
        deps = list(waits) + self.base_waits
        for r in reads:
            w = self.writer.get(r)
            if w is not None:
                deps.append(w)
        for r in writes:
            w = self.writer.get(r)
            if w is not None:
                deps.append(w)
            deps.extend(self.readers.get(r, ()))
        best = {}
        for (de, dv) in deps:
            if best.get(de, -1) < dv:
                best[de] = dv
        if key in best and best[key] > self.count[key]:
            raise RuntimeError(f"self-dep in future on {key}")
        for k in list(best):
            if k.startswith("dma_") and best[k] > 0:
                best[k] = self.count[k]
        g = self.guaranteed.setdefault(eng, {})
        toklist = []
        for k, v in sorted(best.items()):
            if g.get(k, -1) >= v:
                continue
            g[k] = v
            toklist.append((k, v))
        self.ops[eng].append((toklist, fn, inc, key))
        self.count[key] += inc
        tok = (key, self.count[key])
        for r in writes:
            self.writer[r] = tok
            self.readers[r] = []
        for r in reads:
            self.readers.setdefault(r, []).append(tok)
        return tok

    def barrier_token(self):
        return [(k, self.count[k]) for k in self.keys if self.count[k] > 0]

    def finalize(self, blk):
        semmap = self.sem
        for e in ENGINES:
            ops = self.ops[e]
            if not ops:
                continue

            def make_body(ops):
                def body(engobj):
                    for waits, fn, inc, key in ops:
                        for (we, wv) in waits:
                            engobj.wait_ge(semmap[we], wv)
                        fn(engobj).then_inc(semmap[key], inc)
                return body
            getattr(blk, e)(make_body(ops))



def build_launch1():
    nc = bass.Bass("TRN2", target_bir_lowering=False, debug=False)
    x_d = nc.dram_tensor("x", [TOK, D], F32, kind="ExternalInput").ap()
    wq_d = nc.dram_tensor("wq", [D, H0], F32, kind="ExternalInput").ap()
    wk_d = nc.dram_tensor("wk", [D, H0], F32, kind="ExternalInput").ap()
    wv_d = nc.dram_tensor("wv", [D, H0], F32, kind="ExternalInput").ap()
    wo_d = nc.dram_tensor("wo", [H0, D], F32, kind="ExternalInput").ap()
    wup_d = nc.dram_tensor("wup", [DFF, D], F32, kind="ExternalInput").ap()
    wdn_d = nc.dram_tensor("wdn", [D, DFF], F32, kind="ExternalInput").ap()
    wg_d = nc.dram_tensor("wg", [D, NE], F32, kind="ExternalInput").ap()

    x1_d = nc.dram_tensor("x1", [TOK, D], F32, kind="ExternalOutput").ap()
    xn1_d = nc.dram_tensor("xn1", [TOK, D], BF16, kind="ExternalOutput").ap()
    lgt_d = nc.dram_tensor("lgt", [TOK, NE], F32, kind="ExternalOutput").ap()

    w1t_d = nc.dram_tensor("w1t", [DFF // 128, D // 128, 128, 128], BF16).ap()
    w2t_d = nc.dram_tensor("w2t", [D // 128, DFF // 128, 128, 128], BF16).ap()

    es = ExitStack()
    P = Prog(nc, es, streams=("wsmall", "wraw", "wtst", "xin", "w1", "w2", "ost"))

    def sb(name, shape, dtype):
        return es.enter_context(nc.sbuf_tensor(name, shape, dtype))

    def psum(name, shape, dtype):
        return es.enter_context(nc.psum_tensor(name, shape, dtype))

    ones_f = sb("ones_f", [128, 128], F32)
    ones_b = sb("ones_b", [128, 128], BF16)
    ident_b = sb("ident_b", [128, 128], BF16)
    ident_f = sb("ident_f", [128, 128], F32)
    ultri_f = sb("ultri_f", [128, 128], F32)
    onecol_b = sb("onecol_b", [128, 1], BF16)

    qw_s = sb("qw_s", [128, 8 * H0], BF16)
    kw_s = sb("kw_s", [128, 8 * H0], BF16)
    vw_s = sb("vw_s", [128, 8 * H0], BF16)
    ow_s = sb("ow_s", [H0, D], BF16)
    gw_s = sb("gw_s", [128, 8 * NE], F32)


    xrow = sb("xrow", [128, 2 * D], F32)
    xsq = sb("xsq", [128, D], F32)
    s_a = sb("s_a", [128, 1], F32)
    s_b = sb("s_b", [128, 1], F32)
    mu = sb("mu", [128, 1], F32)
    var = sb("var", [128, 1], F32)
    rstd = sb("rstd", [128, 1], F32)
    xn_tok = sb("xn_tok", [128, 2 * D], BF16)
    xnT = sb("xnT", [128, 8 * 512], BF16)
    xnT3 = xnT[:].rearrange("p (c t) -> p c t", c=8)

    qT_s = sb("qT_s", [H0, T], BF16)
    kT_s = sb("kT_s", [H0, T], BF16)
    vT = sb("vT", [H0, T], BF16)
    oT = sb("oT", [H0, T], BF16)
    phi_a = sb("phi_a", [128, 512], F32)
    phi_b = sb("phi_b", [128, 512], F32)
    ktm = sb("ktm", [128, 2 * H0], BF16)
    vtm = sb("vtm", [128, 2 * H0], BF16)
    ktm_f = sb("ktm_f", [128, 128], F32)
    wtm_b = sb("wtm_b", [128, 128], BF16)
    rd_row = sb("rd_row", [1, 128], F32)
    qs_b = sb("qs_b", [H0, 128], BF16)
    st_f = sb("st_f", [H0, 4 * (H0 + 1)], F32)
    st_b = sb("st_b", [H0, H0 + 1], BF16)

    w1sb = sb("w1sb", [128, 2 * 8 * 128], BF16)
    w2sb = sb("w2sb", [128, 2 * 16 * 128], BF16)
    xnormT = sb("xnormT", [128, 8 * 512], BF16)
    hbuf = sb("hbuf", [128, 32 * 512], BF16)
    hsq = sb("hsq", [128, 2 * 512], BF16)
    t1s = sb("t1s", [128, 512], F32)
    t2s = sb("t2s", [128, 512], F32)
    sum_fm = sb("sum_fm", [128, 8 * 512], F32)
    x1tok = sb("x1tok", [128, D], F32)
    xn1tok_f = sb("xn1tok_f", [128, D], F32)
    xn1tok_b = sb("xn1tok_b", [128, D], BF16)
    xn1T_f = sb("xn1T_f", [128, 128], F32)
    lgt_row = sb("lgt_row", [8, 128], F32)
    lgt_f = sb("lgt_f", [128, NE], F32)

    esA = ExitStack()

    def sbA(name, shape, dtype):
        return esA.enter_context(nc.sbuf_tensor(name, shape, dtype))

    wld = sbA("wld", [128, DFF], F32)
    wabs = sbA("wabs", [128, 1], F32)
    wsgn = sbA("wsgn", [128, DFF], F32)
    wtb = sbA("wtb", [128, DFF], BF16)
    alpha_acc = sbA("alpha_acc", [1, 2], F32)
    tau_col = sbA("tau_col", [128, 1], F32)
    tr_ev = sbA("tr_ev", [128, 4 * 128], BF16)

    PA = psum("PA", [128, 512], F32)
    PB = psum("PB", [128, 512], F32)
    PC = psum("PC", [128, 512], F32)
    PD = psum("PD", [128, 1024], F32)
    PEb = psum("PEb", [128, 1024], BF16)
    PX = psum("PX", [128, 512], F32)
    PY = psum("PY", [128, 512], F32)

    blk = es.enter_context(nc.Block())

    P.emit("vector", lambda v: v.memset(ones_f[:], 1.0), writes=["ones_f"])
    P.emit("vector", lambda v: v.memset(ones_b[:], 1.0), writes=["ones_b"])
    P.emit("vector", lambda v: v.memset(onecol_b[:], 1.0), writes=["onecol_b"])
    P.emit("gpsimd", lambda g: g.affine_select(
        ident_b[:], ones_b[:], [[-1, 128]], ALU.is_equal, 0.0, channel_multiplier=1),
        reads=["ones_b"], writes=["ident_b"])
    P.emit("gpsimd", lambda g: g.affine_select(
        ident_f[:], ones_f[:], [[-1, 128]], ALU.is_equal, 0.0, channel_multiplier=1),
        reads=["ones_f"], writes=["ident_f"])
    P.emit("gpsimd", lambda g: g.affine_select(
        ultri_f[:], ones_f[:], [[1, 128]], ALU.is_ge, 0.0, channel_multiplier=-1),
        reads=["ones_f"], writes=["ultri_f"])

    wq3 = wq_d.rearrange("(c p) h -> p c h", p=128)
    wk3 = wk_d.rearrange("(c p) h -> p c h", p=128)
    wv3 = wv_d.rearrange("(c p) h -> p c h", p=128)
    wg3 = wg_d.rearrange("(c p) e -> p c e", p=128)
    P.emit("gpsimd", lambda g: g.dma_start(
        out=qw_s[:].rearrange("p (c h) -> p c h", c=8), in_=wq3), writes=["qw_s"], dma=True, stream="wsmall")
    P.emit("gpsimd", lambda g: g.dma_start(
        out=kw_s[:].rearrange("p (c h) -> p c h", c=8), in_=wk3), writes=["kw_s"], dma=True, stream="wsmall")
    P.emit("gpsimd", lambda g: g.dma_start(
        out=vw_s[:].rearrange("p (c h) -> p c h", c=8), in_=wv3), writes=["vw_s"], dma=True, stream="wsmall")
    P.emit("gpsimd", lambda g: g.dma_start(out=ow_s[:], in_=wo_d[:, :]), writes=["ow_s"], dma=True, stream="wsmall")
    P.emit("gpsimd", lambda g: g.dma_start(
        out=gw_s[:].rearrange("p (c e) -> p c e", c=8), in_=wg3), writes=["gw_s"], dma=True, stream="wsmall")

    def tern_alpha(w_d, nrows, ncols, slot):
        row_tiles = nrows // 128
        for r in range(row_tiles):
            P.emit("sync", lambda s, r=r: s.dma_start(
                out=wld[:, 0:ncols], in_=w_d[r * 128:(r + 1) * 128, :]),
                writes=["wld"], dma=True, stream="wraw")
            P.emit("vector", lambda v: v.tensor_reduce(
                out=wabs[:], in_=wld[:, 0:ncols], axis=AX.X, op=ALU.add,
                apply_absolute_value=True), reads=["wld"], writes=["wabs"])
            P.emit("tensor", lambda t, r=r: t.matmul(
                PY[0:1, slot:slot + 1], ones_f[:, 0:1], wabs[:],
                start=(r == 0), stop=(r == row_tiles - 1)),
                reads=["wabs", "ones_f"], writes=["PY"])
        P.emit("vector", lambda v: v.tensor_copy(
            out=alpha_acc[0:1, slot:slot + 1], in_=PY[0:1, slot:slot + 1]),
            reads=["PY"], writes=["alpha_acc"])
        P.emit("tensor", lambda t: t.matmul(
            PY[0:128, 256:257], ones_f[0:1, :], alpha_acc[0:1, slot:slot + 1],
            start=True, stop=True),
            reads=["alpha_acc"], writes=["PY"])
        P.emit("vector", lambda v: v.tensor_scalar_mul(
            out=tau_col[:], in0=PY[0:128, 256:257], scalar1=0.5 / (nrows * ncols)),
            reads=["PY"], writes=["tau_col"])

    def tern_pass2(w_d, nrows, ncols, wt_out_d):
        row_tiles = nrows // 128
        col_chunks = ncols // 128
        for rb in range(row_tiles):
            P.emit("sync", lambda s, rb=rb: s.dma_start(
                out=wld[:, 0:ncols], in_=w_d[rb * 128:(rb + 1) * 128, :]),
                writes=["wld"], dma=True, stream="wraw")
            P.emit("scalar", lambda sc: sc.activation(
                out=wsgn[:, 0:ncols], in_=wld[:, 0:ncols], func=AF.Sign),
                reads=["wld"], writes=["wsgn"])
            P.emit("scalar", lambda sc: sc.activation(
                out=wld[:, 0:ncols], in_=wld[:, 0:ncols], func=AF.Abs),
                reads=["wld"], writes=["wld"])
            P.emit("vector", lambda v: v.tensor_scalar(
                out=wld[:, 0:ncols], in0=wld[:, 0:ncols], scalar1=tau_col[:, 0:1],
                scalar2=None, op0=ALU.is_gt),
                reads=["wld", "tau_col"], writes=["wld"])
            P.emit("vector", lambda v: v.tensor_tensor(
                out=wtb[:, 0:ncols], in0=wsgn[:, 0:ncols], in1=wld[:, 0:ncols],
                op=ALU.mult), reads=["wsgn", "wld"], writes=["wtb"])
            for cb in range(col_chunks):
                half = cb % 8
                tslot = cb % 4
                P.emit("tensor", lambda t, cb=cb, half=half: t.transpose(
                    PEb[:, half * 128:(half + 1) * 128],
                    wtb[:, cb * 128:(cb + 1) * 128], ident_b[:]),
                    reads=["wtb", "ident_b"], writes=["PEb"])
                P.emit("vector", lambda v, half=half, tslot=tslot: v.tensor_copy(
                    out=tr_ev[:, tslot * 128:(tslot + 1) * 128],
                    in_=PEb[:, half * 128:(half + 1) * 128]),
                    reads=["PEb"], writes=[f"trev{tslot}"])
                P.emit("gpsimd", lambda g, rb=rb, cb=cb, tslot=tslot: g.dma_start(
                    out=wt_out_d[rb, cb, :, :],
                    in_=tr_ev[:, tslot * 128:(tslot + 1) * 128]),
                    reads=[f"trev{tslot}"], dma=True, stream="wtst")

    tern_alpha(wup_d, DFF, D, 0)
    tern_pass2(wup_d, DFF, D, w1t_d)
    tern_alpha(wdn_d, D, DFF, 1)
    tern_pass2(wdn_d, D, DFF, w2t_d)
    esA.close()
    P.base_waits = P.barrier_token()

    NT = T // 128
    NG = T // 512

    def layernorm_tile(src_res, src_ap, dst_ap, dst_res, out_dtype_pair=None):
        P.emit("vector", lambda v: v.tensor_reduce(
            out=s_a[:], in_=src_ap, axis=AX.X, op=ALU.add),
            reads=[src_res], writes=["s_a"])
        P.emit("scalar", lambda sc: sc.activation(
            out=xsq[:], in_=src_ap, func=AF.Square), reads=[src_res], writes=["xsq"])
        P.emit("vector", lambda v: v.tensor_reduce(
            out=s_b[:], in_=xsq[:], axis=AX.X, op=ALU.add),
            reads=["xsq"], writes=["s_b"])
        P.emit("vector", lambda v: v.tensor_scalar_mul(
            out=mu[:], in0=s_a[:], scalar1=1.0 / D), reads=["s_a"], writes=["mu"])
        P.emit("vector", lambda v: v.tensor_scalar_mul(
            out=var[:], in0=s_b[:], scalar1=1.0 / D), reads=["s_b"], writes=["var"])
        P.emit("vector", lambda v: v.tensor_tensor(
            out=s_b[:], in0=mu[:], in1=mu[:], op=ALU.mult),
            reads=["mu"], writes=["s_b"])
        P.emit("vector", lambda v: v.tensor_tensor(
            out=var[:], in0=var[:], in1=s_b[:], op=ALU.subtract),
            reads=["var", "s_b"], writes=["var"])
        P.emit("vector", lambda v: v.tensor_scalar_add(
            out=var[:], in0=var[:], scalar1=LN_EPS), reads=["var"], writes=["var"])
        P.emit("scalar", lambda sc: sc.activation(
            out=rstd[:], in_=var[:], func=AF.Sqrt),
            reads=["var"], writes=["rstd"])
        P.emit("vector", lambda v: v.reciprocal(out=rstd[:], in_=rstd[:]),
            reads=["rstd"], writes=["rstd"])
        P.emit("vector", lambda v: v.tensor_scalar(
            out=dst_ap, in0=src_ap, scalar1=mu[:, 0:1], scalar2=rstd[:, 0:1],
            op0=ALU.subtract, op1=ALU.mult),
            reads=[src_res, "mu", "rstd"], writes=[dst_res])

    for b in range(B_LOC):
        row0 = b * T

        P.emit("vector", lambda v, b=b: v.memset(
            st_f[:, b * (H0 + 1):(b + 1) * (H0 + 1)], 0.0), writes=["st_f"])
        P.emit("vector", lambda v: v.memset(st_b[:], 0.0), writes=["st_b"])
        stf = st_f[:, b * (H0 + 1):(b + 1) * (H0 + 1)]

        for g in range(NG):
            tslc = slice(g * 512, (g + 1) * 512)
            def emit_xload(nl):
                n = g * 4 + nl
                slot = nl % 2
                xr = xrow[:, slot * D:(slot + 1) * D]
                P.emit("sync", lambda s, n=n, xr=xr, row0=row0: s.dma_start(
                    out=xr, in_=x_d[row0 + n * 128:row0 + (n + 1) * 128, :]),
                    writes=[f"xrow{slot}"], dma=True, stream="xin")
            emit_xload(0)
            for nl in range(4):
                n = g * 4 + nl
                slot = nl % 2
                xr = xrow[:, slot * D:(slot + 1) * D]
                xnt = xn_tok[:, slot * D:(slot + 1) * D]
                if nl + 1 < 4:
                    emit_xload(nl + 1)
                layernorm_tile(f"xrow{slot}", xr, xnt, f"xn{slot}")
                for c in range(8):
                    P.emit("tensor", lambda t, c=c, xnt=xnt, slot=slot: t.transpose(
                        PEb[:, c * 128:(c + 1) * 128],
                        xnt[:, c * 128:(c + 1) * 128], ident_b[:]),
                        reads=[f"xn{slot}", "ident_b", "xnT"], writes=["PEb"])
                P.emit("vector", lambda v, nl=nl: v.tensor_copy(
                    out=xnT3[:, :, nl * 128:(nl + 1) * 128],
                    in_=PEb[:].rearrange("p (c t) -> p c t", c=8)),
                    reads=["PEb"], writes=["xnT"])

            for c in range(8):
                P.emit("tensor", lambda t, c=c: t.matmul(
                    PA[0:64, 0:512], qw_s[:, c * H0:(c + 1) * H0],
                    xnT3[:, c, :], start=(c == 0), stop=(c == 7)),
                    reads=["qw_s", "xnT"], writes=["PA"])
            for c in range(8):
                P.emit("tensor", lambda t, c=c: t.matmul(
                    PB[0:64, 0:512], kw_s[:, c * H0:(c + 1) * H0],
                    xnT3[:, c, :], start=(c == 0), stop=(c == 7)),
                    reads=["kw_s", "xnT"], writes=["PB"])
            for c in range(8):
                P.emit("tensor", lambda t, c=c: t.matmul(
                    PC[0:64, 0:512], vw_s[:, c * H0:(c + 1) * H0],
                    xnT3[:, c, :], start=(c == 0), stop=(c == 7)),
                    reads=["vw_s", "xnT"], writes=["PC"])
            for (pp, pres, dst, dres) in (
                    (PA, "PA", qT_s, "qT"), (PB, "PB", kT_s, "kT")):
                P.emit("vector", lambda v, pp=pp: v.tensor_scalar_min(
                    out=phi_a[0:64, :], in0=pp[0:64, 0:512], scalar1=0.0),
                    reads=[pres], writes=["phi_a"])
                P.emit("scalar", lambda sc: sc.activation(
                    out=phi_a[0:64, :], in_=phi_a[0:64, :], func=AF.Exp),
                    reads=["phi_a"], writes=["phi_a"])
                P.emit("vector", lambda v, pp=pp: v.tensor_scalar_max(
                    out=phi_b[0:64, :], in0=pp[0:64, 0:512], scalar1=0.0),
                    reads=[pres], writes=["phi_b"])
                P.emit("vector", lambda v, tslc=tslc, dst=dst: v.tensor_tensor(
                    out=dst[:, tslc], in0=phi_a[0:64, :], in1=phi_b[0:64, :],
                    op=ALU.add),
                    reads=["phi_a", "phi_b"], writes=[dres])
            P.emit("vector", lambda v, tslc=tslc: v.tensor_copy(
                out=vT[:, tslc], in_=PC[0:64, 0:512]),
                reads=["PC"], writes=["vT"])

            for i in range(4 * g, 4 * g + 4):
                t0 = i * 128
                cs = slice(t0, t0 + 128)
                sl = i % 2
                P.emit("tensor", lambda t, cs=cs, sl=sl: t.transpose(
                    PEb[:, 0:64], kT_s[:, cs], ident_b[0:64, 0:64]),
                    reads=["kT", "ident_b", "ktm" + str(sl)], writes=["PEb"])
                P.emit("vector", lambda v, sl=sl: v.tensor_copy(
                    out=ktm[:, sl * H0:(sl + 1) * H0], in_=PEb[0:128, 0:64]),
                    reads=["PEb"], writes=["ktm" + str(sl)])
                P.emit("tensor", lambda t, cs=cs, sl=sl: t.transpose(
                    PEb[:, 64:128], vT[0:64, cs], ident_b[0:64, 0:64]),
                    reads=["vT", "ident_b", "vtm" + str(sl)], writes=["PEb"])
                P.emit("vector", lambda v, sl=sl: v.tensor_copy(
                    out=vtm[:, sl * H0:(sl + 1) * H0], in_=PEb[0:128, 64:128]),
                    reads=["PEb"], writes=["vtm" + str(sl)])
                ktm_s = ktm[:, sl * H0:(sl + 1) * H0]
                vtm_s = vtm[:, sl * H0:(sl + 1) * H0]
                P.emit("tensor", lambda t, cs=cs: t.matmul(
                    PA[:, 0:128], kT_s[:, cs], qT_s[:, cs], start=True, stop=True),
                    reads=["qT", "kT"], writes=["PA"])
                P.emit("vector", lambda v: v.tensor_tensor(
                    out=ktm_f[:], in0=PA[:, 0:128], in1=ultri_f[:], op=ALU.mult),
                    reads=["PA", "ultri_f"], writes=["ktm_f"])
                P.emit("tensor", lambda t: t.matmul(
                    PC[0:1, 0:128], ones_f[:, 0:1], ktm_f[:], start=True, stop=False),
                    reads=["ktm_f", "ones_f"], writes=["PC"])
                P.emit("tensor", lambda t, cs=cs: t.matmul(
                    PC[0:1, 0:128], st_b[:, H0:H0 + 1], qT_s[:, cs],
                    start=False, stop=True),
                    reads=["st_b", "qT"], writes=["PC"])
                P.emit("vector", lambda v: v.tensor_scalar_max(
                    out=rd_row[:], in0=PC[0:1, 0:128], scalar1=1e-6),
                    reads=["PC"], writes=["rd_row"])
                P.emit("vector", lambda v: v.reciprocal(out=rd_row[:], in_=rd_row[:]),
                    reads=["rd_row"], writes=["rd_row"])
                P.emit("tensor", lambda t: t.matmul(
                    PX[0:64, 0:128], ones_f[0:1, 0:64], rd_row[:], start=True, stop=True),
                    reads=["rd_row", "ones_f"], writes=["PX"])
                P.emit("vector", lambda v, cs=cs: v.tensor_tensor(
                    out=qs_b[:], in0=qT_s[:, cs], in1=PX[0:64, 0:128], op=ALU.mult),
                    reads=["qT", "PX"], writes=["qs_b"])
                P.emit("tensor", lambda t, cs=cs: t.matmul(
                    PA[:, 128:256], vT[0:64, cs], qs_b[:], start=True, stop=True),
                    reads=["vT", "qs_b"], writes=["PA"])
                P.emit("vector", lambda v: v.tensor_tensor(
                    out=wtm_b[:], in0=PA[:, 128:256], in1=ultri_f[:], op=ALU.mult),
                    reads=["PA", "ultri_f"], writes=["wtm_b"])
                P.emit("tensor", lambda t: t.matmul(
                    PB[0:64, 0:128], st_b[:, 0:H0], qs_b[:], start=True, stop=False),
                    reads=["st_b", "qs_b"], writes=["PB"])
                P.emit("tensor", lambda t, ktm_s=ktm_s, sl=sl: t.matmul(
                    PB[0:64, 0:128], ktm_s, wtm_b[:], start=False, stop=True),
                    reads=["wtm_b", "ktm" + str(sl)], writes=["PB"])
                P.emit("vector", lambda v, cs=cs: v.tensor_copy(
                    out=oT[:, cs], in_=PB[0:64, 0:128]), reads=["PB"], writes=["oT"])
                P.emit("tensor", lambda t, vtm_s=vtm_s, ktm_s=ktm_s, sl=sl: t.matmul(
                    PX[0:64, 128:192], vtm_s, ktm_s, start=True, stop=True),
                    reads=["ktm" + str(sl), "vtm" + str(sl)], writes=["PX"])
                P.emit("tensor", lambda t, ktm_s=ktm_s, sl=sl: t.matmul(
                    PX[0:64, 192:193], ktm_s, onecol_b[:], start=True, stop=True),
                    reads=["onecol_b", "ktm" + str(sl)], writes=["PX"])
                P.emit("vector", lambda v, stf=stf: v.tensor_tensor(
                    out=stf, in0=stf, in1=PX[0:64, 128:193], op=ALU.add),
                    reads=["PX", "st_f"], writes=["st_f"])
                P.emit("vector", lambda v, stf=stf: v.tensor_copy(out=st_b[:], in_=stf),
                    reads=["st_f"], writes=["st_b"])

            for c in range(8):
                hs = c % 2
                hq = hsq[:, hs * 512:(hs + 1) * 512]
                P.emit("scalar", lambda sc, c=c, hq=hq: sc.activation(
                    out=hq, in_=xnT3[:, c, :], func=AF.Square),
                    reads=["xnT"], writes=[f"hsq{hs}"])
                P.emit("tensor", lambda t, c=c, hq=hq: t.matmul(
                    PX[:, 0:512], ones_b[:], hq, start=(c == 0), stop=(c == 7)),
                    reads=[f"hsq{hs}", "ones_b"], writes=["PX"])
            P.emit("scalar", lambda sc: sc.activation(
                out=t1s[:], in_=PX[:, 0:512], func=AF.Sqrt, scale=1.0 / D),
                reads=["PX"], writes=["t1s"])
            P.emit("vector", lambda v: v.tensor_scalar_add(
                out=t1s[:], in0=t1s[:], scalar1=RMS_EPS), reads=["t1s"], writes=["t1s"])
            P.emit("vector", lambda v: v.reciprocal(out=t1s[:], in_=t1s[:]),
                reads=["t1s"], writes=["t1s"])
            for c in range(8):
                P.emit("vector", lambda v, c=c: v.tensor_tensor(
                    out=xnormT[:, c * 512:(c + 1) * 512], in0=xnT3[:, c, :],
                    in1=t1s[:], op=ALU.mult),
                    reads=["xnT", "t1s"], writes=["xnormT"])

            def emit_w1load(fb):
                wsl = fb % 2
                P.emit("sync", lambda s, fb=fb, wsl=wsl: s.dma_start(
                    out=w1sb[:, wsl * 1024:(wsl + 1) * 1024].rearrange(
                        "p (c f) -> p c f", c=8),
                    in_=w1t_d[fb].rearrange("c p f -> p c f")),
                    writes=[f"w1sb{wsl}"], dma=True, stream="w1")
            emit_w1load(0)
            for fb in range(32):
                wsl = fb % 2
                if fb + 1 < 32:
                    emit_w1load(fb + 1)
                ppy = PA if fb % 2 == 0 else PB
                ppy_res = "PA" if fb % 2 == 0 else "PB"
                for c in range(8):
                    P.emit("tensor", lambda t, fb=fb, c=c, wsl=wsl, ppy=ppy: t.matmul(
                        ppy[:, 0:512],
                        w1sb[:, wsl * 1024 + c * 128:wsl * 1024 + (c + 1) * 128],
                        xnormT[:, c * 512:(c + 1) * 512],
                        start=(c == 0), stop=(c == 7)),
                        reads=[f"w1sb{wsl}", "xnormT"], writes=[ppy_res])
                P.emit("scalar", lambda sc, fb=fb, ppy=ppy: sc.activation(
                    out=hbuf[:, fb * 512:(fb + 1) * 512], in_=ppy[:, 0:512],
                    func=AF.Gelu), reads=[ppy_res], writes=[f"hb{fb}"])
                hs = fb % 2
                hq = hsq[:, hs * 512:(hs + 1) * 512]
                P.emit("scalar", lambda sc, fb=fb, hq=hq: sc.activation(
                    out=hq, in_=hbuf[:, fb * 512:(fb + 1) * 512], func=AF.Square),
                    reads=[f"hb{fb}"], writes=[f"hsq{hs}"])
                P.emit("tensor", lambda t, fb=fb, hq=hq: t.matmul(
                    PY[:, 0:512], ones_b[:], hq, start=(fb == 0), stop=(fb == 31)),
                    reads=[f"hsq{hs}", "ones_b"], writes=["PY"])
            P.emit("scalar", lambda sc: sc.activation(
                out=t2s[:], in_=PY[:, 0:512], func=AF.Sqrt, scale=1.0 / DFF),
                reads=["PY"], writes=["t2s"])
            P.emit("vector", lambda v: v.tensor_scalar_add(
                out=t2s[:], in0=t2s[:], scalar1=RMS_EPS), reads=["t2s"], writes=["t2s"])
            P.emit("vector", lambda v: v.reciprocal(out=t2s[:], in_=t2s[:]),
                reads=["t2s"], writes=["t2s"])

            def emit_w2load(dc, hh):
                wsl = hh
                P.emit("sync", lambda s, dc=dc, hh=hh, wsl=wsl: s.dma_start(
                    out=w2sb[:, wsl * 2048:(wsl + 1) * 2048].rearrange(
                        "p (c f) -> p c f", c=16),
                    in_=w2t_d[dc, hh * 16:(hh + 1) * 16].rearrange(
                        "c p f -> p c f")),
                    writes=[f"w2sb{wsl}"], dma=True, stream="w2")
            emit_w2load(0, 0)
            for dc in range(8):
                emit_w2load(dc, 1)
                ppz = PA if dc % 2 == 0 else PB
                ppz_res = "PA" if dc % 2 == 0 else "PB"
                for fc in range(32):
                    hh = fc // 16
                    wsl = hh
                    fcl = fc % 16
                    P.emit("tensor", lambda t, fc=fc, wsl=wsl, fcl=fcl, ppz=ppz: t.matmul(
                        ppz[:, 0:512],
                        w2sb[:, wsl * 2048 + fcl * 128:wsl * 2048 + (fcl + 1) * 128],
                        hbuf[:, fc * 512:(fc + 1) * 512],
                        start=(fc == 0), stop=(fc == 31)),
                        reads=[f"w2sb{wsl}", f"hb{fc}"], writes=[ppz_res])
                if dc + 1 < 8:
                    emit_w2load(dc + 1, 0)
                P.emit("tensor", lambda t, dc=dc, tslc=tslc: t.matmul(
                    PC[:, 0:512], ow_s[0:64, dc * 128:(dc + 1) * 128], oT[:, tslc],
                    start=True, stop=True), reads=["ow_s", "oT"], writes=["PC"])
                P.emit("vector", lambda v, dc=dc, ppz=ppz: v.tensor_tensor(
                    out=sum_fm[:, dc * 512:(dc + 1) * 512], in0=ppz[:, 0:512],
                    in1=t2s[:], op=ALU.mult),
                    reads=[ppz_res, "t2s"], writes=[f"sfm{dc}"])
                P.emit("vector", lambda v, dc=dc: v.tensor_tensor(
                    out=sum_fm[:, dc * 512:(dc + 1) * 512],
                    in0=sum_fm[:, dc * 512:(dc + 1) * 512],
                    in1=PC[:, 0:512], op=ALU.add),
                    reads=[f"sfm{dc}", "PC"], writes=[f"sfm{dc}"])

            for ts in range(4):
                rows = slice(row0 + g * 512 + ts * 128, row0 + g * 512 + ts * 128 + 128)
                for dc in range(8):
                    P.emit("tensor", lambda t, dc=dc, ts=ts: t.transpose(
                        PD[:, dc * 128:(dc + 1) * 128],
                        sum_fm[:, dc * 512 + ts * 128:dc * 512 + (ts + 1) * 128],
                        ident_f[:]),
                        reads=[f"sfm{dc}", "ident_f", "x1tok"], writes=["PD"])
                P.emit("sync", lambda s, rows=rows: s.dma_start(
                    out=xrow[:, 0:D], in_=x_d[rows, :]), writes=["xrow0"], dma=True, stream="xin")
                P.emit("vector", lambda v: v.tensor_tensor(
                    out=x1tok[:], in0=xrow[:, 0:D], in1=PD[:, 0:1024], op=ALU.add),
                    reads=["xrow0", "PD"], writes=["x1tok"])
                P.emit("gpsimd", lambda g_, rows=rows: g_.dma_start(
                    out=x1_d[rows, :], in_=x1tok[:]), reads=["x1tok"], dma=True, stream="ost")
                layernorm_tile("x1tok", x1tok[:], xn1tok_f[:], "xn1f")
                P.emit("vector", lambda v: v.tensor_copy(
                    out=xn1tok_b[:], in_=xn1tok_f[:]), reads=["xn1f"], writes=["xn1b"])
                P.emit("gpsimd", lambda g_, rows=rows: g_.dma_start(
                    out=xn1_d[rows, :], in_=xn1tok_b[:]), reads=["xn1b"], dma=True, stream="ost")
                for c in range(8):
                    P.emit("tensor", lambda t, c=c: t.transpose(
                        PX[:, 0:128], xn1tok_f[:, c * 128:(c + 1) * 128], ident_f[:]),
                        reads=["xn1f", "ident_f", "xn1T"], writes=["PX"])
                    P.emit("vector", lambda v: v.tensor_copy(
                        out=xn1T_f[:], in_=PX[:, 0:128]), reads=["PX"], writes=["xn1T"])
                    P.emit("tensor", lambda t, c=c: t.matmul(
                        PY[0:8, 0:128], gw_s[:, c * NE:(c + 1) * NE], xn1T_f[:],
                        start=(c == 0), stop=(c == 7)),
                        reads=["gw_s", "xn1T"], writes=["PY"])
                P.emit("vector", lambda v: v.tensor_copy(
                    out=lgt_row[:], in_=PY[0:8, 0:128]), reads=["PY"], writes=["lgt_row"])
                P.emit("tensor", lambda t: t.transpose(
                    PX[:, 128:136], lgt_row[:], ident_f[0:8, 0:8]),
                    reads=["lgt_row", "ident_f", "lgt_f"], writes=["PX"])
                P.emit("vector", lambda v: v.tensor_copy(
                    out=lgt_f[:], in_=PX[0:128, 128:136]), reads=["PX"], writes=["lgt_f"])
                P.emit("gpsimd", lambda g_, rows=rows: g_.dma_start(
                    out=lgt_d[rows, :], in_=lgt_f[:]), reads=["lgt_f"], dma=True, stream="ost")

    toks = P.barrier_token()
    P.emit("vector", lambda v: v.engine_nop(), waits=toks)

    P.finalize(blk)
    es.close()
    return nc





def build_launch2(core_id_unused=None):
    nc = bass.Bass("TRN2", target_bir_lowering=False, debug=False)
    x1_d = nc.dram_tensor("x1", [TOK, D], F32, kind="ExternalInput").ap()
    xn1_d = nc.dram_tensor("xn1", [TOK, D], BF16, kind="ExternalInput").ap()
    lgt_d = nc.dram_tensor("lgt", [NTOK_ALL, NE], F32, kind="ExternalInput").ap()
    w1_d = nc.dram_tensor("w1", [D, 3 * D], F32, kind="ExternalInput").ap()
    b1_d = nc.dram_tensor("b1", [3 * D], F32, kind="ExternalInput").ap()
    wp_d = nc.dram_tensor("wp", [D, D], F32, kind="ExternalInput").ap()
    bp_d = nc.dram_tensor("bp", [D], F32, kind="ExternalInput").ap()
    wi_d = nc.dram_tensor("wi", [NE, D, 2 * FEXP], F32, kind="ExternalInput").ap()
    wo_d = nc.dram_tensor("wo", [NE, FEXP, D], F32, kind="ExternalInput").ap()
    out_d = nc.dram_tensor("out", [TOK, D], F32, kind="ExternalOutput").ap()

    NPOWN = TOK // 512
    coef_d = nc.dram_tensor("coefsc", [NE, NPOWN, 512], F32).ap()
    moe_d = nc.dram_tensor("moesc", [8, 128, TOK], F32).ap()

    es = ExitStack()
    P = Prog(nc, es, streams=(
        "wsmall", "wsmall2", "lgt", "xt", "mz", "wi", "wo2", "macc", "coef",
        "coef2", "xin", "ost", "moin"))

    def sb(name, shape, dtype):
        return es.enter_context(nc.sbuf_tensor(name, shape, dtype))

    def psum(name, shape, dtype):
        return es.enter_context(nc.psum_tensor(name, shape, dtype))

    ones_f = sb("ones_f", [128, 128], F32)
    ones_b = sb("ones_b", [128, 128], BF16)
    ident_b = sb("ident_b", [128, 128], BF16)
    ident_f = sb("ident_f", [128, 128], F32)
    ltm_f = sb("ltm_f", [128, 128], F32)
    cmask = sb("cmask", [128, 128], F32)

    P.emit("vector", lambda v: v.memset(ones_f[:], 1.0), writes=["ones_f"])
    P.emit("vector", lambda v: v.memset(ones_b[:], 1.0), writes=["ones_b"])
    P.emit("gpsimd", lambda g: g.affine_select(
        ident_b[:], ones_b[:], [[-1, 128]], ALU.is_equal, 0.0, channel_multiplier=1),
        reads=["ones_b"], writes=["ident_b"])
    P.emit("gpsimd", lambda g: g.affine_select(
        ident_f[:], ones_f[:], [[-1, 128]], ALU.is_equal, 0.0, channel_multiplier=1),
        reads=["ones_f"], writes=["ident_f"])
    P.emit("gpsimd", lambda g: g.affine_select(
        ltm_f[:], ones_f[:], [[1, 128]], ALU.is_gt, 0.0, channel_multiplier=-1),
        reads=["ones_f"], writes=["ltm_f"])
    P.emit("vector", lambda v: v.memset(cmask[:], 0.0), writes=["cmask"])
    P.emit("gpsimd", lambda g: g.affine_select(
        cmask[:], cmask[:], [[-1, 128]], ALU.is_ge, -1e6, channel_multiplier=1),
        reads=["cmask"], writes=["cmask"])

    NA = NTOK_ALL // 128
    esR = ExitStack()

    def sbR(name, shape, dtype):
        return esR.enter_context(nc.sbuf_tensor(name, shape, dtype))

    lgt_sb = sbR("lgt_sb", [128, NA * NE], F32)
    l3 = lgt_sb[:].rearrange("p (a e) -> p a e", e=NE)
    l3e = lgt_sb[:].rearrange("p (a e) -> p a e", e=NE).transpose([0, 2, 1])
    iota_e = sbR("iota_e", [128, NA * NE], F32)
    i3 = iota_e[:].rearrange("p (a e) -> p a e", e=NE)
    m1 = sbR("m1", [128, NA], F32)
    m2 = sbR("m2", [128, NA], F32)
    idx1 = sbR("idx1", [128, NA], F32)
    idx2 = sbR("idx2", [128, NA], F32)
    e2g = sbR("e2g", [128, NA], F32)
    oh1 = sbR("oh1", [128, NA * NE], F32)
    oh2 = sbR("oh2", [128, NA * NE], F32)
    o13 = oh1[:].rearrange("p (a e) -> p a e", e=NE)
    o23 = oh2[:].rearrange("p (a e) -> p a e", e=NE)
    scr = sbR("scr", [128, NA * NE], F32)
    s3 = scr[:].rearrange("p (a e) -> p a e", e=NE)
    s3e = scr[:].rearrange("p (a e) -> p a e", e=NE).transpose([0, 2, 1])
    scan1 = sbR("scan1", [128, NA * NE], F32)
    c13 = scan1[:].rearrange("p (a e) -> p a e", e=NE)
    c13e = scan1[:].rearrange("p (a e) -> p a e", e=NE).transpose([0, 2, 1])
    scan2 = sbR("scan2", [128, NA * NE], F32)
    c23 = scan2[:].rearrange("p (a e) -> p a e", e=NE)
    c23e = scan2[:].rearrange("p (a e) -> p a e", e=NE).transpose([0, 2, 1])
    cnt1 = sbR("cnt1", [128, NE], F32)
    cnt2 = sbR("cnt2", [128, NE], F32)
    base1 = sbR("base1", [128, NE], F32)
    base2 = sbR("base2", [128, NE], F32)
    tot1r = sbR("tot1r", [1, NE], F32)
    loc1 = sbR("loc1", [128, NA], F32)
    loc2 = sbR("loc2", [128, NA], F32)
    keep1 = sbR("keep1", [128, NA], F32)
    keep2 = sbR("keep2", [128, NA], F32)
    gden = sbR("gden", [128, NA], F32)
    coef = sbR("coef", [128, NA * NE], F32)
    cf3 = coef[:].rearrange("p (a e) -> p a e", e=NE)
    cf3e = coef[:].rearrange("p (a e) -> p a e", e=NE).transpose([0, 2, 1])

    esP1 = ExitStack()
    PR1 = esP1.enter_context(nc.psum_tensor("PR1", [128, 512], F32))
    PR2 = esP1.enter_context(nc.psum_tensor("PR2", [128, 512], F32))

    lgt_pm = lgt_d.rearrange("(p a) e -> p a e", p=128)
    P.emit("sync", lambda s: s.dma_start(out=l3, in_=lgt_pm),
           writes=["lgt_sb"], dma=True, stream="lgt")
    P.emit("gpsimd", lambda g: g.iota(
        iota_e[:], [[0, NA], [1, NE]], channel_multiplier=0,
        allow_small_or_imprecise_dtypes=True), writes=["iota_e"])

    BIG = 1.0e9
    P.emit("vector", lambda v: v.reduce_max(out=m1[:], in_=l3, axis=AX.X),
           reads=["lgt_sb"], writes=["m1"])
    P.emit("vector", lambda v: v.tensor_tensor(
        out=scr[:], in0=l3, in1=m1[:].rearrange("p a -> p a 1" if False else "p (a o) -> p a o", o=1).broadcast_to([128, NA, NE]),
        op=ALU.is_ge), reads=["lgt_sb", "m1"], writes=["scr"])
    P.emit("vector", lambda v: v.tensor_scalar(
        out=scr[:], in0=scr[:], scalar1=-BIG, scalar2=BIG,
        op0=ALU.mult, op1=ALU.add), reads=["scr"], writes=["scr"])
    P.emit("vector", lambda v: v.tensor_tensor(
        out=scr[:], in0=scr[:], in1=iota_e[:], op=ALU.add),
        reads=["scr", "iota_e"], writes=["scr"])
    P.emit("vector", lambda v: v.tensor_reduce(
        out=idx1[:], in_=s3, axis=AX.X, op=ALU.min),
        reads=["scr"], writes=["idx1"])
    P.emit("vector", lambda v: v.tensor_tensor(
        out=oh1[:], in0=iota_e[:],
        in1=idx1[:].rearrange("p (a o) -> p a o", o=1).broadcast_to([128, NA, NE]),
        op=ALU.is_equal), reads=["iota_e", "idx1"], writes=["oh1"])
    P.emit("vector", lambda v: v.tensor_scalar(
        out=scr[:], in0=oh1[:], scalar1=-BIG, scalar2=None, op0=ALU.mult),
        reads=["oh1"], writes=["scr"])
    P.emit("vector", lambda v: v.tensor_tensor(
        out=scr[:], in0=scr[:], in1=lgt_sb[:], op=ALU.add),
        reads=["scr", "lgt_sb"], writes=["scr"])
    P.emit("vector", lambda v: v.reduce_max(out=m2[:], in_=s3, axis=AX.X),
        reads=["scr"], writes=["m2"])
    P.emit("vector", lambda v: v.tensor_tensor(
        out=scr[:], in0=s3,
        in1=m2[:].rearrange("p (a o) -> p a o", o=1).broadcast_to([128, NA, NE]),
        op=ALU.is_ge), reads=["scr", "m2"], writes=["scr"])
    P.emit("vector", lambda v: v.tensor_scalar(
        out=scr[:], in0=scr[:], scalar1=-BIG, scalar2=BIG,
        op0=ALU.mult, op1=ALU.add), reads=["scr"], writes=["scr"])
    P.emit("vector", lambda v: v.tensor_tensor(
        out=scr[:], in0=scr[:], in1=iota_e[:], op=ALU.add),
        reads=["scr", "iota_e"], writes=["scr"])
    P.emit("vector", lambda v: v.tensor_reduce(
        out=idx2[:], in_=s3, axis=AX.X, op=ALU.min),
        reads=["scr"], writes=["idx2"])
    P.emit("vector", lambda v: v.tensor_tensor(
        out=oh2[:], in0=iota_e[:],
        in1=idx2[:].rearrange("p (a o) -> p a o", o=1).broadcast_to([128, NA, NE]),
        op=ALU.is_equal), reads=["iota_e", "idx2"], writes=["oh2"])
    P.emit("vector", lambda v: v.tensor_tensor(
        out=e2g[:], in0=m2[:], in1=m1[:], op=ALU.subtract),
        reads=["m2", "m1"], writes=["e2g"])
    P.emit("scalar", lambda sc: sc.activation(
        out=e2g[:], in_=e2g[:], func=AF.Exp), reads=["e2g"], writes=["e2g"])
    for e in range(NE):
        P.emit("vector", lambda v, e=e: v.tensor_tensor_scan(
            out=c13[:, :, e], data0=o13[:, :, e], data1=o13[:, :, e],
            initial=0.0, op0=ALU.mult, op1=ALU.add),
            reads=["oh1"], writes=["scan1"])
        P.emit("vector", lambda v, e=e: v.tensor_tensor_scan(
            out=c23[:, :, e], data0=o23[:, :, e], data1=o23[:, :, e],
            initial=0.0, op0=ALU.mult, op1=ALU.add),
            reads=["oh2"], writes=["scan2"])
    P.emit("vector", lambda v: v.reduce_sum(out=cnt1[:], in_=o13.transpose([0, 2, 1]),
        axis=AX.X), reads=["oh1"], writes=["cnt1"])
    P.emit("vector", lambda v: v.reduce_sum(out=cnt2[:], in_=o23.transpose([0, 2, 1]),
        axis=AX.X), reads=["oh2"], writes=["cnt2"])
    P.emit("tensor", lambda t: t.matmul(
        PR1[0:128, 0:NE], ltm_f[:], cnt1[:], start=True, stop=True),
        reads=["ltm_f", "cnt1"], writes=["PR1"])
    P.emit("vector", lambda v: v.tensor_copy(out=base1[:], in_=PR1[0:128, 0:NE]),
        reads=["PR1"], writes=["base1"])
    P.emit("tensor", lambda t: t.matmul(
        PR1[0:1, 32:32 + NE], ones_f[:, 0:1], cnt1[:], start=True, stop=True),
        reads=["ones_f", "cnt1"], writes=["PR1"])
    P.emit("vector", lambda v: v.tensor_copy(out=tot1r[:], in_=PR1[0:1, 32:32 + NE]),
        reads=["PR1"], writes=["tot1r"])
    P.emit("tensor", lambda t: t.matmul(
        PR2[0:128, 0:NE], ltm_f[:], cnt2[:], start=True, stop=False),
        reads=["ltm_f", "cnt2"], writes=["PR2"])
    P.emit("tensor", lambda t: t.matmul(
        PR2[0:128, 0:NE], ones_f[0:1, :], tot1r[:], start=False, stop=True),
        reads=["ones_f", "tot1r"], writes=["PR2"])
    P.emit("vector", lambda v: v.tensor_copy(out=base2[:], in_=PR2[0:128, 0:NE]),
        reads=["PR2"], writes=["base2"])
    for (o3, c3, basex, locx, kres) in (
            (o13, c13, base1, loc1, ("oh1", "scan1", "base1", "loc1")),
            (o23, c23, base2, loc2, ("oh2", "scan2", "base2", "loc2"))):
        ohr, scr_r, baser, locr = kres
        P.emit("vector", lambda v, c3=c3, basex=basex, o3=o3: v.tensor_tensor(
            out=s3, in0=c3,
            in1=basex[:].rearrange("p (o e) -> p o e", o=1).broadcast_to([128, NA, NE]),
            op=ALU.add), reads=[scr_r, baser], writes=["scr"])
        P.emit("vector", lambda v, o3=o3: v.tensor_tensor(
            out=scr[:], in0=scr[:], in1=(oh1[:] if o3 is o13 else oh2[:]),
            op=ALU.mult), reads=["scr", ohr], writes=["scr"])
        P.emit("vector", lambda v, locx=locx: v.reduce_sum(
            out=locx[:], in_=s3, axis=AX.X), reads=["scr"], writes=[locr])
    P.emit("vector", lambda v: v.tensor_scalar(
        out=keep1[:], in0=loc1[:], scalar1=float(CAP), scalar2=None, op0=ALU.is_le),
        reads=["loc1"], writes=["keep1"])
    P.emit("vector", lambda v: v.tensor_scalar(
        out=keep2[:], in0=loc2[:], scalar1=float(CAP), scalar2=None, op0=ALU.is_le),
        reads=["loc2"], writes=["keep2"])
    P.emit("vector", lambda v: v.tensor_tensor(
        out=e2g[:], in0=e2g[:], in1=keep2[:], op=ALU.mult),
        reads=["e2g", "keep2"], writes=["e2g"])
    P.emit("vector", lambda v: v.tensor_tensor(
        out=gden[:], in0=keep1[:], in1=e2g[:], op=ALU.add),
        reads=["keep1", "e2g"], writes=["gden"])
    P.emit("vector", lambda v: v.tensor_scalar_max(
        out=gden[:], in0=gden[:], scalar1=1e-9), reads=["gden"], writes=["gden"])
    P.emit("vector", lambda v: v.reciprocal(out=gden[:], in_=gden[:]),
        reads=["gden"], writes=["gden"])
    P.emit("vector", lambda v: v.tensor_tensor(
        out=keep1[:], in0=keep1[:], in1=gden[:], op=ALU.mult),
        reads=["keep1", "gden"], writes=["keep1"])
    P.emit("vector", lambda v: v.tensor_tensor(
        out=e2g[:], in0=e2g[:], in1=gden[:], op=ALU.mult),
        reads=["e2g", "gden"], writes=["e2g"])
    P.emit("vector", lambda v: v.tensor_tensor(
        out=coef[:], in0=oh1[:],
        in1=keep1[:].rearrange("p (a o) -> p a o", o=1).broadcast_to([128, NA, NE]),
        op=ALU.mult), reads=["oh1", "keep1"], writes=["coef"])
    P.emit("vector", lambda v: v.tensor_tensor(
        out=scr[:], in0=oh2[:],
        in1=e2g[:].rearrange("p (a o) -> p a o", o=1).broadcast_to([128, NA, NE]),
        op=ALU.mult), reads=["oh2", "e2g"], writes=["scr"])
    P.emit("vector", lambda v: v.tensor_tensor(
        out=coef[:], in0=coef[:], in1=scr[:], op=ALU.add),
        reads=["coef", "scr"], writes=["coef"])
    sel_d = nc.dram_tensor("sel", [128, NPOWN], F32, kind="ExternalInput").ap()
    sel_sb = sbR("sel_sb", [128, NPOWN], F32)
    owncoef = sbR("owncoef", [NPOWN, NA * NE], F32)
    P.emit("sync", lambda s: s.dma_start(out=sel_sb[:], in_=sel_d[:, :]),
           writes=["sel_sb"], dma=True, stream="wsmall2")
    for j in range(NA * NE // 512):
        P.emit("tensor", lambda t, j=j: t.matmul(
            PR1[0:NPOWN, 0:512], sel_sb[:], coef[:, j * 512:(j + 1) * 512],
            start=True, stop=True), reads=["sel_sb", "coef"], writes=["PR1"])
        P.emit("vector", lambda v, j=j: v.tensor_copy(
            out=owncoef[:, j * 512:(j + 1) * 512], in_=PR1[0:NPOWN, 0:512]),
            reads=["PR1"], writes=["owncoef"])
    oc3 = owncoef[:].rearrange("p (a e) -> p a e", e=NE)
    ocrow = sbR("ocrow", [NPOWN, 512], F32)
    for e in range(NE):
        P.emit("vector", lambda v, e=e: v.tensor_copy(
            out=ocrow[:], in_=oc3[:, :, e]), reads=["owncoef"], writes=["ocrow"])
        P.emit("gpsimd", lambda g, e=e: g.dma_start(
            out=coef_d[e, :, :], in_=ocrow[:]),
            reads=["ocrow"], dma=True, stream="coef")

    P.base_waits = P.barrier_token()
    esP1.close()
    esR.close()

    NTB = TOK // 512
    esM = ExitStack()

    def sbM(name, shape, dtype):
        return esM.enter_context(nc.sbuf_tensor(name, shape, dtype))

    zero_sb = sbM("zero_sb", [128, 2048], F32)
    P.emit("vector", lambda v: v.memset(zero_sb[:], 0.0), writes=["zero_sb"])
    for db in range(8):
        for j in range(TOK // 2048):
            P.emit("gpsimd", lambda g, db=db, j=j: g.dma_start(
                out=moe_d[db, :, j * 2048:(j + 1) * 2048], in_=zero_sb[:]),
                reads=["zero_sb"], writes=["moe_dram"], dma=True, stream="mz")

    wi_all = sbM("wi_all", [128, 8 * 4096], BF16)
    actbuf = sbM("actbuf", [128, 16 * 512], BF16)
    ga_sb = sbM("ga_sb", [128, 512], BF16)
    xnT1m = sbM("xnT1m", [128, 8 * 512], BF16)
    xm3 = xnT1m[:].rearrange("p (c t) -> p c t", c=8)
    wo_all = sbM("wo_all", [128, 16 * 1024], BF16)
    coefrow = sbM("coefrow", [1, TOK], F32)
    coefbc = sbM("coefbc", [128, 512], F32)
    mev = sbM("mev", [128, 2 * 512], F32)

    esP2 = ExitStack()
    PM1 = esP2.enter_context(nc.psum_tensor("PM1", [128, 512], F32))
    PM2 = esP2.enter_context(nc.psum_tensor("PM2", [128, 512], F32))
    PDB = esP2.enter_context(nc.psum_tensor("PDB", [128, 4 * 512], F32))


    for e in range(NE):
        P.emit("sync", lambda s, e=e: s.dma_start(
            out=coefrow[0:1, :].rearrange("o (p a) -> o p a", a=512),
            in_=coef_d[e].rearrange("(o p) a -> o p a", o=1)),
            writes=["coefrow"], dma=True, stream="coef2")
        for hc in range(8):
            P.emit("gpsimd", lambda g, e=e, hc=hc: g.dma_start(
                out=wi_all[:, hc * 4096:(hc + 1) * 4096],
                in_=wi_d[e, hc * 128:(hc + 1) * 128, :]),
                writes=["wi_all"], dma=True, stream="wi")
        for fc in range(16):
            P.emit("gpsimd", lambda g, e=e, fc=fc: g.dma_start(
                out=wo_all[:, fc * 1024:(fc + 1) * 1024],
                in_=wo_d[e, fc * 128:(fc + 1) * 128, :]),
                writes=["wo_all"], dma=True, stream="wo2")
        for tb in range(NTB):
            for c in range(8):
                P.emit("sync", lambda s, tb=tb, c=c: s.dma_start(
                    out=xm3[:, c, :],
                    in_=xn1_d[tb * 512:(tb + 1) * 512, c * 128:(c + 1) * 128],
                    transpose=True), writes=["xnT1m"], dma=True, stream="xt")
            P.emit("tensor", lambda t, tb=tb: t.matmul(
                PM1[:, 0:512], ones_f[0:1, :], coefrow[0:1, tb * 512:(tb + 1) * 512],
                start=True, stop=True), reads=["ones_f", "coefrow"], writes=["PM1"])
            P.emit("vector", lambda v: v.tensor_copy(out=coefbc[:], in_=PM1[:, 0:512]),
                reads=["PM1"], writes=["coefbc"])
            for fc in range(16):
                for hc in range(8):
                    P.emit("tensor", lambda t, fc=fc, hc=hc: t.matmul(
                        PM1[:, 0:512],
                        wi_all[:, hc * 4096 + fc * 128:hc * 4096 + (fc + 1) * 128],
                        xm3[:, hc, :], start=(hc == 0), stop=(hc == 7)),
                        reads=["wi_all", "xnT1m"], writes=["PM1"])
                for hc in range(8):
                    P.emit("tensor", lambda t, fc=fc, hc=hc: t.matmul(
                        PM2[:, 0:512],
                        wi_all[:, hc * 4096 + (fc + 16) * 128:hc * 4096 + (fc + 17) * 128],
                        xm3[:, hc, :], start=(hc == 0), stop=(hc == 7)),
                        reads=["wi_all", "xnT1m"], writes=["PM2"])
                P.emit("scalar", lambda sc: sc.activation(
                    out=ga_sb[:], in_=PM1[:, 0:512], func=AF.Gelu),
                    reads=["PM1"], writes=["ga_sb"])
                P.emit("vector", lambda v, fc=fc: v.tensor_tensor(
                    out=actbuf[:, fc * 512:(fc + 1) * 512], in0=ga_sb[:],
                    in1=PM2[:, 0:512], op=ALU.mult),
                    reads=["ga_sb", "PM2"], writes=["actbuf"])
            for half in range(2):
                for fc in range(16):
                    for dbl in range(4):
                        db = half * 4 + dbl
                        P.emit("tensor", lambda t, fc=fc, db=db, dbl=dbl: t.matmul(
                            PDB[:, dbl * 512:(dbl + 1) * 512],
                            wo_all[:, fc * 1024 + db * 128:fc * 1024 + (db + 1) * 128],
                            actbuf[:, fc * 512:(fc + 1) * 512],
                            start=(fc == 0), stop=(fc == 15)),
                            reads=["wo_all", "actbuf"], writes=[f"PDB{dbl}"])
                for dbl in range(4):
                    db = half * 4 + dbl
                    sl = dbl % 2
                    P.emit("vector", lambda v, dbl=dbl, sl=sl: v.tensor_tensor(
                        out=mev[:, sl * 512:(sl + 1) * 512],
                        in0=PDB[:, dbl * 512:(dbl + 1) * 512], in1=coefbc[:],
                        op=ALU.mult),
                        reads=[f"PDB{dbl}", "coefbc"], writes=[f"mev{sl}"])
                    P.emit("gpsimd", lambda g, db=db, tb=tb, sl=sl: g.dma_start(
                        out=moe_d[db, :, tb * 512:(tb + 1) * 512],
                        in_=mev[:, sl * 512:(sl + 1) * 512], accum_op=ALU.add),
                        reads=[f"mev{sl}"], writes=["moe_dram"], dma=True,
                        stream="macc")

    P.base_waits = P.barrier_token()
    esP2.close()
    esM.close()

    w1_sb = sb("w1_sb", [128, 8 * 3072], BF16)
    wp_sb = sb("wp_sb", [128, 8 * 1024], BF16)
    bcol = sb("bcol", [128, 24], F32)
    bpcol = sb("bpcol", [128, 8], F32)
    brow = sb("brow", [1, 3072], F32)
    xnT1a = sb("xnT1a", [128, 8 * T], BF16)
    xa3 = xnT1a[:].rearrange("p (c t) -> p c t", c=8)
    qT2_sb = sb("qT2_sb", [H0, T], BF16)
    kT2_sb = sb("kT2_sb", [H0, T], BF16)
    vtm_sb = sb("vtm_sb", [128, 16 * 64], BF16)
    pbuf = sb("pbuf", [128, T], BF16)
    pT_sb = sb("pT_sb", [128, 16 * 512], BF16)
    avT_sb = sb("avT_sb", [128, 8 * T], BF16)
    lsum = sb("lsum", [128, 4], F32)
    negm = sb("negm", [128, 1], F32)
    rlrow = sb("rlrow", [1, 512], F32)
    rlbc = sb("rlbc", [H0, 512], F32)
    sum_fm2 = sb("sum_fm2", [128, 8 * 512], F32)
    moe_sb = sb("moe_sb", [128, 2 * 512], F32)
    xrow2 = sb("xrow2", [128, D], F32)
    outtok = sb("outtok", [128, D], F32)

    for c in range(8):
        P.emit("gpsimd", lambda g, c=c: g.dma_start(
            out=w1_sb[:, c * 3072:(c + 1) * 3072],
            in_=w1_d[c * 128:(c + 1) * 128, :]), writes=["w1_sb"], dma=True,
            stream="wi")
        P.emit("gpsimd", lambda g, c=c: g.dma_start(
            out=wp_sb[:, c * 1024:(c + 1) * 1024],
            in_=wp_d[c * 128:(c + 1) * 128, :]), writes=["wp_sb"], dma=True,
            stream="wi")
    P.emit("sync", lambda s: s.dma_start(
        out=brow[0:1, :], in_=b1_d[:].rearrange("(o f) -> o f", o=1)),
        writes=["brow"], dma=True, stream="coef2")
    for cb in range(24):
        P.emit("tensor", lambda t, cb=cb: t.transpose(
            PQK[0:128, 0:1], brow[0:1, cb * 128:(cb + 1) * 128], ident_f[0:1, 0:1]),
            reads=["brow", "ident_f"], writes=["PQK"])
        P.emit("vector", lambda v, cb=cb: v.tensor_copy(
            out=bcol[:, cb:cb + 1], in_=PQK[0:128, 0:1]),
            reads=["PQK"], writes=["bcol"])
    P.emit("sync", lambda s: s.dma_start(
        out=brow[0:1, 0:1024], in_=bp_d[:].rearrange("(o f) -> o f", o=1)),
        writes=["brow"], dma=True, stream="coef2")
    for cb in range(8):
        P.emit("tensor", lambda t, cb=cb: t.transpose(
            PQK[0:128, 0:1], brow[0:1, cb * 128:(cb + 1) * 128], ident_f[0:1, 0:1]),
            reads=["brow", "ident_f"], writes=["PQK"])
        P.emit("vector", lambda v, cb=cb: v.tensor_copy(
            out=bpcol[:, cb:cb + 1], in_=PQK[0:128, 0:1]),
            reads=["PQK"], writes=["bpcol"])

    PS = psum("PS", [128, 2048], F32)
    PEb2 = psum("PEb2", [128, 1024], BF16)
    PAV = psum("PAV", [128, 512], F32)
    PQK = psum("PQK", [128, 512], F32)

    for b in range(B_LOC):
        row0 = b * T
        for c in range(8):
            P.emit("sync", lambda s, b=b, c=c: s.dma_start(
                out=xa3[:, c, :],
                in_=xn1_d[row0:row0 + T, c * 128:(c + 1) * 128],
                transpose=True), writes=["xnT1a"], dma=True, stream="xt")
        for hh in range(NH):
            qcol = hh * 192
            kcol = hh * 192 + 64
            vcol = hh * 192 + 128
            for (col, dstbuf, nm) in ((qcol, qT2_sb, "qk_q"), (kcol, kT2_sb, "qk_k")):
                cb = col // 128
                off = col % 128
                for tg in range(4):
                    for c in range(8):
                        P.emit("tensor", lambda t, col=col, tg=tg, c=c: t.matmul(
                            PQK[0:64, 0:512],
                            w1_sb[:, c * 3072 + col:c * 3072 + col + 64],
                            xa3[:, c, tg * 512:(tg + 1) * 512],
                            start=(c == 0), stop=(c == 7)),
                            reads=["w1_sb", "xnT1a"], writes=["PQK"])
                    P.emit("scalar", lambda sc, dstbuf=dstbuf, tg=tg, cb=cb, off=off: sc.activation(
                        out=dstbuf[:, tg * 512:(tg + 1) * 512],
                        in_=PQK[0:64, 0:512], func=AF.Identity,
                        bias=bcol[off:off + 64, cb:cb + 1]),
                        reads=["PQK", "bcol"], writes=[nm])
            for tg in range(4):
                for c in range(8):
                    P.emit("tensor", lambda t, vcol=vcol, tg=tg, c=c: t.matmul(
                        PQK[0:64, 0:512],
                        w1_sb[:, c * 3072 + vcol:c * 3072 + vcol + 64],
                        xa3[:, c, tg * 512:(tg + 1) * 512],
                        start=(c == 0), stop=(c == 7)),
                        reads=["w1_sb", "xnT1a"], writes=["PQK"])
                vcb = (vcol) // 128
                voff = vcol % 128
                P.emit("scalar", lambda sc, tg=tg, vcb=vcb, voff=voff: sc.activation(
                    out=pbuf[0:64, tg * 512:(tg + 1) * 512], in_=PQK[0:64, 0:512],
                    func=AF.Identity, bias=bcol[voff:voff + 64, vcb:vcb + 1]),
                    reads=["PQK", "bcol"], writes=["pbuf"])
            for tt in range(16):
                P.emit("tensor", lambda t, tt=tt: t.transpose(
                    PEb2[:, 0:64], pbuf[0:64, tt * 128:(tt + 1) * 128],
                    ident_b[0:64, 0:64]),
                    reads=["pbuf", "ident_b", "vtm"], writes=["PEb2"])
                P.emit("vector", lambda v, tt=tt: v.tensor_copy(
                    out=vtm_sb[:, tt * 64:(tt + 1) * 64], in_=PEb2[0:128, 0:64]),
                    reads=["PEb2"], writes=["vtm"])
            for qi in range(16):
                kmax = (qi + 1) * 128
                nkb = ceil_div(kmax, 512)
                for kb in range(nkb):
                    klen = min(512, kmax - kb * 512)
                    P.emit("tensor", lambda t, qi=qi, kb=kb, klen=klen: t.matmul(
                        PS[:, kb * 512:kb * 512 + klen],
                        qT2_sb[:, qi * 128:(qi + 1) * 128],
                        kT2_sb[:, kb * 512:kb * 512 + klen],
                        start=True, stop=True),
                        reads=["qk_q", "qk_k"], writes=["PS"])
                P.emit("vector", lambda v, qi=qi: v.tensor_tensor(
                    out=PS[:, qi * 128:(qi + 1) * 128],
                    in0=PS[:, qi * 128:(qi + 1) * 128], in1=cmask[:], op=ALU.add),
                    reads=["PS", "cmask"], writes=["PS"])
                P.emit("vector", lambda v, qi=qi, kmax=kmax: v.reduce_max(
                    out=negm[:], in_=PS[:, 0:kmax], axis=AX.X),
                    reads=["PS"], writes=["negm"])
                P.emit("vector", lambda v: v.tensor_scalar_mul(
                    out=negm[:], in0=negm[:], scalar1=-0.125),
                    reads=["negm"], writes=["negm"])
                sl4 = qi % 4
                P.emit("vector", lambda v, sl4=sl4: v.memset(
                    lsum[:, sl4:sl4 + 1], 0.0), writes=["lsum"])
                P.emit("scalar", lambda sc, kmax=kmax, sl4=sl4: sc.activation(
                    out=pbuf[:, 0:kmax], in_=PS[:, 0:kmax], func=AF.Exp,
                    bias=negm[:, 0:1], scale=0.125, accum_out=lsum[:, sl4:sl4 + 1]),
                    reads=["PS", "negm", "lsum"], writes=["pbuf", "lsum"])
                for kt in range(qi + 1, (qi // 4) * 4 + 4):
                    P.emit("vector", lambda v, kt=kt, qi=qi: v.memset(
                        pT_sb[:, kt * 512 + (qi % 4) * 128:kt * 512 + (qi % 4) * 128 + 128],
                        0.0), writes=["pT"])
                for kt in range(qi + 1):
                    P.emit("tensor", lambda t, kt=kt: t.transpose(
                        PEb2[:, 0:128], pbuf[:, kt * 128:(kt + 1) * 128], ident_b[:]),
                        reads=["pbuf", "ident_b", "pT"], writes=["PEb2"])
                    P.emit("vector", lambda v, kt=kt, qi=qi: v.tensor_copy(
                        out=pT_sb[:, kt * 512 + (qi % 4) * 128:kt * 512 + (qi % 4) * 128 + 128],
                        in_=PEb2[:, 0:128]), reads=["PEb2"], writes=["pT"])
                if qi % 4 == 3:
                    qg = qi // 4
                    nkt = (qg + 1) * 4
                    for kt in range(nkt):
                        P.emit("tensor", lambda t, kt=kt, nkt=nkt: t.matmul(
                            PAV[0:64, 0:512], vtm_sb[:, kt * 64:(kt + 1) * 64],
                            pT_sb[:, kt * 512:(kt + 1) * 512],
                            start=(kt == 0), stop=(kt == nkt - 1)),
                            reads=["vtm", "pT"], writes=["PAV"])
                    P.emit("vector", lambda v: v.reciprocal(
                        out=lsum[:, 0:4], in_=lsum[:, 0:4]),
                        reads=["lsum"], writes=["lsum"])
                    for j in range(4):
                        P.emit("tensor", lambda t, j=j: t.transpose(
                            PQK[0:1, 0:128], lsum[:, j:j + 1], ident_f[:]),
                            reads=["lsum", "ident_f"], writes=["PQK"])
                        P.emit("vector", lambda v, j=j: v.tensor_copy(
                            out=rlrow[0:1, j * 128:(j + 1) * 128], in_=PQK[0:1, 0:128]),
                            reads=["PQK"], writes=["rlrow"])
                    P.emit("tensor", lambda t: t.matmul(
                        PQK[0:64, 0:512], ones_f[0:1, 0:64], rlrow[0:1, :],
                        start=True, stop=True),
                        reads=["ones_f", "rlrow"], writes=["PQK"])
                    P.emit("vector", lambda v: v.tensor_copy(
                        out=rlbc[:], in_=PQK[0:64, 0:512]),
                        reads=["PQK"], writes=["rlbc"])
                    hslot = hh % 2
                    P.emit("vector", lambda v, hh=hh, qg=qg, hslot=hslot: v.tensor_tensor(
                        out=avT_sb[hslot * 64:hslot * 64 + 64,
                                   (hh // 2) * T + qg * 512:(hh // 2) * T + (qg + 1) * 512],
                        in0=PAV[0:64, 0:512], in1=rlbc[:], op=ALU.mult),
                        reads=["PAV", "rlbc"], writes=["avT"])
        for qg in range(4):
            tslc = slice(qg * 512, (qg + 1) * 512)
            def emit_moeload(db):
                msl = db % 2
                P.emit("sync", lambda s, db=db, b=b, qg=qg, msl=msl: s.dma_start(
                    out=moe_sb[:, msl * 512:(msl + 1) * 512],
                    in_=moe_d[db, :, b * T + qg * 512:b * T + (qg + 1) * 512]),
                    writes=[f"moe_sb{msl}"], dma=True, stream="moin")
            emit_moeload(0)
            for db in range(8):
                if db + 1 < 8:
                    emit_moeload(db + 1)
                for hh in range(NH):
                    P.emit("tensor", lambda t, db=db, hh=hh, qg=qg: t.matmul(
                        PQK[:, 0:512],
                        wp_sb[(hh % 2) * 64:(hh % 2) * 64 + 64,
                              (hh // 4) * 2048 + 0:0 + 0] if False else
                        wp_sb[64 * (hh % 2):64 * (hh % 2) + 64,
                              (hh // 2) * 1024 + db * 128:(hh // 2) * 1024 + (db + 1) * 128],
                        avT_sb[(hh % 2) * 64:(hh % 2) * 64 + 64,
                               (hh // 2) * T + qg * 512:(hh // 2) * T + (qg + 1) * 512],
                        start=(hh == 0), stop=(hh == NH - 1)),
                        reads=["wp_sb", "avT"], writes=["PQK"])
                msl = db % 2
                P.emit("vector", lambda v, db=db: v.tensor_scalar(
                    out=sum_fm2[:, db * 512:(db + 1) * 512], in0=PQK[:, 0:512],
                    scalar1=bpcol[:, db:db + 1], scalar2=None, op0=ALU.add),
                    reads=["PQK", "bpcol"], writes=[f"sf{db}"])
                P.emit("vector", lambda v, db=db, msl=msl: v.tensor_tensor(
                    out=sum_fm2[:, db * 512:(db + 1) * 512],
                    in0=sum_fm2[:, db * 512:(db + 1) * 512],
                    in1=moe_sb[:, msl * 512:(msl + 1) * 512], op=ALU.add),
                    reads=[f"sf{db}", f"moe_sb{msl}"], writes=[f"sf{db}"])
            for ts in range(4):
                rows = slice(row0 + qg * 512 + ts * 128, row0 + qg * 512 + ts * 128 + 128)
                for dc in range(8):
                    P.emit("tensor", lambda t, dc=dc, ts=ts: t.transpose(
                        PS[:, dc * 128:(dc + 1) * 128],
                        sum_fm2[:, dc * 512 + ts * 128:dc * 512 + (ts + 1) * 128],
                        ident_f[:]),
                        reads=[f"sf{dc}", "ident_f", "outtok"], writes=["PS"])
                P.emit("sync", lambda s, rows=rows: s.dma_start(
                    out=xrow2[:], in_=x1_d[rows, :]), writes=["xrow2"], dma=True,
                    stream="xin")
                P.emit("vector", lambda v: v.tensor_tensor(
                    out=outtok[:], in0=xrow2[:], in1=PS[:, 0:1024], op=ALU.add),
                    reads=["xrow2", "PS"], writes=["outtok"])
                P.emit("gpsimd", lambda g, rows=rows: g.dma_start(
                    out=out_d[rows, :], in_=outtok[:]), reads=["outtok"], dma=True,
                    stream="ost")

    toks = P.barrier_token()
    P.emit("vector", lambda v: v.engine_nop(), waits=toks)
    blk = es.enter_context(nc.Block())
    P.finalize(blk)
    es.close()
    return nc

_NC1 = None
_NC2 = None


def kernel(L0_attn_x, L0_attn_W_q, L0_attn_W_k, L0_attn_W_v, L0_attn_W_o,
           L0_ffn_w_up, L0_ffn_w_down, L1_attn_W1, L1_attn_b1, L1_attn_Wp,
           L1_attn_bp, L1_ffn_Wg, L1_ffn_W_in, L1_ffn_W_out):
    global _NC1, _NC2
    x = np.ascontiguousarray(L0_attn_x.reshape(NTOK_ALL, D))
    if _NC1 is None:
        _NC1 = build_launch1()
    in1 = []
    for c in range(N_CORES):
        in1.append({
            "x": x[c * TOK:(c + 1) * TOK],
            "wq": L0_attn_W_q, "wk": L0_attn_W_k, "wv": L0_attn_W_v,
            "wo": L0_attn_W_o, "wup": L0_ffn_w_up, "wdn": L0_ffn_w_down,
            "wg": L1_ffn_Wg,
        })
    r1 = run_bass_kernel_spmd(_NC1, in1, core_ids=list(range(N_CORES)))
    lgt_full = np.concatenate([r["lgt"] for r in r1.results], 0)
    if _NC2 is None:
        _NC2 = build_launch2()
    in2 = []
    for c in range(N_CORES):
        npown = TOK // 512
        sel = np.zeros((128, npown), np.float32)
        for j in range(npown):
            sel[c * npown + j, j] = 1.0
        in2.append({
            "x1": r1.results[c]["x1"], "xn1": r1.results[c]["xn1"],
            "lgt": lgt_full, "sel": sel,
            "w1": L1_attn_W1, "b1": L1_attn_b1, "wp": L1_attn_Wp,
            "bp": L1_attn_bp, "wi": L1_ffn_W_in, "wo": L1_ffn_W_out,
        })
    r2 = run_bass_kernel_spmd(_NC2, in2, core_ids=list(range(N_CORES)))
    out = np.concatenate([r["out"] for r in r2.results], 0)
    return out.reshape(B_ALL, T, D)


if __name__ == "__main__":
    nc = build_launch1()
    print("launch1 built ok; instructions:",
          sum(len(f.instructions) for f in []) if False else "n/a")


# version 05
# speedup vs baseline: 1.6907x; 1.1764x over previous
import math
import numpy as np
from contextlib import ExitStack

import concourse.bass as bass
import concourse.mybir as mybir
from concourse.bass_utils import run_bass_kernel_spmd
from concourse.bass_core_rust import IndirectOffsetOnAxis
import ml_dtypes

F32 = mybir.dt.float32
BF16 = mybir.dt.bfloat16
I32 = mybir.dt.int32
AF = mybir.ActivationFunctionType
ALU = mybir.AluOpType
AX = mybir.AxisListType

N_CORES = 8
B_ALL, T, D = 32, 2048, 1024
B_LOC = B_ALL // N_CORES
TOK = B_LOC * T
NTOK_ALL = B_ALL * T
DFF = 4096
H0 = 64
NH, DH = 16, 64
NE = 8
FEXP = 2048
CAP = int(math.ceil(2 * NTOK_ALL / NE * 1.25))
LN_EPS = 1e-5
RMS_EPS = 1e-8

def ceil_div(a, b):
    return (a + b - 1) // b


ENGINES = ("sync", "tensor", "vector", "scalar", "gpsimd")
COMPUTE_ENGINES = {"tensor", "vector", "scalar"}


class Prog:

    def __init__(self, nc, es, streams=()):
        self.nc = nc
        self.es = es
        self.ops = {e: [] for e in ENGINES}
        self.keys = list(ENGINES) + ["dma_" + s for s in streams]
        self.count = {k: 0 for k in self.keys}
        self.sem = {k: es.enter_context(nc.semaphore(f"prog_{k}")) for k in self.keys}
        self.writer = {}
        self.readers = {}
        self.base_waits = []
        self.guaranteed = {}

    def emit(self, eng, fn, reads=(), writes=(), waits=(), dma=False, stream=None):
        inc = 16 if dma else 1
        if dma:
            assert stream is not None, "DMA emits must name a stream"
            key = "dma_" + stream
        else:
            key = eng
        deps = list(waits) + self.base_waits
        for r in reads:
            w = self.writer.get(r)
            if w is not None:
                deps.append(w)
        for r in writes:
            w = self.writer.get(r)
            if w is not None:
                deps.append(w)
            deps.extend(self.readers.get(r, ()))
        best = {}
        for (de, dv) in deps:
            if best.get(de, -1) < dv:
                best[de] = dv
        if key in best and best[key] > self.count[key]:
            raise RuntimeError(f"self-dep in future on {key}")
        for k in list(best):
            if k.startswith("dma_") and best[k] > 0:
                best[k] = self.count[k]
        g = self.guaranteed.setdefault(eng, {})
        toklist = []
        for k, v in sorted(best.items()):
            if g.get(k, -1) >= v:
                continue
            g[k] = v
            toklist.append((k, v))
        self.ops[eng].append((toklist, fn, inc, key))
        self.count[key] += inc
        tok = (key, self.count[key])
        for r in writes:
            self.writer[r] = tok
            self.readers[r] = []
        for r in reads:
            self.readers.setdefault(r, []).append(tok)
        return tok

    def barrier_token(self):
        return [(k, self.count[k]) for k in self.keys if self.count[k] > 0]

    def finalize(self, blk):
        semmap = self.sem
        for e in ENGINES:
            ops = self.ops[e]
            if not ops:
                continue

            def make_body(ops):
                def body(engobj):
                    for waits, fn, inc, key in ops:
                        for (we, wv) in waits:
                            engobj.wait_ge(semmap[we], wv)
                        fn(engobj).then_inc(semmap[key], inc)
                return body
            getattr(blk, e)(make_body(ops))



def build_launch1():
    nc = bass.Bass("TRN2", target_bir_lowering=False, debug=False)
    x_d = nc.dram_tensor("x", [TOK, D], F32, kind="ExternalInput").ap()
    wq_d = nc.dram_tensor("wq", [D, H0], F32, kind="ExternalInput").ap()
    wk_d = nc.dram_tensor("wk", [D, H0], F32, kind="ExternalInput").ap()
    wv_d = nc.dram_tensor("wv", [D, H0], F32, kind="ExternalInput").ap()
    wo_d = nc.dram_tensor("wo", [H0, D], F32, kind="ExternalInput").ap()
    wup_d = nc.dram_tensor("wup", [DFF, D], F32, kind="ExternalInput").ap()
    wdn_d = nc.dram_tensor("wdn", [D, DFF], F32, kind="ExternalInput").ap()
    wg_d = nc.dram_tensor("wg", [D, NE], F32, kind="ExternalInput").ap()

    x1_d = nc.dram_tensor("x1", [TOK, D], F32, kind="ExternalOutput").ap()
    xn1_d = nc.dram_tensor("xn1", [TOK, D], BF16, kind="ExternalOutput").ap()
    lgt_d = nc.dram_tensor("lgt", [TOK, NE], F32, kind="ExternalOutput").ap()

    w1t_d = nc.dram_tensor("w1t", [DFF // 128, D // 128, 128, 128], BF16).ap()
    w2t_d = nc.dram_tensor("w2t", [D // 128, DFF // 128, 128, 128], BF16).ap()

    es = ExitStack()
    P = Prog(nc, es, streams=("wsmall", "wraw", "wtst", "xin", "w1", "w2", "ost"))

    def sb(name, shape, dtype):
        return es.enter_context(nc.sbuf_tensor(name, shape, dtype))

    def psum(name, shape, dtype):
        return es.enter_context(nc.psum_tensor(name, shape, dtype))

    ones_f = sb("ones_f", [128, 128], F32)
    ones_b = sb("ones_b", [128, 128], BF16)
    ident_b = sb("ident_b", [128, 128], BF16)
    ident_f = sb("ident_f", [128, 128], F32)
    ultri_f = sb("ultri_f", [128, 128], F32)
    onecol_b = sb("onecol_b", [128, 1], BF16)

    qw_s = sb("qw_s", [128, 8 * H0], BF16)
    kw_s = sb("kw_s", [128, 8 * H0], BF16)
    vw_s = sb("vw_s", [128, 8 * H0], BF16)
    ow_s = sb("ow_s", [H0, D], BF16)
    gw_s = sb("gw_s", [128, 8 * NE], F32)


    xrow = sb("xrow", [128, 2 * D], F32)
    xsq = sb("xsq", [128, D], F32)
    s_a = sb("s_a", [128, 1], F32)
    s_b = sb("s_b", [128, 1], F32)
    mu = sb("mu", [128, 1], F32)
    var = sb("var", [128, 1], F32)
    rstd = sb("rstd", [128, 1], F32)
    xn_tok = sb("xn_tok", [128, 2 * D], BF16)
    xnT = sb("xnT", [128, 8 * T], BF16)
    xnT3 = xnT[:].rearrange("p (c t) -> p c t", c=8)

    qT_s = sb("qT_s", [H0, T], BF16)
    kT_s = sb("kT_s", [H0, T], BF16)
    vT = sb("vT", [H0, T], BF16)
    oT = sb("oT", [H0, T], BF16)
    phi_a = sb("phi_a", [128, 512], F32)
    phi_b = sb("phi_b", [128, 512], F32)
    ktm = sb("ktm", [128, 2 * H0], BF16)
    vtm = sb("vtm", [128, 2 * H0], BF16)
    ktm_f = sb("ktm_f", [128, 128], F32)
    wtm_b = sb("wtm_b", [128, 128], BF16)
    rd_row = sb("rd_row", [1, 128], F32)
    qs_b = sb("qs_b", [H0, 128], BF16)
    st_f = sb("st_f", [H0, 4 * (H0 + 1)], F32)
    st_b = sb("st_b", [H0, H0 + 1], BF16)

    w1sb = sb("w1sb", [128, 4 * 8 * 128], BF16)
    w2sb = sb("w2sb", [128, 4 * 16 * 128], BF16)
    xnormT = sb("xnormT", [128, 8 * 512], BF16)
    hbuf = sb("hbuf", [128, 32 * 512], BF16)
    hsq = sb("hsq", [128, 2 * 512], BF16)
    t1s = sb("t1s", [128, 512], F32)
    t2s = sb("t2s", [128, 512], F32)
    sum_fm = sb("sum_fm", [128, 8 * 512], F32)
    x1tok = sb("x1tok", [128, D], F32)
    xn1tok_f = sb("xn1tok_f", [128, D], F32)
    xn1tok_b = sb("xn1tok_b", [128, D], BF16)
    xn1T_f = sb("xn1T_f", [128, 512], F32)
    lgt_row = sb("lgt_row", [8, 128], F32)
    lgt_f = sb("lgt_f", [128, NE], F32)

    esA = ExitStack()

    def sbA(name, shape, dtype):
        return esA.enter_context(nc.sbuf_tensor(name, shape, dtype))

    wld = sbA("wld", [128, 2048], F32)
    wabs = sbA("wabs", [128, 1], F32)
    wsgn = sbA("wsgn", [128, 2048], F32)
    wtb = sbA("wtb", [128, 2048], BF16)
    alpha_acc = sbA("alpha_acc", [1, 2], F32)
    tau_col = sbA("tau_col", [128, 1], F32)
    tr_ev = sbA("tr_ev", [128, 4 * 128], BF16)

    PA = psum("PA", [128, 512], F32)
    PB = psum("PB", [128, 512], F32)
    PC = psum("PC", [128, 512], F32)
    PD = psum("PD", [128, 1024], F32)
    PEb = psum("PEb", [128, 1024], BF16)
    PX = psum("PX", [128, 512], F32)
    PY = psum("PY", [128, 512], F32)

    blk = es.enter_context(nc.Block())

    P.emit("vector", lambda v: v.memset(ones_f[:], 1.0), writes=["ones_f"])
    P.emit("vector", lambda v: v.memset(ones_b[:], 1.0), writes=["ones_b"])
    P.emit("vector", lambda v: v.memset(onecol_b[:], 1.0), writes=["onecol_b"])
    P.emit("gpsimd", lambda g: g.affine_select(
        ident_b[:], ones_b[:], [[-1, 128]], ALU.is_equal, 0.0, channel_multiplier=1),
        reads=["ones_b"], writes=["ident_b"])
    P.emit("gpsimd", lambda g: g.affine_select(
        ident_f[:], ones_f[:], [[-1, 128]], ALU.is_equal, 0.0, channel_multiplier=1),
        reads=["ones_f"], writes=["ident_f"])
    P.emit("gpsimd", lambda g: g.affine_select(
        ultri_f[:], ones_f[:], [[1, 128]], ALU.is_ge, 0.0, channel_multiplier=-1),
        reads=["ones_f"], writes=["ultri_f"])

    wq3 = wq_d.rearrange("(c p) h -> p c h", p=128)
    wk3 = wk_d.rearrange("(c p) h -> p c h", p=128)
    wv3 = wv_d.rearrange("(c p) h -> p c h", p=128)
    wg3 = wg_d.rearrange("(c p) e -> p c e", p=128)
    P.emit("gpsimd", lambda g: g.dma_start(
        out=qw_s[:].rearrange("p (c h) -> p c h", c=8), in_=wq3), writes=["qw_s"], dma=True, stream="wsmall")
    P.emit("gpsimd", lambda g: g.dma_start(
        out=kw_s[:].rearrange("p (c h) -> p c h", c=8), in_=wk3), writes=["kw_s"], dma=True, stream="wsmall")
    P.emit("gpsimd", lambda g: g.dma_start(
        out=vw_s[:].rearrange("p (c h) -> p c h", c=8), in_=wv3), writes=["vw_s"], dma=True, stream="wsmall")
    P.emit("gpsimd", lambda g: g.dma_start(out=ow_s[:], in_=wo_d[:, :]), writes=["ow_s"], dma=True, stream="wsmall")
    P.emit("gpsimd", lambda g: g.dma_start(
        out=gw_s[:].rearrange("p (c e) -> p c e", c=8), in_=wg3), writes=["gw_s"], dma=True, stream="wsmall")

    def tern_alpha(w_d, nrows, ncols, slot):
        row_tiles = nrows // 128
        hunks = ceil_div(ncols, 2048)
        first = True
        for r in range(row_tiles):
            for h in range(hunks):
                hc = min(2048, ncols - h * 2048)
                P.emit("sync", lambda s, r=r, h=h, hc=hc: s.dma_start(
                    out=wld[:, 0:hc],
                    in_=w_d[r * 128:(r + 1) * 128, h * 2048:h * 2048 + hc]),
                    writes=["wld"], dma=True, stream="wraw")
                P.emit("vector", lambda v, hc=hc: v.tensor_reduce(
                    out=wabs[:], in_=wld[:, 0:hc], axis=AX.X, op=ALU.add,
                    apply_absolute_value=True), reads=["wld"], writes=["wabs"])
                P.emit("tensor", lambda t, first=first, last=(r == row_tiles - 1 and h == hunks - 1): t.matmul(
                    PY[0:1, slot:slot + 1], ones_f[:, 0:1], wabs[:],
                    start=first, stop=last),
                    reads=["wabs", "ones_f"], writes=["PY"])
                first = False
        P.emit("vector", lambda v: v.tensor_copy(
            out=alpha_acc[0:1, slot:slot + 1], in_=PY[0:1, slot:slot + 1]),
            reads=["PY"], writes=["alpha_acc"])
        P.emit("tensor", lambda t: t.matmul(
            PY[0:128, 256:257], ones_f[0:1, :], alpha_acc[0:1, slot:slot + 1],
            start=True, stop=True),
            reads=["alpha_acc"], writes=["PY"])
        P.emit("vector", lambda v: v.tensor_scalar_mul(
            out=tau_col[:], in0=PY[0:128, 256:257], scalar1=0.5 / (nrows * ncols)),
            reads=["PY"], writes=["tau_col"])

    def tern_pass2(w_d, nrows, ncols, wt_out_d):
        row_tiles = nrows // 128
        hunks = ceil_div(ncols, 2048)
        for rb in range(row_tiles):
            for h in range(hunks):
                hc = min(2048, ncols - h * 2048)
                P.emit("sync", lambda s, rb=rb, h=h, hc=hc: s.dma_start(
                    out=wld[:, 0:hc],
                    in_=w_d[rb * 128:(rb + 1) * 128, h * 2048:h * 2048 + hc]),
                    writes=["wld"], dma=True, stream="wraw")
                P.emit("scalar", lambda sc, hc=hc: sc.activation(
                    out=wsgn[:, 0:hc], in_=wld[:, 0:hc], func=AF.Sign),
                    reads=["wld"], writes=["wsgn"])
                P.emit("scalar", lambda sc, hc=hc: sc.activation(
                    out=wld[:, 0:hc], in_=wld[:, 0:hc], func=AF.Abs),
                    reads=["wld"], writes=["wld"])
                P.emit("vector", lambda v, hc=hc: v.tensor_scalar(
                    out=wld[:, 0:hc], in0=wld[:, 0:hc], scalar1=tau_col[:, 0:1],
                    scalar2=None, op0=ALU.is_gt),
                    reads=["wld", "tau_col"], writes=["wld"])
                P.emit("vector", lambda v, hc=hc: v.tensor_tensor(
                    out=wtb[:, 0:hc], in0=wsgn[:, 0:hc], in1=wld[:, 0:hc],
                    op=ALU.mult), reads=["wsgn", "wld"], writes=["wtb"])
                for cbl in range(hc // 128):
                    cb = h * 16 + cbl
                    half = cbl % 8
                    tslot = cbl % 4
                    P.emit("tensor", lambda t, cbl=cbl, half=half: t.transpose(
                        PEb[:, half * 128:(half + 1) * 128],
                        wtb[:, cbl * 128:(cbl + 1) * 128], ident_b[:]),
                        reads=["wtb", "ident_b"], writes=["PEb"])
                    P.emit("vector", lambda v, half=half, tslot=tslot: v.tensor_copy(
                        out=tr_ev[:, tslot * 128:(tslot + 1) * 128],
                        in_=PEb[:, half * 128:(half + 1) * 128]),
                        reads=["PEb"], writes=[f"trev{tslot}"])
                    P.emit("gpsimd", lambda g, rb=rb, cb=cb, tslot=tslot: g.dma_start(
                        out=wt_out_d[rb, cb, :, :],
                        in_=tr_ev[:, tslot * 128:(tslot + 1) * 128]),
                        reads=[f"trev{tslot}"], dma=True, stream="wtst")

    tern_alpha(wup_d, DFF, D, 0)
    tern_pass2(wup_d, DFF, D, w1t_d)
    tern_alpha(wdn_d, D, DFF, 1)
    tern_pass2(wdn_d, D, DFF, w2t_d)
    esA.close()
    P.base_waits = P.barrier_token()

    NT = T // 128
    NG = T // 512

    def layernorm_tile(src_res, src_ap, dst_ap, dst_res, out_dtype_pair=None):
        P.emit("vector", lambda v: v.tensor_reduce(
            out=s_a[:], in_=src_ap, axis=AX.X, op=ALU.add),
            reads=[src_res], writes=["s_a"])
        P.emit("scalar", lambda sc: sc.activation(
            out=xsq[:], in_=src_ap, func=AF.Square), reads=[src_res], writes=["xsq"])
        P.emit("vector", lambda v: v.tensor_reduce(
            out=s_b[:], in_=xsq[:], axis=AX.X, op=ALU.add),
            reads=["xsq"], writes=["s_b"])
        P.emit("vector", lambda v: v.tensor_scalar_mul(
            out=mu[:], in0=s_a[:], scalar1=1.0 / D), reads=["s_a"], writes=["mu"])
        P.emit("vector", lambda v: v.tensor_scalar_mul(
            out=var[:], in0=s_b[:], scalar1=1.0 / D), reads=["s_b"], writes=["var"])
        P.emit("vector", lambda v: v.tensor_tensor(
            out=s_b[:], in0=mu[:], in1=mu[:], op=ALU.mult),
            reads=["mu"], writes=["s_b"])
        P.emit("vector", lambda v: v.tensor_tensor(
            out=var[:], in0=var[:], in1=s_b[:], op=ALU.subtract),
            reads=["var", "s_b"], writes=["var"])
        P.emit("vector", lambda v: v.tensor_scalar_add(
            out=var[:], in0=var[:], scalar1=LN_EPS), reads=["var"], writes=["var"])
        P.emit("scalar", lambda sc: sc.activation(
            out=rstd[:], in_=var[:], func=AF.Sqrt),
            reads=["var"], writes=["rstd"])
        P.emit("vector", lambda v: v.reciprocal(out=rstd[:], in_=rstd[:]),
            reads=["rstd"], writes=["rstd"])
        P.emit("vector", lambda v: v.tensor_scalar(
            out=dst_ap, in0=src_ap, scalar1=mu[:, 0:1], scalar2=rstd[:, 0:1],
            op0=ALU.subtract, op1=ALU.mult),
            reads=[src_res, "mu", "rstd"], writes=[dst_res])

    for b in range(B_LOC):
        row0 = b * T

        P.emit("vector", lambda v, b=b: v.memset(
            st_f[:, b * (H0 + 1):(b + 1) * (H0 + 1)], 0.0), writes=["st_f"])
        P.emit("vector", lambda v: v.memset(st_b[:], 0.0), writes=["st_b"])
        stf = st_f[:, b * (H0 + 1):(b + 1) * (H0 + 1)]

        def emit_xload(n):
            slot = n % 2
            xr = xrow[:, slot * D:(slot + 1) * D]
            P.emit("sync", lambda s, n=n, xr=xr, row0=row0: s.dma_start(
                out=xr, in_=x_d[row0 + n * 128:row0 + (n + 1) * 128, :]),
                writes=[f"xrow{slot}"], dma=True, stream="xin")
        emit_xload(0)
        for n in range(16):
            slot = n % 2
            xr = xrow[:, slot * D:(slot + 1) * D]
            xnt = xn_tok[:, slot * D:(slot + 1) * D]
            if n + 1 < 16:
                emit_xload(n + 1)
            layernorm_tile(f"xrow{slot}", xr, xnt, f"xn{slot}")
            for c in range(8):
                P.emit("tensor", lambda t, c=c, xnt=xnt, slot=slot: t.transpose(
                    PEb[:, c * 128:(c + 1) * 128],
                    xnt[:, c * 128:(c + 1) * 128], ident_b[:]),
                    reads=[f"xn{slot}", "ident_b", "xnT"], writes=["PEb"])
            P.emit("vector", lambda v, n=n: v.tensor_copy(
                out=xnT3[:, :, n * 128:(n + 1) * 128],
                in_=PEb[:].rearrange("p (c t) -> p c t", c=8)),
                reads=["PEb"], writes=["xnT"])

        for g in range(NG):
            gslc = slice(g * 512, (g + 1) * 512)
            for c in range(8):
                P.emit("tensor", lambda t, c=c, gslc=gslc: t.matmul(
                    PA[0:64, 0:512], qw_s[:, c * H0:(c + 1) * H0],
                    xnT3[:, c, gslc], start=(c == 0), stop=(c == 7)),
                    reads=["qw_s", "xnT"], writes=["PA"])
            for c in range(8):
                P.emit("tensor", lambda t, c=c, gslc=gslc: t.matmul(
                    PB[0:64, 0:512], kw_s[:, c * H0:(c + 1) * H0],
                    xnT3[:, c, gslc], start=(c == 0), stop=(c == 7)),
                    reads=["kw_s", "xnT"], writes=["PB"])
            for c in range(8):
                P.emit("tensor", lambda t, c=c, gslc=gslc: t.matmul(
                    PC[0:64, 0:512], vw_s[:, c * H0:(c + 1) * H0],
                    xnT3[:, c, gslc], start=(c == 0), stop=(c == 7)),
                    reads=["vw_s", "xnT"], writes=["PC"])
            for (pp, pres, dst, dres) in (
                    (PA, "PA", qT_s, "qT"), (PB, "PB", kT_s, "kT")):
                P.emit("vector", lambda v, pp=pp: v.tensor_scalar_min(
                    out=phi_a[0:64, :], in0=pp[0:64, 0:512], scalar1=0.0),
                    reads=[pres], writes=["phi_a"])
                P.emit("scalar", lambda sc: sc.activation(
                    out=phi_a[0:64, :], in_=phi_a[0:64, :], func=AF.Exp),
                    reads=["phi_a"], writes=["phi_a"])
                P.emit("vector", lambda v, pp=pp: v.tensor_scalar_max(
                    out=phi_b[0:64, :], in0=pp[0:64, 0:512], scalar1=0.0),
                    reads=[pres], writes=["phi_b"])
                P.emit("vector", lambda v, gslc=gslc, dst=dst: v.tensor_tensor(
                    out=dst[:, gslc], in0=phi_a[0:64, :], in1=phi_b[0:64, :],
                    op=ALU.add),
                    reads=["phi_a", "phi_b"], writes=[dres])
            P.emit("vector", lambda v, gslc=gslc: v.tensor_copy(
                out=vT[:, gslc], in_=PC[0:64, 0:512]),
                reads=["PC"], writes=["vT"])

        for i in range(16):
            t0 = i * 128
            cs = slice(t0, t0 + 128)
            sl = i % 2
            P.emit("tensor", lambda t, cs=cs, sl=sl: t.transpose(
                PEb[:, 0:64], kT_s[:, cs], ident_b[0:64, 0:64]),
                reads=["kT", "ident_b", "ktm" + str(sl)], writes=["PEb"])
            P.emit("vector", lambda v, sl=sl: v.tensor_copy(
                out=ktm[:, sl * H0:(sl + 1) * H0], in_=PEb[0:128, 0:64]),
                reads=["PEb"], writes=["ktm" + str(sl)])
            P.emit("tensor", lambda t, cs=cs, sl=sl: t.transpose(
                PEb[:, 64:128], vT[0:64, cs], ident_b[0:64, 0:64]),
                reads=["vT", "ident_b", "vtm" + str(sl)], writes=["PEb"])
            P.emit("vector", lambda v, sl=sl: v.tensor_copy(
                out=vtm[:, sl * H0:(sl + 1) * H0], in_=PEb[0:128, 64:128]),
                reads=["PEb"], writes=["vtm" + str(sl)])
            ktm_s = ktm[:, sl * H0:(sl + 1) * H0]
            vtm_s = vtm[:, sl * H0:(sl + 1) * H0]
            P.emit("tensor", lambda t, cs=cs: t.matmul(
                PA[:, 0:128], kT_s[:, cs], qT_s[:, cs], start=True, stop=True),
                reads=["qT", "kT"], writes=["PA"])
            P.emit("vector", lambda v: v.tensor_tensor(
                out=ktm_f[:], in0=PA[:, 0:128], in1=ultri_f[:], op=ALU.mult),
                reads=["PA", "ultri_f"], writes=["ktm_f"])
            P.emit("tensor", lambda t: t.matmul(
                PC[0:1, 0:128], ones_f[:, 0:1], ktm_f[:], start=True, stop=False),
                reads=["ktm_f", "ones_f"], writes=["PC"])
            P.emit("tensor", lambda t, cs=cs: t.matmul(
                PC[0:1, 0:128], st_b[:, H0:H0 + 1], qT_s[:, cs],
                start=False, stop=True),
                reads=["st_b", "qT"], writes=["PC"])
            P.emit("vector", lambda v: v.tensor_scalar_max(
                out=rd_row[:], in0=PC[0:1, 0:128], scalar1=1e-6),
                reads=["PC"], writes=["rd_row"])
            P.emit("vector", lambda v: v.reciprocal(out=rd_row[:], in_=rd_row[:]),
                reads=["rd_row"], writes=["rd_row"])
            P.emit("tensor", lambda t: t.matmul(
                PX[0:64, 0:128], ones_f[0:1, 0:64], rd_row[:], start=True, stop=True),
                reads=["rd_row", "ones_f"], writes=["PX"])
            P.emit("vector", lambda v, cs=cs: v.tensor_tensor(
                out=qs_b[:], in0=qT_s[:, cs], in1=PX[0:64, 0:128], op=ALU.mult),
                reads=["qT", "PX"], writes=["qs_b"])
            P.emit("tensor", lambda t, cs=cs: t.matmul(
                PA[:, 128:256], vT[0:64, cs], qs_b[:], start=True, stop=True),
                reads=["vT", "qs_b"], writes=["PA"])
            P.emit("vector", lambda v: v.tensor_tensor(
                out=wtm_b[:], in0=PA[:, 128:256], in1=ultri_f[:], op=ALU.mult),
                reads=["PA", "ultri_f"], writes=["wtm_b"])
            P.emit("tensor", lambda t: t.matmul(
                PB[0:64, 0:128], st_b[:, 0:H0], qs_b[:], start=True, stop=False),
                reads=["st_b", "qs_b"], writes=["PB"])
            P.emit("tensor", lambda t, ktm_s=ktm_s, sl=sl: t.matmul(
                PB[0:64, 0:128], ktm_s, wtm_b[:], start=False, stop=True),
                reads=["wtm_b", "ktm" + str(sl)], writes=["PB"])
            P.emit("vector", lambda v, cs=cs: v.tensor_copy(
                out=oT[:, cs], in_=PB[0:64, 0:128]), reads=["PB"], writes=["oT"])
            P.emit("tensor", lambda t, vtm_s=vtm_s, ktm_s=ktm_s, sl=sl: t.matmul(
                PX[0:64, 128:192], vtm_s, ktm_s, start=True, stop=True),
                reads=["ktm" + str(sl), "vtm" + str(sl)], writes=["PX"])
            P.emit("tensor", lambda t, ktm_s=ktm_s, sl=sl: t.matmul(
                PX[0:64, 192:193], ktm_s, onecol_b[:], start=True, stop=True),
                reads=["onecol_b", "ktm" + str(sl)], writes=["PX"])
            P.emit("vector", lambda v, stf=stf: v.tensor_tensor(
                out=stf, in0=stf, in1=PX[0:64, 128:193], op=ALU.add),
                reads=["PX", "st_f"], writes=["st_f"])
            P.emit("vector", lambda v, stf=stf: v.tensor_copy(out=st_b[:], in_=stf),
                reads=["st_f"], writes=["st_b"])

        for g in range(NG):
            tslc = slice(g * 512, (g + 1) * 512)
            for c in range(8):
                hs = c % 2
                hq = hsq[:, hs * 512:(hs + 1) * 512]
                P.emit("scalar", lambda sc, c=c, hq=hq, tslc=tslc: sc.activation(
                    out=hq, in_=xnT3[:, c, tslc], func=AF.Square),
                    reads=["xnT"], writes=[f"hsq{hs}"])
                P.emit("tensor", lambda t, c=c, hq=hq: t.matmul(
                    PX[:, 0:512], ones_b[:], hq, start=(c == 0), stop=(c == 7)),
                    reads=[f"hsq{hs}", "ones_b"], writes=["PX"])
            P.emit("scalar", lambda sc: sc.activation(
                out=t1s[:], in_=PX[:, 0:512], func=AF.Sqrt, scale=1.0 / D),
                reads=["PX"], writes=["t1s"])
            P.emit("vector", lambda v: v.tensor_scalar_add(
                out=t1s[:], in0=t1s[:], scalar1=RMS_EPS), reads=["t1s"], writes=["t1s"])
            P.emit("vector", lambda v: v.reciprocal(out=t1s[:], in_=t1s[:]),
                reads=["t1s"], writes=["t1s"])
            for c in range(8):
                P.emit("vector", lambda v, c=c, tslc=tslc: v.tensor_tensor(
                    out=xnormT[:, c * 512:(c + 1) * 512], in0=xnT3[:, c, tslc],
                    in1=t1s[:], op=ALU.mult),
                    reads=["xnT", "t1s"], writes=["xnormT"])

            def emit_w1load(fb):
                wsl = fb % 4
                P.emit("sync", lambda s, fb=fb, wsl=wsl: s.dma_start(
                    out=w1sb[:, wsl * 1024:(wsl + 1) * 1024].rearrange(
                        "p (c f) -> p c f", c=8),
                    in_=w1t_d[fb].rearrange("c p f -> p c f")),
                    writes=[f"w1sb{wsl}"], dma=True, stream="w1")
            for fb in range(3):
                emit_w1load(fb)
            for fb in range(32):
                wsl = fb % 4
                if fb + 3 < 32:
                    emit_w1load(fb + 3)
                ppy = PA if fb % 2 == 0 else PB
                ppy_res = "PA" if fb % 2 == 0 else "PB"
                for c in range(8):
                    P.emit("tensor", lambda t, fb=fb, c=c, wsl=wsl, ppy=ppy: t.matmul(
                        ppy[:, 0:512],
                        w1sb[:, wsl * 1024 + c * 128:wsl * 1024 + (c + 1) * 128],
                        xnormT[:, c * 512:(c + 1) * 512],
                        start=(c == 0), stop=(c == 7)),
                        reads=[f"w1sb{wsl}", "xnormT"], writes=[ppy_res])
                P.emit("scalar", lambda sc, fb=fb, ppy=ppy: sc.activation(
                    out=hbuf[:, fb * 512:(fb + 1) * 512], in_=ppy[:, 0:512],
                    func=AF.Gelu), reads=[ppy_res], writes=[f"hb{fb}"])
                hs = fb % 2
                hq = hsq[:, hs * 512:(hs + 1) * 512]
                P.emit("scalar", lambda sc, fb=fb, hq=hq: sc.activation(
                    out=hq, in_=hbuf[:, fb * 512:(fb + 1) * 512], func=AF.Square),
                    reads=[f"hb{fb}"], writes=[f"hsq{hs}"])
                P.emit("tensor", lambda t, fb=fb, hq=hq: t.matmul(
                    PY[:, 0:512], ones_b[:], hq, start=(fb == 0), stop=(fb == 31)),
                    reads=[f"hsq{hs}", "ones_b"], writes=["PY"])
            P.emit("scalar", lambda sc: sc.activation(
                out=t2s[:], in_=PY[:, 0:512], func=AF.Sqrt, scale=1.0 / DFF),
                reads=["PY"], writes=["t2s"])
            P.emit("vector", lambda v: v.tensor_scalar_add(
                out=t2s[:], in0=t2s[:], scalar1=RMS_EPS), reads=["t2s"], writes=["t2s"])
            P.emit("vector", lambda v: v.reciprocal(out=t2s[:], in_=t2s[:]),
                reads=["t2s"], writes=["t2s"])

            def emit_w2load(j):
                dc, hh = divmod(j, 2)
                wsl = j % 4
                P.emit("sync", lambda s, dc=dc, hh=hh, wsl=wsl: s.dma_start(
                    out=w2sb[:, wsl * 2048:(wsl + 1) * 2048].rearrange(
                        "p (c f) -> p c f", c=16),
                    in_=w2t_d[dc, hh * 16:(hh + 1) * 16].rearrange(
                        "c p f -> p c f")),
                    writes=[f"w2sb{wsl}"], dma=True, stream="w2")
            for j in range(3):
                emit_w2load(j)
            for dc in range(8):
                ppz = PA if dc % 2 == 0 else PB
                ppz_res = "PA" if dc % 2 == 0 else "PB"
                for fc in range(32):
                    j = dc * 2 + fc // 16
                    if fc % 16 == 0 and j + 3 < 16:
                        emit_w2load(j + 3)
                    wsl = j % 4
                    fcl = fc % 16
                    P.emit("tensor", lambda t, fc=fc, wsl=wsl, fcl=fcl, ppz=ppz: t.matmul(
                        ppz[:, 0:512],
                        w2sb[:, wsl * 2048 + fcl * 128:wsl * 2048 + (fcl + 1) * 128],
                        hbuf[:, fc * 512:(fc + 1) * 512],
                        start=(fc == 0), stop=(fc == 31)),
                        reads=[f"w2sb{wsl}", f"hb{fc}"], writes=[ppz_res])
                P.emit("tensor", lambda t, dc=dc, tslc=tslc: t.matmul(
                    PC[:, 0:512], ow_s[0:64, dc * 128:(dc + 1) * 128], oT[:, tslc],
                    start=True, stop=True), reads=["ow_s", "oT"], writes=["PC"])
                P.emit("vector", lambda v, dc=dc, ppz=ppz: v.tensor_tensor(
                    out=sum_fm[:, dc * 512:(dc + 1) * 512], in0=ppz[:, 0:512],
                    in1=t2s[:], op=ALU.mult),
                    reads=[ppz_res, "t2s"], writes=[f"sfm{dc}"])
                P.emit("vector", lambda v, dc=dc: v.tensor_tensor(
                    out=sum_fm[:, dc * 512:(dc + 1) * 512],
                    in0=sum_fm[:, dc * 512:(dc + 1) * 512],
                    in1=PC[:, 0:512], op=ALU.add),
                    reads=[f"sfm{dc}", "PC"], writes=[f"sfm{dc}"])

            for ts in range(4):
                rows = slice(row0 + g * 512 + ts * 128, row0 + g * 512 + ts * 128 + 128)
                for dc in range(8):
                    P.emit("tensor", lambda t, dc=dc, ts=ts: t.transpose(
                        PD[:, dc * 128:(dc + 1) * 128],
                        sum_fm[:, dc * 512 + ts * 128:dc * 512 + (ts + 1) * 128],
                        ident_f[:]),
                        reads=[f"sfm{dc}", "ident_f", "x1tok"], writes=["PD"])
                P.emit("sync", lambda s, rows=rows: s.dma_start(
                    out=xrow[:, 0:D], in_=x_d[rows, :]), writes=["xrow0"], dma=True,
                    stream="xin")
                P.emit("vector", lambda v: v.tensor_tensor(
                    out=x1tok[:], in0=xrow[:, 0:D], in1=PD[:, 0:1024], op=ALU.add),
                    reads=["xrow0", "PD"], writes=["x1tok"])
                P.emit("gpsimd", lambda g_, rows=rows: g_.dma_start(
                    out=x1_d[rows, :], in_=x1tok[:]), reads=["x1tok"], dma=True, stream="ost")
                layernorm_tile("x1tok", x1tok[:], xn1tok_f[:], "xn1f")
                P.emit("vector", lambda v: v.tensor_copy(
                    out=xn1tok_b[:], in_=xn1tok_f[:]), reads=["xn1f"], writes=["xn1b"])
                P.emit("gpsimd", lambda g_, rows=rows: g_.dma_start(
                    out=xn1_d[rows, :], in_=xn1tok_b[:]), reads=["xn1b"], dma=True, stream="ost")
                for ch in range(2):
                    for c4 in range(4):
                        c = ch * 4 + c4
                        P.emit("tensor", lambda t, c=c, c4=c4: t.transpose(
                            PX[:, c4 * 128:(c4 + 1) * 128],
                            xn1tok_f[:, c * 128:(c + 1) * 128], ident_f[:]),
                            reads=["xn1f", "ident_f", "xn1T"], writes=["PX"])
                    P.emit("vector", lambda v: v.tensor_copy(
                        out=xn1T_f[:], in_=PX[:, 0:512]), reads=["PX"], writes=["xn1T"])
                    for c4 in range(4):
                        c = ch * 4 + c4
                        P.emit("tensor", lambda t, c=c, c4=c4, ch=ch: t.matmul(
                            PY[0:8, 0:128], gw_s[:, c * NE:(c + 1) * NE],
                            xn1T_f[:, c4 * 128:(c4 + 1) * 128],
                            start=(c == 0), stop=(c == 7)),
                            reads=["gw_s", "xn1T"], writes=["PY"])
                P.emit("vector", lambda v: v.tensor_copy(
                    out=lgt_row[:], in_=PY[0:8, 0:128]), reads=["PY"], writes=["lgt_row"])
                P.emit("tensor", lambda t: t.transpose(
                    PX[:, 128:136], lgt_row[:], ident_f[0:8, 0:8]),
                    reads=["lgt_row", "ident_f", "lgt_f"], writes=["PX"])
                P.emit("vector", lambda v: v.tensor_copy(
                    out=lgt_f[:], in_=PX[0:128, 128:136]), reads=["PX"], writes=["lgt_f"])
                P.emit("gpsimd", lambda g_, rows=rows: g_.dma_start(
                    out=lgt_d[rows, :], in_=lgt_f[:]), reads=["lgt_f"], dma=True, stream="ost")

    toks = P.barrier_token()
    P.emit("vector", lambda v: v.engine_nop(), waits=toks)

    P.finalize(blk)
    es.close()
    return nc





def build_launch2(core_id_unused=None):
    nc = bass.Bass("TRN2", target_bir_lowering=False, debug=False)
    x1_d = nc.dram_tensor("x1", [TOK, D], F32, kind="ExternalInput").ap()
    xn1_d = nc.dram_tensor("xn1", [TOK, D], BF16, kind="ExternalInput").ap()
    lgt_d = nc.dram_tensor("lgt", [NTOK_ALL, NE], F32, kind="ExternalInput").ap()
    w1_d = nc.dram_tensor("w1", [D, 3 * D], F32, kind="ExternalInput").ap()
    b1_d = nc.dram_tensor("b1", [3 * D], F32, kind="ExternalInput").ap()
    wp_d = nc.dram_tensor("wp", [D, D], F32, kind="ExternalInput").ap()
    bp_d = nc.dram_tensor("bp", [D], F32, kind="ExternalInput").ap()
    wi_d = nc.dram_tensor("wi", [NE, D, 2 * FEXP], F32, kind="ExternalInput").ap()
    wo_d = nc.dram_tensor("wo", [NE, FEXP, D], F32, kind="ExternalInput").ap()
    out_d = nc.dram_tensor("out", [TOK, D], F32, kind="ExternalOutput").ap()

    NPOWN = TOK // 512
    coef_d = nc.dram_tensor("coefsc", [NE, NPOWN, 512], F32).ap()
    moe_d = nc.dram_tensor("moesc", [8, 128, TOK], F32).ap()

    es = ExitStack()
    P = Prog(nc, es, streams=(
        "wsmall", "wsmall2", "lgt", "xt", "mz", "wi", "wo2", "macc", "coef",
        "coef2", "xin", "ost", "moin"))

    def sb(name, shape, dtype):
        return es.enter_context(nc.sbuf_tensor(name, shape, dtype))

    def psum(name, shape, dtype):
        return es.enter_context(nc.psum_tensor(name, shape, dtype))

    ones_f = sb("ones_f", [128, 128], F32)
    ones_b = sb("ones_b", [128, 128], BF16)
    ident_b = sb("ident_b", [128, 128], BF16)
    ident_f = sb("ident_f", [128, 128], F32)
    ltm_f = sb("ltm_f", [128, 128], F32)
    cmask = sb("cmask", [128, 128], F32)

    P.emit("vector", lambda v: v.memset(ones_f[:], 1.0), writes=["ones_f"])
    P.emit("vector", lambda v: v.memset(ones_b[:], 1.0), writes=["ones_b"])
    P.emit("gpsimd", lambda g: g.affine_select(
        ident_b[:], ones_b[:], [[-1, 128]], ALU.is_equal, 0.0, channel_multiplier=1),
        reads=["ones_b"], writes=["ident_b"])
    P.emit("gpsimd", lambda g: g.affine_select(
        ident_f[:], ones_f[:], [[-1, 128]], ALU.is_equal, 0.0, channel_multiplier=1),
        reads=["ones_f"], writes=["ident_f"])
    P.emit("gpsimd", lambda g: g.affine_select(
        ltm_f[:], ones_f[:], [[1, 128]], ALU.is_gt, 0.0, channel_multiplier=-1),
        reads=["ones_f"], writes=["ltm_f"])
    P.emit("vector", lambda v: v.memset(cmask[:], 0.0), writes=["cmask"])
    P.emit("gpsimd", lambda g: g.affine_select(
        cmask[:], cmask[:], [[-1, 128]], ALU.is_ge, -1e6, channel_multiplier=1),
        reads=["cmask"], writes=["cmask"])

    NA = NTOK_ALL // 128
    esR = ExitStack()

    def sbR(name, shape, dtype):
        return esR.enter_context(nc.sbuf_tensor(name, shape, dtype))

    lgt_sb = sbR("lgt_sb", [128, NA * NE], F32)
    l3 = lgt_sb[:].rearrange("p (a e) -> p a e", e=NE)
    l3e = lgt_sb[:].rearrange("p (a e) -> p a e", e=NE).transpose([0, 2, 1])
    iota_e = sbR("iota_e", [128, NA * NE], F32)
    i3 = iota_e[:].rearrange("p (a e) -> p a e", e=NE)
    m1 = sbR("m1", [128, NA], F32)
    m2 = sbR("m2", [128, NA], F32)
    idx1 = sbR("idx1", [128, NA], F32)
    idx2 = sbR("idx2", [128, NA], F32)
    e2g = sbR("e2g", [128, NA], F32)
    oh1 = sbR("oh1", [128, NA * NE], F32)
    oh2 = sbR("oh2", [128, NA * NE], F32)
    o13 = oh1[:].rearrange("p (a e) -> p a e", e=NE)
    o23 = oh2[:].rearrange("p (a e) -> p a e", e=NE)
    scr = sbR("scr", [128, NA * NE], F32)
    s3 = scr[:].rearrange("p (a e) -> p a e", e=NE)
    s3e = scr[:].rearrange("p (a e) -> p a e", e=NE).transpose([0, 2, 1])
    scan1 = sbR("scan1", [128, NA * NE], F32)
    c13 = scan1[:].rearrange("p (a e) -> p a e", e=NE)
    c13e = scan1[:].rearrange("p (a e) -> p a e", e=NE).transpose([0, 2, 1])
    scan2 = sbR("scan2", [128, NA * NE], F32)
    c23 = scan2[:].rearrange("p (a e) -> p a e", e=NE)
    c23e = scan2[:].rearrange("p (a e) -> p a e", e=NE).transpose([0, 2, 1])
    cnt1 = sbR("cnt1", [128, NE], F32)
    cnt2 = sbR("cnt2", [128, NE], F32)
    base1 = sbR("base1", [128, NE], F32)
    base2 = sbR("base2", [128, NE], F32)
    tot1r = sbR("tot1r", [1, NE], F32)
    loc1 = sbR("loc1", [128, NA], F32)
    loc2 = sbR("loc2", [128, NA], F32)
    keep1 = sbR("keep1", [128, NA], F32)
    keep2 = sbR("keep2", [128, NA], F32)
    gden = sbR("gden", [128, NA], F32)
    coef = sbR("coef", [128, NA * NE], F32)
    cf3 = coef[:].rearrange("p (a e) -> p a e", e=NE)
    cf3e = coef[:].rearrange("p (a e) -> p a e", e=NE).transpose([0, 2, 1])

    esP1 = ExitStack()
    PR1 = esP1.enter_context(nc.psum_tensor("PR1", [128, 512], F32))
    PR2 = esP1.enter_context(nc.psum_tensor("PR2", [128, 512], F32))

    lgt_pm = lgt_d.rearrange("(p a) e -> p a e", p=128)
    P.emit("sync", lambda s: s.dma_start(out=l3, in_=lgt_pm),
           writes=["lgt_sb"], dma=True, stream="lgt")
    P.emit("gpsimd", lambda g: g.iota(
        iota_e[:], [[0, NA], [1, NE]], channel_multiplier=0,
        allow_small_or_imprecise_dtypes=True), writes=["iota_e"])

    BIG = 1.0e9
    P.emit("vector", lambda v: v.reduce_max(out=m1[:], in_=l3, axis=AX.X),
           reads=["lgt_sb"], writes=["m1"])
    P.emit("vector", lambda v: v.tensor_tensor(
        out=scr[:], in0=l3, in1=m1[:].rearrange("p a -> p a 1" if False else "p (a o) -> p a o", o=1).broadcast_to([128, NA, NE]),
        op=ALU.is_ge), reads=["lgt_sb", "m1"], writes=["scr"])
    P.emit("vector", lambda v: v.tensor_scalar(
        out=scr[:], in0=scr[:], scalar1=-BIG, scalar2=BIG,
        op0=ALU.mult, op1=ALU.add), reads=["scr"], writes=["scr"])
    P.emit("vector", lambda v: v.tensor_tensor(
        out=scr[:], in0=scr[:], in1=iota_e[:], op=ALU.add),
        reads=["scr", "iota_e"], writes=["scr"])
    P.emit("vector", lambda v: v.tensor_reduce(
        out=idx1[:], in_=s3, axis=AX.X, op=ALU.min),
        reads=["scr"], writes=["idx1"])
    P.emit("vector", lambda v: v.tensor_tensor(
        out=oh1[:], in0=iota_e[:],
        in1=idx1[:].rearrange("p (a o) -> p a o", o=1).broadcast_to([128, NA, NE]),
        op=ALU.is_equal), reads=["iota_e", "idx1"], writes=["oh1"])
    P.emit("vector", lambda v: v.tensor_scalar(
        out=scr[:], in0=oh1[:], scalar1=-BIG, scalar2=None, op0=ALU.mult),
        reads=["oh1"], writes=["scr"])
    P.emit("vector", lambda v: v.tensor_tensor(
        out=scr[:], in0=scr[:], in1=lgt_sb[:], op=ALU.add),
        reads=["scr", "lgt_sb"], writes=["scr"])
    P.emit("vector", lambda v: v.reduce_max(out=m2[:], in_=s3, axis=AX.X),
        reads=["scr"], writes=["m2"])
    P.emit("vector", lambda v: v.tensor_tensor(
        out=scr[:], in0=s3,
        in1=m2[:].rearrange("p (a o) -> p a o", o=1).broadcast_to([128, NA, NE]),
        op=ALU.is_ge), reads=["scr", "m2"], writes=["scr"])
    P.emit("vector", lambda v: v.tensor_scalar(
        out=scr[:], in0=scr[:], scalar1=-BIG, scalar2=BIG,
        op0=ALU.mult, op1=ALU.add), reads=["scr"], writes=["scr"])
    P.emit("vector", lambda v: v.tensor_tensor(
        out=scr[:], in0=scr[:], in1=iota_e[:], op=ALU.add),
        reads=["scr", "iota_e"], writes=["scr"])
    P.emit("vector", lambda v: v.tensor_reduce(
        out=idx2[:], in_=s3, axis=AX.X, op=ALU.min),
        reads=["scr"], writes=["idx2"])
    P.emit("vector", lambda v: v.tensor_tensor(
        out=oh2[:], in0=iota_e[:],
        in1=idx2[:].rearrange("p (a o) -> p a o", o=1).broadcast_to([128, NA, NE]),
        op=ALU.is_equal), reads=["iota_e", "idx2"], writes=["oh2"])
    P.emit("vector", lambda v: v.tensor_tensor(
        out=e2g[:], in0=m2[:], in1=m1[:], op=ALU.subtract),
        reads=["m2", "m1"], writes=["e2g"])
    P.emit("scalar", lambda sc: sc.activation(
        out=e2g[:], in_=e2g[:], func=AF.Exp), reads=["e2g"], writes=["e2g"])
    for e in range(NE):
        P.emit("vector", lambda v, e=e: v.tensor_tensor_scan(
            out=c13[:, :, e], data0=o13[:, :, e], data1=o13[:, :, e],
            initial=0.0, op0=ALU.mult, op1=ALU.add),
            reads=["oh1"], writes=["scan1"])
        P.emit("vector", lambda v, e=e: v.tensor_tensor_scan(
            out=c23[:, :, e], data0=o23[:, :, e], data1=o23[:, :, e],
            initial=0.0, op0=ALU.mult, op1=ALU.add),
            reads=["oh2"], writes=["scan2"])
    P.emit("vector", lambda v: v.reduce_sum(out=cnt1[:], in_=o13.transpose([0, 2, 1]),
        axis=AX.X), reads=["oh1"], writes=["cnt1"])
    P.emit("vector", lambda v: v.reduce_sum(out=cnt2[:], in_=o23.transpose([0, 2, 1]),
        axis=AX.X), reads=["oh2"], writes=["cnt2"])
    P.emit("tensor", lambda t: t.matmul(
        PR1[0:128, 0:NE], ltm_f[:], cnt1[:], start=True, stop=True),
        reads=["ltm_f", "cnt1"], writes=["PR1"])
    P.emit("vector", lambda v: v.tensor_copy(out=base1[:], in_=PR1[0:128, 0:NE]),
        reads=["PR1"], writes=["base1"])
    P.emit("tensor", lambda t: t.matmul(
        PR1[0:1, 32:32 + NE], ones_f[:, 0:1], cnt1[:], start=True, stop=True),
        reads=["ones_f", "cnt1"], writes=["PR1"])
    P.emit("vector", lambda v: v.tensor_copy(out=tot1r[:], in_=PR1[0:1, 32:32 + NE]),
        reads=["PR1"], writes=["tot1r"])
    P.emit("tensor", lambda t: t.matmul(
        PR2[0:128, 0:NE], ltm_f[:], cnt2[:], start=True, stop=False),
        reads=["ltm_f", "cnt2"], writes=["PR2"])
    P.emit("tensor", lambda t: t.matmul(
        PR2[0:128, 0:NE], ones_f[0:1, :], tot1r[:], start=False, stop=True),
        reads=["ones_f", "tot1r"], writes=["PR2"])
    P.emit("vector", lambda v: v.tensor_copy(out=base2[:], in_=PR2[0:128, 0:NE]),
        reads=["PR2"], writes=["base2"])
    for (o3, c3, basex, locx, kres) in (
            (o13, c13, base1, loc1, ("oh1", "scan1", "base1", "loc1")),
            (o23, c23, base2, loc2, ("oh2", "scan2", "base2", "loc2"))):
        ohr, scr_r, baser, locr = kres
        P.emit("vector", lambda v, c3=c3, basex=basex, o3=o3: v.tensor_tensor(
            out=s3, in0=c3,
            in1=basex[:].rearrange("p (o e) -> p o e", o=1).broadcast_to([128, NA, NE]),
            op=ALU.add), reads=[scr_r, baser], writes=["scr"])
        P.emit("vector", lambda v, o3=o3: v.tensor_tensor(
            out=scr[:], in0=scr[:], in1=(oh1[:] if o3 is o13 else oh2[:]),
            op=ALU.mult), reads=["scr", ohr], writes=["scr"])
        P.emit("vector", lambda v, locx=locx: v.reduce_sum(
            out=locx[:], in_=s3, axis=AX.X), reads=["scr"], writes=[locr])
    P.emit("vector", lambda v: v.tensor_scalar(
        out=keep1[:], in0=loc1[:], scalar1=float(CAP), scalar2=None, op0=ALU.is_le),
        reads=["loc1"], writes=["keep1"])
    P.emit("vector", lambda v: v.tensor_scalar(
        out=keep2[:], in0=loc2[:], scalar1=float(CAP), scalar2=None, op0=ALU.is_le),
        reads=["loc2"], writes=["keep2"])
    P.emit("vector", lambda v: v.tensor_tensor(
        out=e2g[:], in0=e2g[:], in1=keep2[:], op=ALU.mult),
        reads=["e2g", "keep2"], writes=["e2g"])
    P.emit("vector", lambda v: v.tensor_tensor(
        out=gden[:], in0=keep1[:], in1=e2g[:], op=ALU.add),
        reads=["keep1", "e2g"], writes=["gden"])
    P.emit("vector", lambda v: v.tensor_scalar_max(
        out=gden[:], in0=gden[:], scalar1=1e-9), reads=["gden"], writes=["gden"])
    P.emit("vector", lambda v: v.reciprocal(out=gden[:], in_=gden[:]),
        reads=["gden"], writes=["gden"])
    P.emit("vector", lambda v: v.tensor_tensor(
        out=keep1[:], in0=keep1[:], in1=gden[:], op=ALU.mult),
        reads=["keep1", "gden"], writes=["keep1"])
    P.emit("vector", lambda v: v.tensor_tensor(
        out=e2g[:], in0=e2g[:], in1=gden[:], op=ALU.mult),
        reads=["e2g", "gden"], writes=["e2g"])
    P.emit("vector", lambda v: v.tensor_tensor(
        out=coef[:], in0=oh1[:],
        in1=keep1[:].rearrange("p (a o) -> p a o", o=1).broadcast_to([128, NA, NE]),
        op=ALU.mult), reads=["oh1", "keep1"], writes=["coef"])
    P.emit("vector", lambda v: v.tensor_tensor(
        out=scr[:], in0=oh2[:],
        in1=e2g[:].rearrange("p (a o) -> p a o", o=1).broadcast_to([128, NA, NE]),
        op=ALU.mult), reads=["oh2", "e2g"], writes=["scr"])
    P.emit("vector", lambda v: v.tensor_tensor(
        out=coef[:], in0=coef[:], in1=scr[:], op=ALU.add),
        reads=["coef", "scr"], writes=["coef"])
    sel_d = nc.dram_tensor("sel", [128, NPOWN], F32, kind="ExternalInput").ap()
    sel_sb = sbR("sel_sb", [128, NPOWN], F32)
    owncoef = sbR("owncoef", [NPOWN, NA * NE], F32)
    P.emit("sync", lambda s: s.dma_start(out=sel_sb[:], in_=sel_d[:, :]),
           writes=["sel_sb"], dma=True, stream="wsmall2")
    for j in range(NA * NE // 512):
        P.emit("tensor", lambda t, j=j: t.matmul(
            PR1[0:NPOWN, 0:512], sel_sb[:], coef[:, j * 512:(j + 1) * 512],
            start=True, stop=True), reads=["sel_sb", "coef"], writes=["PR1"])
        P.emit("vector", lambda v, j=j: v.tensor_copy(
            out=owncoef[:, j * 512:(j + 1) * 512], in_=PR1[0:NPOWN, 0:512]),
            reads=["PR1"], writes=["owncoef"])
    oc3 = owncoef[:].rearrange("p (a e) -> p a e", e=NE)
    ocrow = sbR("ocrow", [NPOWN, 512], F32)
    for e in range(NE):
        P.emit("vector", lambda v, e=e: v.tensor_copy(
            out=ocrow[:], in_=oc3[:, :, e]), reads=["owncoef"], writes=["ocrow"])
        P.emit("gpsimd", lambda g, e=e: g.dma_start(
            out=coef_d[e, :, :], in_=ocrow[:]),
            reads=["ocrow"], dma=True, stream="coef")

    P.base_waits = P.barrier_token()
    esP1.close()
    esR.close()

    NTB = TOK // 512
    esM = ExitStack()

    def sbM(name, shape, dtype):
        return esM.enter_context(nc.sbuf_tensor(name, shape, dtype))

    zero_sb = sbM("zero_sb", [128, 2048], F32)
    P.emit("vector", lambda v: v.memset(zero_sb[:], 0.0), writes=["zero_sb"])
    for db in range(8):
        for j in range(TOK // 2048):
            P.emit("gpsimd", lambda g, db=db, j=j: g.dma_start(
                out=moe_d[db, :, j * 2048:(j + 1) * 2048], in_=zero_sb[:]),
                reads=["zero_sb"], writes=["moe_dram"], dma=True, stream="mz")

    wi_all = sbM("wi_all", [128, 8 * 4096], BF16)
    actbuf = sbM("actbuf", [128, 16 * 512], BF16)
    ga_sb = sbM("ga_sb", [128, 512], BF16)
    xnT1m = sbM("xnT1m", [128, 8 * 512], BF16)
    xm3 = xnT1m[:].rearrange("p (c t) -> p c t", c=8)
    wo_all = sbM("wo_all", [128, 16 * 1024], BF16)
    coefrow = sbM("coefrow", [1, TOK], F32)
    coefbc = sbM("coefbc", [128, 512], F32)
    mev = sbM("mev", [128, 2 * 512], F32)

    esP2 = ExitStack()
    PM1 = esP2.enter_context(nc.psum_tensor("PM1", [128, 512], F32))
    PM2 = esP2.enter_context(nc.psum_tensor("PM2", [128, 512], F32))
    PDB = esP2.enter_context(nc.psum_tensor("PDB", [128, 4 * 512], F32))


    for e in range(NE):
        P.emit("sync", lambda s, e=e: s.dma_start(
            out=coefrow[0:1, :].rearrange("o (p a) -> o p a", a=512),
            in_=coef_d[e].rearrange("(o p) a -> o p a", o=1)),
            writes=["coefrow"], dma=True, stream="coef2")
        for hc in range(8):
            P.emit("gpsimd", lambda g, e=e, hc=hc: g.dma_start(
                out=wi_all[:, hc * 4096:(hc + 1) * 4096],
                in_=wi_d[e, hc * 128:(hc + 1) * 128, :]),
                writes=["wi_all"], dma=True, stream="wi")
        for fc in range(16):
            P.emit("gpsimd", lambda g, e=e, fc=fc: g.dma_start(
                out=wo_all[:, fc * 1024:(fc + 1) * 1024],
                in_=wo_d[e, fc * 128:(fc + 1) * 128, :]),
                writes=["wo_all"], dma=True, stream="wo2")
        for tb in range(NTB):
            for c in range(8):
                P.emit("sync", lambda s, tb=tb, c=c: s.dma_start(
                    out=xm3[:, c, :],
                    in_=xn1_d[tb * 512:(tb + 1) * 512, c * 128:(c + 1) * 128],
                    transpose=True), writes=["xnT1m"], dma=True, stream="xt")
            P.emit("tensor", lambda t, tb=tb: t.matmul(
                PM1[:, 0:512], ones_f[0:1, :], coefrow[0:1, tb * 512:(tb + 1) * 512],
                start=True, stop=True), reads=["ones_f", "coefrow"], writes=["PM1"])
            P.emit("vector", lambda v: v.tensor_copy(out=coefbc[:], in_=PM1[:, 0:512]),
                reads=["PM1"], writes=["coefbc"])
            for fc in range(16):
                for hc in range(8):
                    P.emit("tensor", lambda t, fc=fc, hc=hc: t.matmul(
                        PM1[:, 0:512],
                        wi_all[:, hc * 4096 + fc * 128:hc * 4096 + (fc + 1) * 128],
                        xm3[:, hc, :], start=(hc == 0), stop=(hc == 7)),
                        reads=["wi_all", "xnT1m"], writes=["PM1"])
                for hc in range(8):
                    P.emit("tensor", lambda t, fc=fc, hc=hc: t.matmul(
                        PM2[:, 0:512],
                        wi_all[:, hc * 4096 + (fc + 16) * 128:hc * 4096 + (fc + 17) * 128],
                        xm3[:, hc, :], start=(hc == 0), stop=(hc == 7)),
                        reads=["wi_all", "xnT1m"], writes=["PM2"])
                P.emit("scalar", lambda sc: sc.activation(
                    out=ga_sb[:], in_=PM1[:, 0:512], func=AF.Gelu),
                    reads=["PM1"], writes=["ga_sb"])
                P.emit("vector", lambda v, fc=fc: v.tensor_tensor(
                    out=actbuf[:, fc * 512:(fc + 1) * 512], in0=ga_sb[:],
                    in1=PM2[:, 0:512], op=ALU.mult),
                    reads=["ga_sb", "PM2"], writes=["actbuf"])
            for half in range(2):
                for fc in range(16):
                    for dbl in range(4):
                        db = half * 4 + dbl
                        P.emit("tensor", lambda t, fc=fc, db=db, dbl=dbl: t.matmul(
                            PDB[:, dbl * 512:(dbl + 1) * 512],
                            wo_all[:, fc * 1024 + db * 128:fc * 1024 + (db + 1) * 128],
                            actbuf[:, fc * 512:(fc + 1) * 512],
                            start=(fc == 0), stop=(fc == 15)),
                            reads=["wo_all", "actbuf"], writes=[f"PDB{dbl}"])
                for dbl in range(4):
                    db = half * 4 + dbl
                    sl = dbl % 2
                    P.emit("vector", lambda v, dbl=dbl, sl=sl: v.tensor_tensor(
                        out=mev[:, sl * 512:(sl + 1) * 512],
                        in0=PDB[:, dbl * 512:(dbl + 1) * 512], in1=coefbc[:],
                        op=ALU.mult),
                        reads=[f"PDB{dbl}", "coefbc"], writes=[f"mev{sl}"])
                    P.emit("gpsimd", lambda g, db=db, tb=tb, sl=sl: g.dma_start(
                        out=moe_d[db, :, tb * 512:(tb + 1) * 512],
                        in_=mev[:, sl * 512:(sl + 1) * 512], accum_op=ALU.add),
                        reads=[f"mev{sl}"], writes=["moe_dram"], dma=True,
                        stream="macc")

    P.base_waits = P.barrier_token()
    esP2.close()
    esM.close()

    w1_sb = sb("w1_sb", [128, 8 * 3072], BF16)
    wp_sb = sb("wp_sb", [128, 8 * 1024], BF16)
    bcol = sb("bcol", [128, 24], F32)
    bpcol = sb("bpcol", [128, 8], F32)
    brow = sb("brow", [1, 3072], F32)
    xnT1a = sb("xnT1a", [128, 8 * T], BF16)
    xa3 = xnT1a[:].rearrange("p (c t) -> p c t", c=8)
    qT2_sb = sb("qT2_sb", [H0, T], BF16)
    kT2_sb = sb("kT2_sb", [H0, T], BF16)
    vtm_sb = sb("vtm_sb", [128, 16 * 64], BF16)
    pbuf = sb("pbuf", [128, T], BF16)
    pT_sb = sb("pT_sb", [128, 16 * 512], BF16)
    avT_sb = sb("avT_sb", [128, 8 * T], BF16)
    lsum = sb("lsum", [128, 4], F32)
    negm = sb("negm", [128, 1], F32)
    rlrow = sb("rlrow", [1, 512], F32)
    rlbc = sb("rlbc", [H0, 512], F32)
    sum_fm2 = sb("sum_fm2", [128, 8 * 512], F32)
    moe_sb = sb("moe_sb", [128, 2 * 512], F32)
    xrow2 = sb("xrow2", [128, D], F32)
    outtok = sb("outtok", [128, D], F32)

    for c in range(8):
        P.emit("gpsimd", lambda g, c=c: g.dma_start(
            out=w1_sb[:, c * 3072:(c + 1) * 3072],
            in_=w1_d[c * 128:(c + 1) * 128, :]), writes=["w1_sb"], dma=True,
            stream="wi")
        P.emit("gpsimd", lambda g, c=c: g.dma_start(
            out=wp_sb[:, c * 1024:(c + 1) * 1024],
            in_=wp_d[c * 128:(c + 1) * 128, :]), writes=["wp_sb"], dma=True,
            stream="wi")
    P.emit("sync", lambda s: s.dma_start(
        out=brow[0:1, :], in_=b1_d[:].rearrange("(o f) -> o f", o=1)),
        writes=["brow"], dma=True, stream="coef2")
    for cb in range(24):
        P.emit("tensor", lambda t, cb=cb: t.transpose(
            PQK[0:128, 0:1], brow[0:1, cb * 128:(cb + 1) * 128], ident_f[0:1, 0:1]),
            reads=["brow", "ident_f"], writes=["PQK"])
        P.emit("vector", lambda v, cb=cb: v.tensor_copy(
            out=bcol[:, cb:cb + 1], in_=PQK[0:128, 0:1]),
            reads=["PQK"], writes=["bcol"])
    P.emit("sync", lambda s: s.dma_start(
        out=brow[0:1, 0:1024], in_=bp_d[:].rearrange("(o f) -> o f", o=1)),
        writes=["brow"], dma=True, stream="coef2")
    for cb in range(8):
        P.emit("tensor", lambda t, cb=cb: t.transpose(
            PQK[0:128, 0:1], brow[0:1, cb * 128:(cb + 1) * 128], ident_f[0:1, 0:1]),
            reads=["brow", "ident_f"], writes=["PQK"])
        P.emit("vector", lambda v, cb=cb: v.tensor_copy(
            out=bpcol[:, cb:cb + 1], in_=PQK[0:128, 0:1]),
            reads=["PQK"], writes=["bpcol"])

    PS = psum("PS", [128, 2048], F32)
    PEb2 = psum("PEb2", [128, 1024], BF16)
    PAV = psum("PAV", [128, 512], F32)
    PQK = psum("PQK", [128, 512], F32)

    for b in range(B_LOC):
        row0 = b * T
        for c in range(8):
            P.emit("sync", lambda s, b=b, c=c: s.dma_start(
                out=xa3[:, c, :],
                in_=xn1_d[row0:row0 + T, c * 128:(c + 1) * 128],
                transpose=True), writes=["xnT1a"], dma=True, stream="xt")
        for hh in range(NH):
            qcol = hh * 192
            kcol = hh * 192 + 64
            vcol = hh * 192 + 128
            for (col, dstbuf, nm) in ((qcol, qT2_sb, "qk_q"), (kcol, kT2_sb, "qk_k")):
                cb = col // 128
                off = col % 128
                for tg in range(4):
                    for c in range(8):
                        P.emit("tensor", lambda t, col=col, tg=tg, c=c: t.matmul(
                            PQK[0:64, 0:512],
                            w1_sb[:, c * 3072 + col:c * 3072 + col + 64],
                            xa3[:, c, tg * 512:(tg + 1) * 512],
                            start=(c == 0), stop=(c == 7)),
                            reads=["w1_sb", "xnT1a"], writes=["PQK"])
                    P.emit("scalar", lambda sc, dstbuf=dstbuf, tg=tg, cb=cb, off=off: sc.activation(
                        out=dstbuf[:, tg * 512:(tg + 1) * 512],
                        in_=PQK[0:64, 0:512], func=AF.Identity,
                        bias=bcol[off:off + 64, cb:cb + 1]),
                        reads=["PQK", "bcol"], writes=[nm])
            for tg in range(4):
                for c in range(8):
                    P.emit("tensor", lambda t, vcol=vcol, tg=tg, c=c: t.matmul(
                        PQK[0:64, 0:512],
                        w1_sb[:, c * 3072 + vcol:c * 3072 + vcol + 64],
                        xa3[:, c, tg * 512:(tg + 1) * 512],
                        start=(c == 0), stop=(c == 7)),
                        reads=["w1_sb", "xnT1a"], writes=["PQK"])
                vcb = (vcol) // 128
                voff = vcol % 128
                P.emit("scalar", lambda sc, tg=tg, vcb=vcb, voff=voff: sc.activation(
                    out=pbuf[0:64, tg * 512:(tg + 1) * 512], in_=PQK[0:64, 0:512],
                    func=AF.Identity, bias=bcol[voff:voff + 64, vcb:vcb + 1]),
                    reads=["PQK", "bcol"], writes=["pbuf"])
            for tt in range(16):
                P.emit("tensor", lambda t, tt=tt: t.transpose(
                    PEb2[:, 0:64], pbuf[0:64, tt * 128:(tt + 1) * 128],
                    ident_b[0:64, 0:64]),
                    reads=["pbuf", "ident_b", "vtm"], writes=["PEb2"])
                P.emit("vector", lambda v, tt=tt: v.tensor_copy(
                    out=vtm_sb[:, tt * 64:(tt + 1) * 64], in_=PEb2[0:128, 0:64]),
                    reads=["PEb2"], writes=["vtm"])
            for qi in range(16):
                kmax = (qi + 1) * 128
                nkb = ceil_div(kmax, 512)
                for kb in range(nkb):
                    klen = min(512, kmax - kb * 512)
                    P.emit("tensor", lambda t, qi=qi, kb=kb, klen=klen: t.matmul(
                        PS[:, kb * 512:kb * 512 + klen],
                        qT2_sb[:, qi * 128:(qi + 1) * 128],
                        kT2_sb[:, kb * 512:kb * 512 + klen],
                        start=True, stop=True),
                        reads=["qk_q", "qk_k"], writes=["PS"])
                P.emit("vector", lambda v, qi=qi: v.tensor_tensor(
                    out=PS[:, qi * 128:(qi + 1) * 128],
                    in0=PS[:, qi * 128:(qi + 1) * 128], in1=cmask[:], op=ALU.add),
                    reads=["PS", "cmask"], writes=["PS"])
                P.emit("vector", lambda v, qi=qi, kmax=kmax: v.reduce_max(
                    out=negm[:], in_=PS[:, 0:kmax], axis=AX.X),
                    reads=["PS"], writes=["negm"])
                P.emit("vector", lambda v: v.tensor_scalar_mul(
                    out=negm[:], in0=negm[:], scalar1=-0.125),
                    reads=["negm"], writes=["negm"])
                sl4 = qi % 4
                P.emit("vector", lambda v, sl4=sl4: v.memset(
                    lsum[:, sl4:sl4 + 1], 0.0), writes=["lsum"])
                P.emit("scalar", lambda sc, kmax=kmax, sl4=sl4: sc.activation(
                    out=pbuf[:, 0:kmax], in_=PS[:, 0:kmax], func=AF.Exp,
                    bias=negm[:, 0:1], scale=0.125, accum_out=lsum[:, sl4:sl4 + 1]),
                    reads=["PS", "negm", "lsum"], writes=["pbuf", "lsum"])
                for kt in range(qi + 1, (qi // 4) * 4 + 4):
                    P.emit("vector", lambda v, kt=kt, qi=qi: v.memset(
                        pT_sb[:, kt * 512 + (qi % 4) * 128:kt * 512 + (qi % 4) * 128 + 128],
                        0.0), writes=["pT"])
                for kt in range(qi + 1):
                    P.emit("tensor", lambda t, kt=kt: t.transpose(
                        PEb2[:, 0:128], pbuf[:, kt * 128:(kt + 1) * 128], ident_b[:]),
                        reads=["pbuf", "ident_b", "pT"], writes=["PEb2"])
                    P.emit("vector", lambda v, kt=kt, qi=qi: v.tensor_copy(
                        out=pT_sb[:, kt * 512 + (qi % 4) * 128:kt * 512 + (qi % 4) * 128 + 128],
                        in_=PEb2[:, 0:128]), reads=["PEb2"], writes=["pT"])
                if qi % 4 == 3:
                    qg = qi // 4
                    nkt = (qg + 1) * 4
                    for kt in range(nkt):
                        P.emit("tensor", lambda t, kt=kt, nkt=nkt: t.matmul(
                            PAV[0:64, 0:512], vtm_sb[:, kt * 64:(kt + 1) * 64],
                            pT_sb[:, kt * 512:(kt + 1) * 512],
                            start=(kt == 0), stop=(kt == nkt - 1)),
                            reads=["vtm", "pT"], writes=["PAV"])
                    P.emit("vector", lambda v: v.reciprocal(
                        out=lsum[:, 0:4], in_=lsum[:, 0:4]),
                        reads=["lsum"], writes=["lsum"])
                    for j in range(4):
                        P.emit("tensor", lambda t, j=j: t.transpose(
                            PQK[0:1, 0:128], lsum[:, j:j + 1], ident_f[:]),
                            reads=["lsum", "ident_f"], writes=["PQK"])
                        P.emit("vector", lambda v, j=j: v.tensor_copy(
                            out=rlrow[0:1, j * 128:(j + 1) * 128], in_=PQK[0:1, 0:128]),
                            reads=["PQK"], writes=["rlrow"])
                    P.emit("tensor", lambda t: t.matmul(
                        PQK[0:64, 0:512], ones_f[0:1, 0:64], rlrow[0:1, :],
                        start=True, stop=True),
                        reads=["ones_f", "rlrow"], writes=["PQK"])
                    P.emit("vector", lambda v: v.tensor_copy(
                        out=rlbc[:], in_=PQK[0:64, 0:512]),
                        reads=["PQK"], writes=["rlbc"])
                    hslot = hh % 2
                    P.emit("vector", lambda v, hh=hh, qg=qg, hslot=hslot: v.tensor_tensor(
                        out=avT_sb[hslot * 64:hslot * 64 + 64,
                                   (hh // 2) * T + qg * 512:(hh // 2) * T + (qg + 1) * 512],
                        in0=PAV[0:64, 0:512], in1=rlbc[:], op=ALU.mult),
                        reads=["PAV", "rlbc"], writes=["avT"])
        for qg in range(4):
            tslc = slice(qg * 512, (qg + 1) * 512)
            def emit_moeload(db):
                msl = db % 2
                P.emit("sync", lambda s, db=db, b=b, qg=qg, msl=msl: s.dma_start(
                    out=moe_sb[:, msl * 512:(msl + 1) * 512],
                    in_=moe_d[db, :, b * T + qg * 512:b * T + (qg + 1) * 512]),
                    writes=[f"moe_sb{msl}"], dma=True, stream="moin")
            emit_moeload(0)
            for db in range(8):
                if db + 1 < 8:
                    emit_moeload(db + 1)
                for hh in range(NH):
                    P.emit("tensor", lambda t, db=db, hh=hh, qg=qg: t.matmul(
                        PQK[:, 0:512],
                        wp_sb[(hh % 2) * 64:(hh % 2) * 64 + 64,
                              (hh // 4) * 2048 + 0:0 + 0] if False else
                        wp_sb[64 * (hh % 2):64 * (hh % 2) + 64,
                              (hh // 2) * 1024 + db * 128:(hh // 2) * 1024 + (db + 1) * 128],
                        avT_sb[(hh % 2) * 64:(hh % 2) * 64 + 64,
                               (hh // 2) * T + qg * 512:(hh // 2) * T + (qg + 1) * 512],
                        start=(hh == 0), stop=(hh == NH - 1)),
                        reads=["wp_sb", "avT"], writes=["PQK"])
                msl = db % 2
                P.emit("vector", lambda v, db=db: v.tensor_scalar(
                    out=sum_fm2[:, db * 512:(db + 1) * 512], in0=PQK[:, 0:512],
                    scalar1=bpcol[:, db:db + 1], scalar2=None, op0=ALU.add),
                    reads=["PQK", "bpcol"], writes=[f"sf{db}"])
                P.emit("vector", lambda v, db=db, msl=msl: v.tensor_tensor(
                    out=sum_fm2[:, db * 512:(db + 1) * 512],
                    in0=sum_fm2[:, db * 512:(db + 1) * 512],
                    in1=moe_sb[:, msl * 512:(msl + 1) * 512], op=ALU.add),
                    reads=[f"sf{db}", f"moe_sb{msl}"], writes=[f"sf{db}"])
            for ts in range(4):
                rows = slice(row0 + qg * 512 + ts * 128, row0 + qg * 512 + ts * 128 + 128)
                for dc in range(8):
                    P.emit("tensor", lambda t, dc=dc, ts=ts: t.transpose(
                        PS[:, dc * 128:(dc + 1) * 128],
                        sum_fm2[:, dc * 512 + ts * 128:dc * 512 + (ts + 1) * 128],
                        ident_f[:]),
                        reads=[f"sf{dc}", "ident_f", "outtok"], writes=["PS"])
                P.emit("sync", lambda s, rows=rows: s.dma_start(
                    out=xrow2[:], in_=x1_d[rows, :]), writes=["xrow2"], dma=True,
                    stream="xin")
                P.emit("vector", lambda v: v.tensor_tensor(
                    out=outtok[:], in0=xrow2[:], in1=PS[:, 0:1024], op=ALU.add),
                    reads=["xrow2", "PS"], writes=["outtok"])
                P.emit("gpsimd", lambda g, rows=rows: g.dma_start(
                    out=out_d[rows, :], in_=outtok[:]), reads=["outtok"], dma=True,
                    stream="ost")

    toks = P.barrier_token()
    P.emit("vector", lambda v: v.engine_nop(), waits=toks)
    blk = es.enter_context(nc.Block())
    P.finalize(blk)
    es.close()
    return nc

_NC1 = None
_NC2 = None


def kernel(L0_attn_x, L0_attn_W_q, L0_attn_W_k, L0_attn_W_v, L0_attn_W_o,
           L0_ffn_w_up, L0_ffn_w_down, L1_attn_W1, L1_attn_b1, L1_attn_Wp,
           L1_attn_bp, L1_ffn_Wg, L1_ffn_W_in, L1_ffn_W_out):
    global _NC1, _NC2
    x = np.ascontiguousarray(L0_attn_x.reshape(NTOK_ALL, D))
    if _NC1 is None:
        _NC1 = build_launch1()
    in1 = []
    for c in range(N_CORES):
        in1.append({
            "x": x[c * TOK:(c + 1) * TOK],
            "wq": L0_attn_W_q, "wk": L0_attn_W_k, "wv": L0_attn_W_v,
            "wo": L0_attn_W_o, "wup": L0_ffn_w_up, "wdn": L0_ffn_w_down,
            "wg": L1_ffn_Wg,
        })
    r1 = run_bass_kernel_spmd(_NC1, in1, core_ids=list(range(N_CORES)))
    lgt_full = np.concatenate([r["lgt"] for r in r1.results], 0)
    if _NC2 is None:
        _NC2 = build_launch2()
    in2 = []
    for c in range(N_CORES):
        npown = TOK // 512
        sel = np.zeros((128, npown), np.float32)
        for j in range(npown):
            sel[c * npown + j, j] = 1.0
        in2.append({
            "x1": r1.results[c]["x1"], "xn1": r1.results[c]["xn1"],
            "lgt": lgt_full, "sel": sel,
            "w1": L1_attn_W1, "b1": L1_attn_b1, "wp": L1_attn_Wp,
            "bp": L1_attn_bp, "wi": L1_ffn_W_in, "wo": L1_ffn_W_out,
        })
    r2 = run_bass_kernel_spmd(_NC2, in2, core_ids=list(range(N_CORES)))
    out = np.concatenate([r["out"] for r in r2.results], 0)
    return out.reshape(B_ALL, T, D)


if __name__ == "__main__":
    nc = build_launch1()
    print("launch1 built ok; instructions:",
          sum(len(f.instructions) for f in []) if False else "n/a")


# version 07
# speedup vs baseline: 1.8502x; 1.0943x over previous
import math
import numpy as np
from contextlib import ExitStack

import concourse.bass as bass
import concourse.mybir as mybir
from concourse.bass_utils import run_bass_kernel_spmd
from concourse.bass_core_rust import IndirectOffsetOnAxis
import ml_dtypes

F32 = mybir.dt.float32
BF16 = mybir.dt.bfloat16
I32 = mybir.dt.int32
AF = mybir.ActivationFunctionType
ALU = mybir.AluOpType
AX = mybir.AxisListType

N_CORES = 8
B_ALL, T, D = 32, 2048, 1024
B_LOC = B_ALL // N_CORES
TOK = B_LOC * T
NTOK_ALL = B_ALL * T
DFF = 4096
H0 = 64
NH, DH = 16, 64
NE = 8
FEXP = 2048
CAP = int(math.ceil(2 * NTOK_ALL / NE * 1.25))
LN_EPS = 1e-5
RMS_EPS = 1e-8

def ceil_div(a, b):
    return (a + b - 1) // b


ENGINES = ("sync", "tensor", "vector", "scalar", "gpsimd")
COMPUTE_ENGINES = {"tensor", "vector", "scalar"}


class Prog:

    def __init__(self, nc, es, streams=(), relax_tensor=True):
        self.nc = nc
        self.relax_tensor = relax_tensor
        self.es = es
        self.ops = {e: [] for e in ENGINES}
        self.nops = {}
        self.keys = list(ENGINES) + ["dma_" + s for s in streams]
        for k in self.keys:
            self.nops[k] = 0
        self.sem = {k: es.enter_context(nc.semaphore(f"prog_{k}")) for k in self.keys}
        self.needed = {k: set() for k in self.keys}
        self.writer = {}
        self.readers = {}
        self.base_waits = []
        self.guaranteed = {}

    def emit(self, eng, fn, reads=(), writes=(), waits=(), dma=False, stream=None,
             pe_relax=False):
        if dma:
            assert stream is not None, "DMA emits must name a stream"
            key = "dma_" + stream
        else:
            key = eng
        deps = list(waits) + self.base_waits
        for r in reads:
            w = self.writer.get(r)
            if w is not None:
                deps.append(w)
        for r in writes:
            w = self.writer.get(r)
            if w is not None:
                deps.append(w)
            deps.extend(self.readers.get(r, ()))
        best = {}
        for (dk, di) in deps:
            if self.relax_tensor and dk == "tensor" and eng == "tensor" and not dma:
                continue
            if best.get(dk, -1) < di:
                best[dk] = di
        for k in list(best):
            if k.startswith("dma_") and not (k == key and dma):
                best[k] = self.nops[k] - 1
        if key in best and best[key] >= self.nops[key]:
            raise RuntimeError(f"self-dep in future on {key}")
        g = self.guaranteed.setdefault(eng, {})
        toklist = []
        for k, i in sorted(best.items()):
            if g.get(k, -1) >= i:
                continue
            g[k] = i
            toklist.append((k, i))
            self.needed[k].add(i)
        opidx = self.nops[key]
        if dma:
            self.needed[key].add(opidx)
        self.ops[eng].append((toklist, fn, key, opidx))
        self.nops[key] += 1
        tok = (key, opidx)
        for r in writes:
            self.writer[r] = tok
            self.readers[r] = []
        for r in reads:
            self.readers.setdefault(r, []).append(tok)
        return tok

    def barrier_token(self):
        return [(k, self.nops[k] - 1) for k in self.keys if self.nops[k] > 0]

    def finalize(self, blk):
        cum = {}
        incof = {}
        for k in self.keys:
            step = 16 if k.startswith("dma_") else 1
            need = self.needed[k]
            c = 0
            arr = []
            inc = []
            for i in range(self.nops[k]):
                if i in need:
                    c += step
                    inc.append(step)
                else:
                    inc.append(0)
                arr.append(c)
            cum[k] = arr
            incof[k] = inc
        semmap = self.sem
        for e in ENGINES:
            ops = self.ops[e]
            if not ops:
                continue

            def make_body(ops):
                def body(engobj):
                    for waits, fn, key, opidx in ops:
                        for (wk, wi) in waits:
                            engobj.wait_ge(semmap[wk], cum[wk][wi])
                        inst = fn(engobj)
                        step = incof[key][opidx]
                        if step:
                            inst.then_inc(semmap[key], step)
                return body
            getattr(blk, e)(make_body(ops))



def build_launch1(skip=()):
    nc = bass.Bass("TRN2", target_bir_lowering=False, debug=False)
    x_d = nc.dram_tensor("x", [TOK, D], F32, kind="ExternalInput").ap()
    wq_d = nc.dram_tensor("wq", [D, H0], F32, kind="ExternalInput").ap()
    wk_d = nc.dram_tensor("wk", [D, H0], F32, kind="ExternalInput").ap()
    wv_d = nc.dram_tensor("wv", [D, H0], F32, kind="ExternalInput").ap()
    wo_d = nc.dram_tensor("wo", [H0, D], F32, kind="ExternalInput").ap()
    wup_d = nc.dram_tensor("wup", [DFF, D], F32, kind="ExternalInput").ap()
    wdn_d = nc.dram_tensor("wdn", [D, DFF], F32, kind="ExternalInput").ap()
    wg_d = nc.dram_tensor("wg", [D, NE], F32, kind="ExternalInput").ap()

    x1_d = nc.dram_tensor("x1", [TOK, D], F32, kind="ExternalOutput").ap()
    xn1_d = nc.dram_tensor("xn1", [TOK, D], BF16, kind="ExternalOutput").ap()
    lgt_d = nc.dram_tensor("lgt", [TOK, NE], F32, kind="ExternalOutput").ap()

    w1t_d = nc.dram_tensor("w1t", [DFF // 128, 128, D // 128, 128], BF16).ap()
    w2t_d = nc.dram_tensor("w2t", [D // 128, 128, DFF // 128, 128], BF16).ap()

    es = ExitStack()
    P = Prog(nc, es, streams=("wsmall", "wraw", "wtst", "xin", "ost",
                              "w10", "w11", "w12", "w20", "w21", "w22"))

    def sb(name, shape, dtype):
        return es.enter_context(nc.sbuf_tensor(name, shape, dtype))

    def psum(name, shape, dtype):
        return es.enter_context(nc.psum_tensor(name, shape, dtype))

    ones_f = sb("ones_f", [128, 128], F32)
    ones_b = sb("ones_b", [128, 128], BF16)
    ident_b = sb("ident_b", [128, 128], BF16)
    ident_f = sb("ident_f", [128, 128], F32)
    ultri_f = sb("ultri_f", [128, 128], F32)
    onecol_b = sb("onecol_b", [128, 1], BF16)

    qw_s = sb("qw_s", [128, 8 * H0], BF16)
    kw_s = sb("kw_s", [128, 8 * H0], BF16)
    vw_s = sb("vw_s", [128, 8 * H0], BF16)
    ow_s = sb("ow_s", [H0, D], BF16)
    gw_s = sb("gw_s", [128, 8 * NE], F32)


    xrow = sb("xrow", [128, 2 * D], F32)
    xsq = sb("xsq", [128, D], F32)
    s_a = sb("s_a", [128, 1], F32)
    s_b = sb("s_b", [128, 1], F32)
    mu = sb("mu", [128, 1], F32)
    var = sb("var", [128, 1], F32)
    rstd = sb("rstd", [128, 1], F32)
    xn_tok = sb("xn_tok", [128, 2 * D], BF16)
    xnT = sb("xnT", [128, 8 * T], BF16)
    xnT3 = xnT[:].rearrange("p (c t) -> p c t", c=8)

    qT_s = sb("qT_s", [H0, T], BF16)
    kT_s = sb("kT_s", [H0, T], BF16)
    vT = sb("vT", [H0, T], BF16)
    oT = sb("oT", [H0, T], BF16)
    ktm = sb("ktm", [128, 2 * H0], BF16)
    vtm = sb("vtm", [128, 2 * H0], BF16)
    ktm_f = sb("ktm_f", [128, 128], F32)
    wtm_b = sb("wtm_b", [128, 128], BF16)
    rd_row = sb("rd_row", [1, 128], F32)
    qs_b = sb("qs_b", [H0, 128], BF16)
    st_f = sb("st_f", [H0, 4 * (H0 + 1)], F32)
    st_b = sb("st_b", [H0, H0 + 1], BF16)

    w1sb = sb("w1sb", [128, 3 * 4 * 8 * 128], BF16)
    w2sb = sb("w2sb", [128, 3 * 32 * 128], BF16)
    xnormT = sb("xnormT", [128, 8 * 512], BF16)
    hbuf = sb("hbuf", [128, 32 * 512], BF16)
    hsq = sb("hsq", [128, 2 * 512], BF16)
    t1s = sb("t1s", [128, 512], F32)
    t2s = sb("t2s", [128, 512], F32)
    sum_fm = sb("sum_fm", [128, 8 * 512], F32)
    x1tok = sb("x1tok", [128, D], F32)
    xn1tok_f = sb("xn1tok_f", [128, D], F32)
    xn1tok_b = sb("xn1tok_b", [128, D], BF16)
    xn1T_f = sb("xn1T_f", [128, 512], F32)
    lgt_row = sb("lgt_row", [8, 128], F32)
    lgt_f = sb("lgt_f", [128, NE], F32)

    esA = ExitStack()

    def sbA(name, shape, dtype):
        return esA.enter_context(nc.sbuf_tensor(name, shape, dtype))

    wld = sbA("wld", [128, 1024], F32)
    wabs = sbA("wabs", [128, 1], F32)
    wsgn = sbA("wsgn", [128, 1024], F32)
    wtb = sbA("wtb", [128, 1024], BF16)
    alpha_acc = sbA("alpha_acc", [1, 2], F32)
    tau_col = sbA("tau_col", [128, 1], F32)
    tr_ev = sbA("tr_ev", [128, 2 * 128], BF16)

    PA = psum("PA", [128, 512], F32)
    PB = psum("PB", [128, 512], F32)
    PC = psum("PC", [128, 512], F32)
    PD = psum("PD", [128, 1024], F32)
    PEb = psum("PEb", [128, 1024], BF16)
    PX = psum("PX", [128, 512], F32)
    PY = psum("PY", [128, 512], F32)

    blk = es.enter_context(nc.Block())

    P.emit("vector", lambda v: v.memset(ones_f[:], 1.0), writes=["ones_f"])
    P.emit("vector", lambda v: v.memset(ones_b[:], 1.0), writes=["ones_b"])
    P.emit("vector", lambda v: v.memset(onecol_b[:], 1.0), writes=["onecol_b"])
    P.emit("gpsimd", lambda g: g.affine_select(
        ident_b[:], ones_b[:], [[-1, 128]], ALU.is_equal, 0.0, channel_multiplier=1),
        reads=["ones_b"], writes=["ident_b"])
    P.emit("gpsimd", lambda g: g.affine_select(
        ident_f[:], ones_f[:], [[-1, 128]], ALU.is_equal, 0.0, channel_multiplier=1),
        reads=["ones_f"], writes=["ident_f"])
    P.emit("gpsimd", lambda g: g.affine_select(
        ultri_f[:], ones_f[:], [[1, 128]], ALU.is_ge, 0.0, channel_multiplier=-1),
        reads=["ones_f"], writes=["ultri_f"])

    wq3 = wq_d.rearrange("(c p) h -> p c h", p=128)
    wk3 = wk_d.rearrange("(c p) h -> p c h", p=128)
    wv3 = wv_d.rearrange("(c p) h -> p c h", p=128)
    wg3 = wg_d.rearrange("(c p) e -> p c e", p=128)
    P.emit("gpsimd", lambda g: g.dma_start(
        out=qw_s[:].rearrange("p (c h) -> p c h", c=8), in_=wq3), writes=["qw_s"], dma=True, stream="wsmall")
    P.emit("gpsimd", lambda g: g.dma_start(
        out=kw_s[:].rearrange("p (c h) -> p c h", c=8), in_=wk3), writes=["kw_s"], dma=True, stream="wsmall")
    P.emit("gpsimd", lambda g: g.dma_start(
        out=vw_s[:].rearrange("p (c h) -> p c h", c=8), in_=wv3), writes=["vw_s"], dma=True, stream="wsmall")
    P.emit("gpsimd", lambda g: g.dma_start(out=ow_s[:], in_=wo_d[:, :]), writes=["ow_s"], dma=True, stream="wsmall")
    P.emit("gpsimd", lambda g: g.dma_start(
        out=gw_s[:].rearrange("p (c e) -> p c e", c=8), in_=wg3), writes=["gw_s"], dma=True, stream="wsmall")

    def tern_alpha(w_d, nrows, ncols, slot):
        row_tiles = nrows // 128
        hunks = ceil_div(ncols, 1024)
        first = True
        for r in range(row_tiles):
            for h in range(hunks):
                hc = min(1024, ncols - h * 1024)
                P.emit("sync", lambda s, r=r, h=h, hc=hc: s.dma_start(
                    out=wld[:, 0:hc],
                    in_=w_d[r * 128:(r + 1) * 128, h * 1024:h * 1024 + hc]),
                    writes=["wld"], dma=True, stream="wraw")
                P.emit("vector", lambda v, hc=hc: v.tensor_reduce(
                    out=wabs[:], in_=wld[:, 0:hc], axis=AX.X, op=ALU.add,
                    apply_absolute_value=True), reads=["wld"], writes=["wabs"])
                P.emit("tensor", lambda t, first=first, last=(r == row_tiles - 1 and h == hunks - 1): t.matmul(
                    PY[0:1, slot:slot + 1], ones_f[:, 0:1], wabs[:],
                    start=first, stop=last),
                    reads=["wabs", "ones_f"], writes=["PY"])
                first = False
        P.emit("vector", lambda v: v.tensor_copy(
            out=alpha_acc[0:1, slot:slot + 1], in_=PY[0:1, slot:slot + 1]),
            reads=["PY"], writes=["alpha_acc"])
        P.emit("tensor", lambda t: t.matmul(
            PY[0:128, 256:257], ones_f[0:1, :], alpha_acc[0:1, slot:slot + 1],
            start=True, stop=True),
            reads=["alpha_acc"], writes=["PY"])
        P.emit("vector", lambda v: v.tensor_scalar_mul(
            out=tau_col[:], in0=PY[0:128, 256:257], scalar1=0.5 / (nrows * ncols)),
            reads=["PY"], writes=["tau_col"])

    def tern_pass2(w_d, nrows, ncols, wt_out_d):
        row_tiles = nrows // 128
        hunks = ceil_div(ncols, 1024)
        for rb in range(row_tiles):
            for h in range(hunks):
                hc = min(1024, ncols - h * 1024)
                P.emit("sync", lambda s, rb=rb, h=h, hc=hc: s.dma_start(
                    out=wld[:, 0:hc],
                    in_=w_d[rb * 128:(rb + 1) * 128, h * 1024:h * 1024 + hc]),
                    writes=["wld"], dma=True, stream="wraw")
                P.emit("scalar", lambda sc, hc=hc: sc.activation(
                    out=wsgn[:, 0:hc], in_=wld[:, 0:hc], func=AF.Sign),
                    reads=["wld"], writes=["wsgn"])
                P.emit("scalar", lambda sc, hc=hc: sc.activation(
                    out=wld[:, 0:hc], in_=wld[:, 0:hc], func=AF.Abs),
                    reads=["wld"], writes=["wld"])
                P.emit("vector", lambda v, hc=hc: v.tensor_scalar(
                    out=wld[:, 0:hc], in0=wld[:, 0:hc], scalar1=tau_col[:, 0:1],
                    scalar2=None, op0=ALU.is_gt),
                    reads=["wld", "tau_col"], writes=["wld"])
                P.emit("vector", lambda v, hc=hc: v.tensor_tensor(
                    out=wtb[:, 0:hc], in0=wsgn[:, 0:hc], in1=wld[:, 0:hc],
                    op=ALU.mult), reads=["wsgn", "wld"], writes=["wtb"])
                for cbl in range(hc // 128):
                    cb = h * 8 + cbl
                    half = cbl % 8
                    tslot = cbl % 2
                    P.emit("tensor", lambda t, cbl=cbl, half=half: t.transpose(
                        PEb[:, half * 128:(half + 1) * 128],
                        wtb[:, cbl * 128:(cbl + 1) * 128], ident_b[:]),
                        reads=["wtb", "ident_b"], writes=["PEb"])
                    P.emit("vector", lambda v, half=half, tslot=tslot: v.tensor_copy(
                        out=tr_ev[:, tslot * 128:(tslot + 1) * 128],
                        in_=PEb[:, half * 128:(half + 1) * 128]),
                        reads=["PEb"], writes=[f"trev{tslot}"])
                    P.emit("gpsimd", lambda g, rb=rb, cb=cb, tslot=tslot: g.dma_start(
                        out=wt_out_d[rb, :, cb, :],
                        in_=tr_ev[:, tslot * 128:(tslot + 1) * 128]),
                        reads=[f"trev{tslot}"], dma=True, stream="wtst")

    if "tern" not in skip:
        tern_alpha(wup_d, DFF, D, 0)
        tern_pass2(wup_d, DFF, D, w1t_d)
        tern_alpha(wdn_d, D, DFF, 1)
        tern_pass2(wdn_d, D, DFF, w2t_d)
    esA.close()
    P.base_waits = P.barrier_token()

    NT = T // 128
    NG = T // 512

    def layernorm_tile(src_res, src_ap, dst_ap, dst_res, out_dtype_pair=None):
        P.emit("vector", lambda v: v.tensor_reduce(
            out=s_a[:], in_=src_ap, axis=AX.X, op=ALU.add),
            reads=[src_res], writes=["s_a"])
        P.emit("scalar", lambda sc: sc.activation(
            out=xsq[:], in_=src_ap, func=AF.Square), reads=[src_res], writes=["xsq"])
        P.emit("vector", lambda v: v.tensor_reduce(
            out=s_b[:], in_=xsq[:], axis=AX.X, op=ALU.add),
            reads=["xsq"], writes=["s_b"])
        P.emit("vector", lambda v: v.tensor_scalar_mul(
            out=mu[:], in0=s_a[:], scalar1=1.0 / D), reads=["s_a"], writes=["mu"])
        P.emit("vector", lambda v: v.tensor_scalar_mul(
            out=var[:], in0=s_b[:], scalar1=1.0 / D), reads=["s_b"], writes=["var"])
        P.emit("vector", lambda v: v.tensor_tensor(
            out=s_b[:], in0=mu[:], in1=mu[:], op=ALU.mult),
            reads=["mu"], writes=["s_b"])
        P.emit("vector", lambda v: v.tensor_tensor(
            out=var[:], in0=var[:], in1=s_b[:], op=ALU.subtract),
            reads=["var", "s_b"], writes=["var"])
        P.emit("vector", lambda v: v.tensor_scalar_add(
            out=var[:], in0=var[:], scalar1=LN_EPS), reads=["var"], writes=["var"])
        P.emit("scalar", lambda sc: sc.activation(
            out=rstd[:], in_=var[:], func=AF.Sqrt),
            reads=["var"], writes=["rstd"])
        P.emit("vector", lambda v: v.reciprocal(out=rstd[:], in_=rstd[:]),
            reads=["rstd"], writes=["rstd"])
        P.emit("vector", lambda v: v.tensor_scalar(
            out=dst_ap, in0=src_ap, scalar1=mu[:, 0:1], scalar2=rstd[:, 0:1],
            op0=ALU.subtract, op1=ALU.mult),
            reads=[src_res, "mu", "rstd"], writes=[dst_res])

    for b in range(B_LOC):
        row0 = b * T

        P.emit("vector", lambda v, b=b: v.memset(
            st_f[:, b * (H0 + 1):(b + 1) * (H0 + 1)], 0.0), writes=["st_f"])
        P.emit("vector", lambda v: v.memset(st_b[:], 0.0), writes=["st_b"])
        stf = st_f[:, b * (H0 + 1):(b + 1) * (H0 + 1)]

        def emit_xload(n):
            slot = n % 2
            xr = xrow[:, slot * D:(slot + 1) * D]
            P.emit("gpsimd", lambda s, n=n, xr=xr, row0=row0: s.dma_start(
                out=xr, in_=x_d[row0 + n * 128:row0 + (n + 1) * 128, :]),
                writes=[f"xrow{slot}"], dma=True, stream="xin")
        if "ln" in skip:
            P.emit("vector", lambda v: v.memset(xnT[:], 0.01), writes=["xnT"])
        emit_xload(0)
        for n in range(0 if "ln" in skip else 16):
            slot = n % 2
            xr = xrow[:, slot * D:(slot + 1) * D]
            xnt = xn_tok[:, slot * D:(slot + 1) * D]
            if n + 1 < 16:
                emit_xload(n + 1)
            layernorm_tile(f"xrow{slot}", xr, xnt, f"xn{slot}")
            for c in range(8):
                P.emit("tensor", lambda t, c=c, xnt=xnt, slot=slot: t.transpose(
                    PEb[:, c * 128:(c + 1) * 128],
                    xnt[:, c * 128:(c + 1) * 128], ident_b[:]),
                    reads=[f"xn{slot}", "ident_b", "xnT"], writes=["PEb"])
            P.emit("vector", lambda v, n=n: v.tensor_copy(
                out=xnT3[:, :, n * 128:(n + 1) * 128],
                in_=PEb[:].rearrange("p (c t) -> p c t", c=8)),
                reads=["PEb"], writes=["xnT"])

        for g in range(0 if "qkv" in skip else NG):
            gslc = slice(g * 512, (g + 1) * 512)
            for c in range(8):
                P.emit("tensor", lambda t, c=c, gslc=gslc: t.matmul(
                    PA[0:64, 0:512], qw_s[:, c * H0:(c + 1) * H0],
                    xnT3[:, c, gslc], start=(c == 0), stop=(c == 7)),
                    reads=["qw_s", "xnT"], writes=["PA"])
            for c in range(8):
                P.emit("tensor", lambda t, c=c, gslc=gslc: t.matmul(
                    PB[0:64, 0:512], kw_s[:, c * H0:(c + 1) * H0],
                    xnT3[:, c, gslc], start=(c == 0), stop=(c == 7)),
                    reads=["kw_s", "xnT"], writes=["PB"])
            for c in range(8):
                P.emit("tensor", lambda t, c=c, gslc=gslc: t.matmul(
                    PC[0:64, 0:512], vw_s[:, c * H0:(c + 1) * H0],
                    xnT3[:, c, gslc], start=(c == 0), stop=(c == 7)),
                    reads=["vw_s", "xnT"], writes=["PC"])
            for (pp, pres, dst, dres) in (
                    (PA, "PA", qT_s, "qT"), (PB, "PB", kT_s, "kT")):
                P.emit("vector", lambda v, pp=pp: v.tensor_scalar_min(
                    out=xsq[0:64, 0:512], in0=pp[0:64, 0:512], scalar1=0.0),
                    reads=[pres], writes=["xsq"])
                P.emit("scalar", lambda sc: sc.activation(
                    out=xsq[0:64, 0:512], in_=xsq[0:64, 0:512], func=AF.Exp),
                    reads=["xsq"], writes=["xsq"])
                P.emit("vector", lambda v, pp=pp: v.tensor_scalar_max(
                    out=xsq[0:64, 512:1024], in0=pp[0:64, 0:512], scalar1=0.0),
                    reads=[pres], writes=["xsq"])
                P.emit("vector", lambda v, gslc=gslc, dst=dst: v.tensor_tensor(
                    out=dst[:, gslc], in0=xsq[0:64, 0:512], in1=xsq[0:64, 512:1024],
                    op=ALU.add),
                    reads=["xsq"], writes=[dres])
            P.emit("vector", lambda v, gslc=gslc: v.tensor_copy(
                out=vT[:, gslc], in_=PC[0:64, 0:512]),
                reads=["PC"], writes=["vT"])

        if "scan" in skip:
            P.emit("vector", lambda v: v.memset(oT[:], 0.0), writes=["oT"])
        for i in range(0 if "scan" in skip else 16):
            t0 = i * 128
            cs = slice(t0, t0 + 128)
            sl = i % 2
            P.emit("tensor", lambda t, cs=cs, sl=sl: t.transpose(
                PEb[:, 0:64], kT_s[:, cs], ident_b[0:64, 0:64]),
                reads=["kT", "ident_b", "ktm" + str(sl)], writes=["PEb"])
            P.emit("vector", lambda v, sl=sl: v.tensor_copy(
                out=ktm[:, sl * H0:(sl + 1) * H0], in_=PEb[0:128, 0:64]),
                reads=["PEb"], writes=["ktm" + str(sl)])
            P.emit("tensor", lambda t, cs=cs, sl=sl: t.transpose(
                PEb[:, 64:128], vT[0:64, cs], ident_b[0:64, 0:64]),
                reads=["vT", "ident_b", "vtm" + str(sl)], writes=["PEb"])
            P.emit("vector", lambda v, sl=sl: v.tensor_copy(
                out=vtm[:, sl * H0:(sl + 1) * H0], in_=PEb[0:128, 64:128]),
                reads=["PEb"], writes=["vtm" + str(sl)])
            ktm_s = ktm[:, sl * H0:(sl + 1) * H0]
            vtm_s = vtm[:, sl * H0:(sl + 1) * H0]
            P.emit("tensor", lambda t, cs=cs: t.matmul(
                PA[:, 0:128], kT_s[:, cs], qT_s[:, cs], start=True, stop=True),
                reads=["qT", "kT"], writes=["PA"])
            P.emit("vector", lambda v: v.tensor_tensor(
                out=ktm_f[:], in0=PA[:, 0:128], in1=ultri_f[:], op=ALU.mult),
                reads=["PA", "ultri_f"], writes=["ktm_f"])
            P.emit("tensor", lambda t: t.matmul(
                PC[0:1, 0:128], ones_f[:, 0:1], ktm_f[:], start=True, stop=False),
                reads=["ktm_f", "ones_f"], writes=["PC"])
            P.emit("tensor", lambda t, cs=cs: t.matmul(
                PC[0:1, 0:128], st_b[:, H0:H0 + 1], qT_s[:, cs],
                start=False, stop=True),
                reads=["st_b", "qT"], writes=["PC"])
            P.emit("vector", lambda v: v.tensor_scalar_max(
                out=rd_row[:], in0=PC[0:1, 0:128], scalar1=1e-6),
                reads=["PC"], writes=["rd_row"])
            P.emit("vector", lambda v: v.reciprocal(out=rd_row[:], in_=rd_row[:]),
                reads=["rd_row"], writes=["rd_row"])
            P.emit("tensor", lambda t: t.matmul(
                PX[0:64, 0:128], ones_f[0:1, 0:64], rd_row[:], start=True, stop=True),
                reads=["rd_row", "ones_f"], writes=["PX"])
            P.emit("vector", lambda v, cs=cs: v.tensor_tensor(
                out=qs_b[:], in0=qT_s[:, cs], in1=PX[0:64, 0:128], op=ALU.mult),
                reads=["qT", "PX"], writes=["qs_b"])
            P.emit("tensor", lambda t, cs=cs: t.matmul(
                PA[:, 128:256], vT[0:64, cs], qs_b[:], start=True, stop=True),
                reads=["vT", "qs_b"], writes=["PA"])
            P.emit("vector", lambda v: v.tensor_tensor(
                out=wtm_b[:], in0=PA[:, 128:256], in1=ultri_f[:], op=ALU.mult),
                reads=["PA", "ultri_f"], writes=["wtm_b"])
            P.emit("tensor", lambda t: t.matmul(
                PB[0:64, 0:128], st_b[:, 0:H0], qs_b[:], start=True, stop=False),
                reads=["st_b", "qs_b"], writes=["PB"])
            P.emit("tensor", lambda t, ktm_s=ktm_s, sl=sl: t.matmul(
                PB[0:64, 0:128], ktm_s, wtm_b[:], start=False, stop=True),
                reads=["wtm_b", "ktm" + str(sl)], writes=["PB"])
            P.emit("vector", lambda v, cs=cs: v.tensor_copy(
                out=oT[:, cs], in_=PB[0:64, 0:128]), reads=["PB"], writes=["oT"])
            P.emit("tensor", lambda t, vtm_s=vtm_s, ktm_s=ktm_s, sl=sl: t.matmul(
                PX[0:64, 128:192], vtm_s, ktm_s, start=True, stop=True),
                reads=["ktm" + str(sl), "vtm" + str(sl)], writes=["PX"])
            P.emit("tensor", lambda t, ktm_s=ktm_s, sl=sl: t.matmul(
                PX[0:64, 192:193], ktm_s, onecol_b[:], start=True, stop=True),
                reads=["onecol_b", "ktm" + str(sl)], writes=["PX"])
            P.emit("vector", lambda v, stf=stf: v.tensor_tensor(
                out=stf, in0=stf, in1=PX[0:64, 128:193], op=ALU.add),
                reads=["PX", "st_f"], writes=["st_f"])
            P.emit("vector", lambda v, stf=stf: v.tensor_copy(out=st_b[:], in_=stf),
                reads=["st_f"], writes=["st_b"])

        for g in range(NG):
            tslc = slice(g * 512, (g + 1) * 512)
            for c in range(8):
                hs = c % 2
                hq = hsq[:, hs * 512:(hs + 1) * 512]
                P.emit("scalar", lambda sc, c=c, hq=hq, tslc=tslc: sc.activation(
                    out=hq, in_=xnT3[:, c, tslc], func=AF.Square),
                    reads=["xnT"], writes=[f"hsq{hs}"])
                P.emit("tensor", lambda t, c=c, hq=hq: t.matmul(
                    PX[:, 0:512], ones_b[:], hq, start=(c == 0), stop=(c == 7)),
                    reads=[f"hsq{hs}", "ones_b"], writes=["PX"])
            P.emit("scalar", lambda sc: sc.activation(
                out=t1s[:], in_=PX[:, 0:512], func=AF.Sqrt, scale=1.0 / D),
                reads=["PX"], writes=["t1s"])
            P.emit("vector", lambda v: v.tensor_scalar_add(
                out=t1s[:], in0=t1s[:], scalar1=RMS_EPS), reads=["t1s"], writes=["t1s"])
            P.emit("vector", lambda v: v.reciprocal(out=t1s[:], in_=t1s[:]),
                reads=["t1s"], writes=["t1s"])
            for c in range(8):
                P.emit("vector", lambda v, c=c, tslc=tslc: v.tensor_tensor(
                    out=xnormT[:, c * 512:(c + 1) * 512], in0=xnT3[:, c, tslc],
                    in1=t1s[:], op=ALU.mult),
                    reads=["xnT", "t1s"], writes=["xnormT"])

            def emit_w1load(sb4):
                wsl = sb4 % 3
                P.emit("sync", lambda s, sb4=sb4, wsl=wsl: s.dma_start(
                    out=w1sb[:, wsl * 4096:(wsl + 1) * 4096].rearrange(
                        "p (b c f) -> p b c f", b=4, c=8),
                    in_=w1t_d[sb4 * 4:(sb4 + 1) * 4].rearrange("b p c f -> p b c f")),
                    writes=[f"w1sb{wsl}"], dma=True, stream=f"w1{wsl}")
            emit_w1load(0)
            emit_w1load(1)
            for fb in range(32):
                wsl = (fb // 4) % 3
                if fb % 4 == 0 and fb // 4 + 2 < 8:
                    emit_w1load(fb // 4 + 2)
                ppy = PA if fb % 2 == 0 else PB
                ppy_res = "PA" if fb % 2 == 0 else "PB"
                for c in range(8):
                    off = wsl * 4096 + (fb % 4) * 1024 + c * 128
                    P.emit("tensor", lambda t, c=c, off=off, ppy=ppy: t.matmul(
                        ppy[:, 0:512],
                        w1sb[:, off:off + 128],
                        xnormT[:, c * 512:(c + 1) * 512],
                        start=(c == 0), stop=(c == 7)),
                        reads=[f"w1sb{wsl}", "xnormT"], writes=[ppy_res],
                        pe_relax=(c != 0))
                P.emit("scalar", lambda sc, fb=fb, ppy=ppy: sc.activation(
                    out=hbuf[:, fb * 512:(fb + 1) * 512], in_=ppy[:, 0:512],
                    func=AF.Gelu), reads=[ppy_res], writes=[f"hb{fb}"])
                hs = fb % 2
                hq = hsq[:, hs * 512:(hs + 1) * 512]
                P.emit("scalar", lambda sc, fb=fb, hq=hq: sc.activation(
                    out=hq, in_=hbuf[:, fb * 512:(fb + 1) * 512], func=AF.Square),
                    reads=[f"hb{fb}"], writes=[f"hsq{hs}"])
                P.emit("tensor", lambda t, fb=fb, hq=hq: t.matmul(
                    PY[:, 0:512], ones_b[:], hq, start=(fb == 0), stop=(fb == 31)),
                    reads=[f"hsq{hs}", "ones_b"], writes=["PY"])
            P.emit("scalar", lambda sc: sc.activation(
                out=t2s[:], in_=PY[:, 0:512], func=AF.Sqrt, scale=1.0 / DFF),
                reads=["PY"], writes=["t2s"])
            P.emit("vector", lambda v: v.tensor_scalar_add(
                out=t2s[:], in0=t2s[:], scalar1=RMS_EPS), reads=["t2s"], writes=["t2s"])
            P.emit("vector", lambda v: v.reciprocal(out=t2s[:], in_=t2s[:]),
                reads=["t2s"], writes=["t2s"])

            def emit_w2load(dc):
                wsl = dc % 3
                P.emit("sync", lambda s, dc=dc, wsl=wsl: s.dma_start(
                    out=w2sb[:, wsl * 4096:(wsl + 1) * 4096].rearrange(
                        "p (c f) -> p c f", c=32),
                    in_=w2t_d[dc]),
                    writes=[f"w2sb{wsl}"], dma=True, stream=f"w2{wsl}")
            emit_w2load(0)
            emit_w2load(1)
            for dc in range(8):
                ppz = PA if dc % 2 == 0 else PB
                ppz_res = "PA" if dc % 2 == 0 else "PB"
                wsl = dc % 3
                if dc + 2 < 8:
                    emit_w2load(dc + 2)
                for fc in range(32):
                    P.emit("tensor", lambda t, fc=fc, wsl=wsl, ppz=ppz: t.matmul(
                        ppz[:, 0:512],
                        w2sb[:, wsl * 4096 + fc * 128:wsl * 4096 + (fc + 1) * 128],
                        hbuf[:, fc * 512:(fc + 1) * 512],
                        start=(fc == 0), stop=(fc == 31)),
                        reads=[f"w2sb{wsl}", f"hb{fc}"], writes=[ppz_res])
                P.emit("tensor", lambda t, dc=dc, tslc=tslc: t.matmul(
                    PC[:, 0:512], ow_s[0:64, dc * 128:(dc + 1) * 128], oT[:, tslc],
                    start=True, stop=True), reads=["ow_s", "oT"], writes=["PC"])
                P.emit("vector", lambda v, dc=dc, ppz=ppz: v.tensor_tensor(
                    out=sum_fm[:, dc * 512:(dc + 1) * 512], in0=ppz[:, 0:512],
                    in1=t2s[:], op=ALU.mult),
                    reads=[ppz_res, "t2s"], writes=[f"sfm{dc}"])
                P.emit("vector", lambda v, dc=dc: v.tensor_tensor(
                    out=sum_fm[:, dc * 512:(dc + 1) * 512],
                    in0=sum_fm[:, dc * 512:(dc + 1) * 512],
                    in1=PC[:, 0:512], op=ALU.add),
                    reads=[f"sfm{dc}", "PC"], writes=[f"sfm{dc}"])

            for ts in range(0 if "ts" in skip else 4):
                rows = slice(row0 + g * 512 + ts * 128, row0 + g * 512 + ts * 128 + 128)
                for dc in range(8):
                    P.emit("tensor", lambda t, dc=dc, ts=ts: t.transpose(
                        PD[:, dc * 128:(dc + 1) * 128],
                        sum_fm[:, dc * 512 + ts * 128:dc * 512 + (ts + 1) * 128],
                        ident_f[:]),
                        reads=[f"sfm{dc}", "ident_f", "x1tok"], writes=["PD"])
                P.emit("gpsimd", lambda s, rows=rows: s.dma_start(
                    out=xrow[:, 0:D], in_=x_d[rows, :]), writes=["xrow0"], dma=True,
                    stream="xin")
                P.emit("vector", lambda v: v.tensor_tensor(
                    out=x1tok[:], in0=xrow[:, 0:D], in1=PD[:, 0:1024], op=ALU.add),
                    reads=["xrow0", "PD"], writes=["x1tok"])
                P.emit("gpsimd", lambda g_, rows=rows: g_.dma_start(
                    out=x1_d[rows, :], in_=x1tok[:]), reads=["x1tok"], dma=True, stream="ost")
                layernorm_tile("x1tok", x1tok[:], xn1tok_f[:], "xn1f")
                P.emit("vector", lambda v: v.tensor_copy(
                    out=xn1tok_b[:], in_=xn1tok_f[:]), reads=["xn1f"], writes=["xn1b"])
                P.emit("gpsimd", lambda g_, rows=rows: g_.dma_start(
                    out=xn1_d[rows, :], in_=xn1tok_b[:]), reads=["xn1b"], dma=True, stream="ost")
                for ch in range(2):
                    for c4 in range(4):
                        c = ch * 4 + c4
                        P.emit("tensor", lambda t, c=c, c4=c4: t.transpose(
                            PX[:, c4 * 128:(c4 + 1) * 128],
                            xn1tok_f[:, c * 128:(c + 1) * 128], ident_f[:]),
                            reads=["xn1f", "ident_f", "xn1T"], writes=["PX"])
                    P.emit("vector", lambda v: v.tensor_copy(
                        out=xn1T_f[:], in_=PX[:, 0:512]), reads=["PX"], writes=["xn1T"])
                    for c4 in range(4):
                        c = ch * 4 + c4
                        P.emit("tensor", lambda t, c=c, c4=c4, ch=ch: t.matmul(
                            PY[0:8, 0:128], gw_s[:, c * NE:(c + 1) * NE],
                            xn1T_f[:, c4 * 128:(c4 + 1) * 128],
                            start=(c == 0), stop=(c == 7)),
                            reads=["gw_s", "xn1T"], writes=["PY"])
                P.emit("vector", lambda v: v.tensor_copy(
                    out=lgt_row[:], in_=PY[0:8, 0:128]), reads=["PY"], writes=["lgt_row"])
                P.emit("tensor", lambda t: t.transpose(
                    PX[:, 128:136], lgt_row[:], ident_f[0:8, 0:8]),
                    reads=["lgt_row", "ident_f", "lgt_f"], writes=["PX"])
                P.emit("vector", lambda v: v.tensor_copy(
                    out=lgt_f[:], in_=PX[0:128, 128:136]), reads=["PX"], writes=["lgt_f"])
                P.emit("gpsimd", lambda g_, rows=rows: g_.dma_start(
                    out=lgt_d[rows, :], in_=lgt_f[:]), reads=["lgt_f"], dma=True, stream="ost")

    toks = P.barrier_token()
    P.emit("vector", lambda v: v.engine_nop(), waits=toks)

    P.finalize(blk)
    es.close()
    return nc





def build_launch2(core_id_unused=None):
    nc = bass.Bass("TRN2", target_bir_lowering=False, debug=False)
    x1_d = nc.dram_tensor("x1", [TOK, D], F32, kind="ExternalInput").ap()
    xn1_d = nc.dram_tensor("xn1", [TOK, D], BF16, kind="ExternalInput").ap()
    lgt_d = nc.dram_tensor("lgt", [NTOK_ALL, NE], F32, kind="ExternalInput").ap()
    w1_d = nc.dram_tensor("w1", [D, 3 * D], F32, kind="ExternalInput").ap()
    b1_d = nc.dram_tensor("b1", [3 * D], F32, kind="ExternalInput").ap()
    wp_d = nc.dram_tensor("wp", [D, D], F32, kind="ExternalInput").ap()
    bp_d = nc.dram_tensor("bp", [D], F32, kind="ExternalInput").ap()
    wi_d = nc.dram_tensor("wi", [NE, D, 2 * FEXP], F32, kind="ExternalInput").ap()
    wo_d = nc.dram_tensor("wo", [NE, FEXP, D], F32, kind="ExternalInput").ap()
    out_d = nc.dram_tensor("out", [TOK, D], F32, kind="ExternalOutput").ap()

    NPOWN = TOK // 512
    coef_d = nc.dram_tensor("coefsc", [NE, NPOWN, 512], F32).ap()
    moe_d = nc.dram_tensor("moesc", [8, 128, TOK], F32).ap()

    es = ExitStack()
    P = Prog(nc, es, streams=(
        "wsmall", "wsmall2", "lgt", "xt", "mz", "wi", "wo2", "macc", "coef",
        "coef2", "xin", "ost", "moin"), relax_tensor=False)

    def sb(name, shape, dtype):
        return es.enter_context(nc.sbuf_tensor(name, shape, dtype))

    def psum(name, shape, dtype):
        return es.enter_context(nc.psum_tensor(name, shape, dtype))

    ones_f = sb("ones_f", [128, 128], F32)
    ones_b = sb("ones_b", [128, 128], BF16)
    ident_b = sb("ident_b", [128, 128], BF16)
    ident_f = sb("ident_f", [128, 128], F32)
    ltm_f = sb("ltm_f", [128, 128], F32)
    cmask = sb("cmask", [128, 128], F32)

    P.emit("vector", lambda v: v.memset(ones_f[:], 1.0), writes=["ones_f"])
    P.emit("vector", lambda v: v.memset(ones_b[:], 1.0), writes=["ones_b"])
    P.emit("gpsimd", lambda g: g.affine_select(
        ident_b[:], ones_b[:], [[-1, 128]], ALU.is_equal, 0.0, channel_multiplier=1),
        reads=["ones_b"], writes=["ident_b"])
    P.emit("gpsimd", lambda g: g.affine_select(
        ident_f[:], ones_f[:], [[-1, 128]], ALU.is_equal, 0.0, channel_multiplier=1),
        reads=["ones_f"], writes=["ident_f"])
    P.emit("gpsimd", lambda g: g.affine_select(
        ltm_f[:], ones_f[:], [[1, 128]], ALU.is_gt, 0.0, channel_multiplier=-1),
        reads=["ones_f"], writes=["ltm_f"])
    P.emit("vector", lambda v: v.memset(cmask[:], 0.0), writes=["cmask"])
    P.emit("gpsimd", lambda g: g.affine_select(
        cmask[:], cmask[:], [[-1, 128]], ALU.is_ge, -1e6, channel_multiplier=1),
        reads=["cmask"], writes=["cmask"])

    NA = NTOK_ALL // 128
    esR = ExitStack()

    def sbR(name, shape, dtype):
        return esR.enter_context(nc.sbuf_tensor(name, shape, dtype))

    lgt_sb = sbR("lgt_sb", [128, NA * NE], F32)
    l3 = lgt_sb[:].rearrange("p (a e) -> p a e", e=NE)
    l3e = lgt_sb[:].rearrange("p (a e) -> p a e", e=NE).transpose([0, 2, 1])
    iota_e = sbR("iota_e", [128, NA * NE], F32)
    i3 = iota_e[:].rearrange("p (a e) -> p a e", e=NE)
    m1 = sbR("m1", [128, NA], F32)
    m2 = sbR("m2", [128, NA], F32)
    idx1 = sbR("idx1", [128, NA], F32)
    idx2 = sbR("idx2", [128, NA], F32)
    e2g = sbR("e2g", [128, NA], F32)
    oh1 = sbR("oh1", [128, NA * NE], F32)
    oh2 = sbR("oh2", [128, NA * NE], F32)
    o13 = oh1[:].rearrange("p (a e) -> p a e", e=NE)
    o23 = oh2[:].rearrange("p (a e) -> p a e", e=NE)
    scr = sbR("scr", [128, NA * NE], F32)
    s3 = scr[:].rearrange("p (a e) -> p a e", e=NE)
    s3e = scr[:].rearrange("p (a e) -> p a e", e=NE).transpose([0, 2, 1])
    scan1 = sbR("scan1", [128, NA * NE], F32)
    c13 = scan1[:].rearrange("p (a e) -> p a e", e=NE)
    c13e = scan1[:].rearrange("p (a e) -> p a e", e=NE).transpose([0, 2, 1])
    scan2 = sbR("scan2", [128, NA * NE], F32)
    c23 = scan2[:].rearrange("p (a e) -> p a e", e=NE)
    c23e = scan2[:].rearrange("p (a e) -> p a e", e=NE).transpose([0, 2, 1])
    cnt1 = sbR("cnt1", [128, NE], F32)
    cnt2 = sbR("cnt2", [128, NE], F32)
    base1 = sbR("base1", [128, NE], F32)
    base2 = sbR("base2", [128, NE], F32)
    tot1r = sbR("tot1r", [1, NE], F32)
    loc1 = sbR("loc1", [128, NA], F32)
    loc2 = sbR("loc2", [128, NA], F32)
    keep1 = sbR("keep1", [128, NA], F32)
    keep2 = sbR("keep2", [128, NA], F32)
    gden = sbR("gden", [128, NA], F32)
    coef = sbR("coef", [128, NA * NE], F32)
    cf3 = coef[:].rearrange("p (a e) -> p a e", e=NE)
    cf3e = coef[:].rearrange("p (a e) -> p a e", e=NE).transpose([0, 2, 1])

    esP1 = ExitStack()
    PR1 = esP1.enter_context(nc.psum_tensor("PR1", [128, 512], F32))
    PR2 = esP1.enter_context(nc.psum_tensor("PR2", [128, 512], F32))

    lgt_pm = lgt_d.rearrange("(p a) e -> p a e", p=128)
    P.emit("sync", lambda s: s.dma_start(out=l3, in_=lgt_pm),
           writes=["lgt_sb"], dma=True, stream="lgt")
    P.emit("gpsimd", lambda g: g.iota(
        iota_e[:], [[0, NA], [1, NE]], channel_multiplier=0,
        allow_small_or_imprecise_dtypes=True), writes=["iota_e"])

    BIG = 1.0e9
    P.emit("vector", lambda v: v.reduce_max(out=m1[:], in_=l3, axis=AX.X),
           reads=["lgt_sb"], writes=["m1"])
    P.emit("vector", lambda v: v.tensor_tensor(
        out=scr[:], in0=l3, in1=m1[:].rearrange("p a -> p a 1" if False else "p (a o) -> p a o", o=1).broadcast_to([128, NA, NE]),
        op=ALU.is_ge), reads=["lgt_sb", "m1"], writes=["scr"])
    P.emit("vector", lambda v: v.tensor_scalar(
        out=scr[:], in0=scr[:], scalar1=-BIG, scalar2=BIG,
        op0=ALU.mult, op1=ALU.add), reads=["scr"], writes=["scr"])
    P.emit("vector", lambda v: v.tensor_tensor(
        out=scr[:], in0=scr[:], in1=iota_e[:], op=ALU.add),
        reads=["scr", "iota_e"], writes=["scr"])
    P.emit("vector", lambda v: v.tensor_reduce(
        out=idx1[:], in_=s3, axis=AX.X, op=ALU.min),
        reads=["scr"], writes=["idx1"])
    P.emit("vector", lambda v: v.tensor_tensor(
        out=oh1[:], in0=iota_e[:],
        in1=idx1[:].rearrange("p (a o) -> p a o", o=1).broadcast_to([128, NA, NE]),
        op=ALU.is_equal), reads=["iota_e", "idx1"], writes=["oh1"])
    P.emit("vector", lambda v: v.tensor_scalar(
        out=scr[:], in0=oh1[:], scalar1=-BIG, scalar2=None, op0=ALU.mult),
        reads=["oh1"], writes=["scr"])
    P.emit("vector", lambda v: v.tensor_tensor(
        out=scr[:], in0=scr[:], in1=lgt_sb[:], op=ALU.add),
        reads=["scr", "lgt_sb"], writes=["scr"])
    P.emit("vector", lambda v: v.reduce_max(out=m2[:], in_=s3, axis=AX.X),
        reads=["scr"], writes=["m2"])
    P.emit("vector", lambda v: v.tensor_tensor(
        out=scr[:], in0=s3,
        in1=m2[:].rearrange("p (a o) -> p a o", o=1).broadcast_to([128, NA, NE]),
        op=ALU.is_ge), reads=["scr", "m2"], writes=["scr"])
    P.emit("vector", lambda v: v.tensor_scalar(
        out=scr[:], in0=scr[:], scalar1=-BIG, scalar2=BIG,
        op0=ALU.mult, op1=ALU.add), reads=["scr"], writes=["scr"])
    P.emit("vector", lambda v: v.tensor_tensor(
        out=scr[:], in0=scr[:], in1=iota_e[:], op=ALU.add),
        reads=["scr", "iota_e"], writes=["scr"])
    P.emit("vector", lambda v: v.tensor_reduce(
        out=idx2[:], in_=s3, axis=AX.X, op=ALU.min),
        reads=["scr"], writes=["idx2"])
    P.emit("vector", lambda v: v.tensor_tensor(
        out=oh2[:], in0=iota_e[:],
        in1=idx2[:].rearrange("p (a o) -> p a o", o=1).broadcast_to([128, NA, NE]),
        op=ALU.is_equal), reads=["iota_e", "idx2"], writes=["oh2"])
    P.emit("vector", lambda v: v.tensor_tensor(
        out=e2g[:], in0=m2[:], in1=m1[:], op=ALU.subtract),
        reads=["m2", "m1"], writes=["e2g"])
    P.emit("scalar", lambda sc: sc.activation(
        out=e2g[:], in_=e2g[:], func=AF.Exp), reads=["e2g"], writes=["e2g"])
    for e in range(NE):
        P.emit("vector", lambda v, e=e: v.tensor_tensor_scan(
            out=c13[:, :, e], data0=o13[:, :, e], data1=o13[:, :, e],
            initial=0.0, op0=ALU.mult, op1=ALU.add),
            reads=["oh1"], writes=["scan1"])
        P.emit("vector", lambda v, e=e: v.tensor_tensor_scan(
            out=c23[:, :, e], data0=o23[:, :, e], data1=o23[:, :, e],
            initial=0.0, op0=ALU.mult, op1=ALU.add),
            reads=["oh2"], writes=["scan2"])
    P.emit("vector", lambda v: v.reduce_sum(out=cnt1[:], in_=o13.transpose([0, 2, 1]),
        axis=AX.X), reads=["oh1"], writes=["cnt1"])
    P.emit("vector", lambda v: v.reduce_sum(out=cnt2[:], in_=o23.transpose([0, 2, 1]),
        axis=AX.X), reads=["oh2"], writes=["cnt2"])
    P.emit("tensor", lambda t: t.matmul(
        PR1[0:128, 0:NE], ltm_f[:], cnt1[:], start=True, stop=True),
        reads=["ltm_f", "cnt1"], writes=["PR1"])
    P.emit("vector", lambda v: v.tensor_copy(out=base1[:], in_=PR1[0:128, 0:NE]),
        reads=["PR1"], writes=["base1"])
    P.emit("tensor", lambda t: t.matmul(
        PR1[0:1, 32:32 + NE], ones_f[:, 0:1], cnt1[:], start=True, stop=True),
        reads=["ones_f", "cnt1"], writes=["PR1"])
    P.emit("vector", lambda v: v.tensor_copy(out=tot1r[:], in_=PR1[0:1, 32:32 + NE]),
        reads=["PR1"], writes=["tot1r"])
    P.emit("tensor", lambda t: t.matmul(
        PR2[0:128, 0:NE], ltm_f[:], cnt2[:], start=True, stop=False),
        reads=["ltm_f", "cnt2"], writes=["PR2"])
    P.emit("tensor", lambda t: t.matmul(
        PR2[0:128, 0:NE], ones_f[0:1, :], tot1r[:], start=False, stop=True),
        reads=["ones_f", "tot1r"], writes=["PR2"])
    P.emit("vector", lambda v: v.tensor_copy(out=base2[:], in_=PR2[0:128, 0:NE]),
        reads=["PR2"], writes=["base2"])
    for (o3, c3, basex, locx, kres) in (
            (o13, c13, base1, loc1, ("oh1", "scan1", "base1", "loc1")),
            (o23, c23, base2, loc2, ("oh2", "scan2", "base2", "loc2"))):
        ohr, scr_r, baser, locr = kres
        P.emit("vector", lambda v, c3=c3, basex=basex, o3=o3: v.tensor_tensor(
            out=s3, in0=c3,
            in1=basex[:].rearrange("p (o e) -> p o e", o=1).broadcast_to([128, NA, NE]),
            op=ALU.add), reads=[scr_r, baser], writes=["scr"])
        P.emit("vector", lambda v, o3=o3: v.tensor_tensor(
            out=scr[:], in0=scr[:], in1=(oh1[:] if o3 is o13 else oh2[:]),
            op=ALU.mult), reads=["scr", ohr], writes=["scr"])
        P.emit("vector", lambda v, locx=locx: v.reduce_sum(
            out=locx[:], in_=s3, axis=AX.X), reads=["scr"], writes=[locr])
    P.emit("vector", lambda v: v.tensor_scalar(
        out=keep1[:], in0=loc1[:], scalar1=float(CAP), scalar2=None, op0=ALU.is_le),
        reads=["loc1"], writes=["keep1"])
    P.emit("vector", lambda v: v.tensor_scalar(
        out=keep2[:], in0=loc2[:], scalar1=float(CAP), scalar2=None, op0=ALU.is_le),
        reads=["loc2"], writes=["keep2"])
    P.emit("vector", lambda v: v.tensor_tensor(
        out=e2g[:], in0=e2g[:], in1=keep2[:], op=ALU.mult),
        reads=["e2g", "keep2"], writes=["e2g"])
    P.emit("vector", lambda v: v.tensor_tensor(
        out=gden[:], in0=keep1[:], in1=e2g[:], op=ALU.add),
        reads=["keep1", "e2g"], writes=["gden"])
    P.emit("vector", lambda v: v.tensor_scalar_max(
        out=gden[:], in0=gden[:], scalar1=1e-9), reads=["gden"], writes=["gden"])
    P.emit("vector", lambda v: v.reciprocal(out=gden[:], in_=gden[:]),
        reads=["gden"], writes=["gden"])
    P.emit("vector", lambda v: v.tensor_tensor(
        out=keep1[:], in0=keep1[:], in1=gden[:], op=ALU.mult),
        reads=["keep1", "gden"], writes=["keep1"])
    P.emit("vector", lambda v: v.tensor_tensor(
        out=e2g[:], in0=e2g[:], in1=gden[:], op=ALU.mult),
        reads=["e2g", "gden"], writes=["e2g"])
    P.emit("vector", lambda v: v.tensor_tensor(
        out=coef[:], in0=oh1[:],
        in1=keep1[:].rearrange("p (a o) -> p a o", o=1).broadcast_to([128, NA, NE]),
        op=ALU.mult), reads=["oh1", "keep1"], writes=["coef"])
    P.emit("vector", lambda v: v.tensor_tensor(
        out=scr[:], in0=oh2[:],
        in1=e2g[:].rearrange("p (a o) -> p a o", o=1).broadcast_to([128, NA, NE]),
        op=ALU.mult), reads=["oh2", "e2g"], writes=["scr"])
    P.emit("vector", lambda v: v.tensor_tensor(
        out=coef[:], in0=coef[:], in1=scr[:], op=ALU.add),
        reads=["coef", "scr"], writes=["coef"])
    sel_d = nc.dram_tensor("sel", [128, NPOWN], F32, kind="ExternalInput").ap()
    sel_sb = sbR("sel_sb", [128, NPOWN], F32)
    owncoef = sbR("owncoef", [NPOWN, NA * NE], F32)
    P.emit("sync", lambda s: s.dma_start(out=sel_sb[:], in_=sel_d[:, :]),
           writes=["sel_sb"], dma=True, stream="wsmall2")
    for j in range(NA * NE // 512):
        P.emit("tensor", lambda t, j=j: t.matmul(
            PR1[0:NPOWN, 0:512], sel_sb[:], coef[:, j * 512:(j + 1) * 512],
            start=True, stop=True), reads=["sel_sb", "coef"], writes=["PR1"])
        P.emit("vector", lambda v, j=j: v.tensor_copy(
            out=owncoef[:, j * 512:(j + 1) * 512], in_=PR1[0:NPOWN, 0:512]),
            reads=["PR1"], writes=["owncoef"])
    oc3 = owncoef[:].rearrange("p (a e) -> p a e", e=NE)
    ocrow = sbR("ocrow", [NPOWN, 512], F32)
    for e in range(NE):
        P.emit("vector", lambda v, e=e: v.tensor_copy(
            out=ocrow[:], in_=oc3[:, :, e]), reads=["owncoef"], writes=["ocrow"])
        P.emit("gpsimd", lambda g, e=e: g.dma_start(
            out=coef_d[e, :, :], in_=ocrow[:]),
            reads=["ocrow"], dma=True, stream="coef")

    P.base_waits = P.barrier_token()
    esP1.close()
    esR.close()

    NTB = TOK // 512
    esM = ExitStack()

    def sbM(name, shape, dtype):
        return esM.enter_context(nc.sbuf_tensor(name, shape, dtype))

    zero_sb = sbM("zero_sb", [128, 2048], F32)
    P.emit("vector", lambda v: v.memset(zero_sb[:], 0.0), writes=["zero_sb"])
    for db in range(8):
        for j in range(TOK // 2048):
            P.emit("gpsimd", lambda g, db=db, j=j: g.dma_start(
                out=moe_d[db, :, j * 2048:(j + 1) * 2048], in_=zero_sb[:]),
                reads=["zero_sb"], writes=["moe_dram"], dma=True, stream="mz")

    wi_all = sbM("wi_all", [128, 8 * 4096], BF16)
    actbuf = sbM("actbuf", [128, 16 * 512], BF16)
    ga_sb = sbM("ga_sb", [128, 512], BF16)
    xnT1m = sbM("xnT1m", [128, 8 * 512], BF16)
    xm3 = xnT1m[:].rearrange("p (c t) -> p c t", c=8)
    wo_all = sbM("wo_all", [128, 16 * 1024], BF16)
    coefrow = sbM("coefrow", [1, TOK], F32)
    coefbc = sbM("coefbc", [128, 512], F32)
    mev = sbM("mev", [128, 2 * 512], F32)

    esP2 = ExitStack()
    PM1 = esP2.enter_context(nc.psum_tensor("PM1", [128, 512], F32))
    PM2 = esP2.enter_context(nc.psum_tensor("PM2", [128, 512], F32))
    PDB = esP2.enter_context(nc.psum_tensor("PDB", [128, 4 * 512], F32))


    for e in range(NE):
        P.emit("sync", lambda s, e=e: s.dma_start(
            out=coefrow[0:1, :].rearrange("o (p a) -> o p a", a=512),
            in_=coef_d[e].rearrange("(o p) a -> o p a", o=1)),
            writes=["coefrow"], dma=True, stream="coef2")
        for hc in range(8):
            P.emit("gpsimd", lambda g, e=e, hc=hc: g.dma_start(
                out=wi_all[:, hc * 4096:(hc + 1) * 4096],
                in_=wi_d[e, hc * 128:(hc + 1) * 128, :]),
                writes=["wi_all"], dma=True, stream="wi")
        for fc in range(16):
            P.emit("gpsimd", lambda g, e=e, fc=fc: g.dma_start(
                out=wo_all[:, fc * 1024:(fc + 1) * 1024],
                in_=wo_d[e, fc * 128:(fc + 1) * 128, :]),
                writes=["wo_all"], dma=True, stream="wo2")
        for tb in range(NTB):
            for c in range(8):
                P.emit("sync", lambda s, tb=tb, c=c: s.dma_start(
                    out=xm3[:, c, :],
                    in_=xn1_d[tb * 512:(tb + 1) * 512, c * 128:(c + 1) * 128],
                    transpose=True), writes=["xnT1m"], dma=True, stream="xt")
            P.emit("tensor", lambda t, tb=tb: t.matmul(
                PM1[:, 0:512], ones_f[0:1, :], coefrow[0:1, tb * 512:(tb + 1) * 512],
                start=True, stop=True), reads=["ones_f", "coefrow"], writes=["PM1"])
            P.emit("vector", lambda v: v.tensor_copy(out=coefbc[:], in_=PM1[:, 0:512]),
                reads=["PM1"], writes=["coefbc"])
            for fc in range(16):
                for hc in range(8):
                    P.emit("tensor", lambda t, fc=fc, hc=hc: t.matmul(
                        PM1[:, 0:512],
                        wi_all[:, hc * 4096 + fc * 128:hc * 4096 + (fc + 1) * 128],
                        xm3[:, hc, :], start=(hc == 0), stop=(hc == 7)),
                        reads=["wi_all", "xnT1m"], writes=["PM1"])
                for hc in range(8):
                    P.emit("tensor", lambda t, fc=fc, hc=hc: t.matmul(
                        PM2[:, 0:512],
                        wi_all[:, hc * 4096 + (fc + 16) * 128:hc * 4096 + (fc + 17) * 128],
                        xm3[:, hc, :], start=(hc == 0), stop=(hc == 7)),
                        reads=["wi_all", "xnT1m"], writes=["PM2"])
                P.emit("scalar", lambda sc: sc.activation(
                    out=ga_sb[:], in_=PM1[:, 0:512], func=AF.Gelu),
                    reads=["PM1"], writes=["ga_sb"])
                P.emit("vector", lambda v, fc=fc: v.tensor_tensor(
                    out=actbuf[:, fc * 512:(fc + 1) * 512], in0=ga_sb[:],
                    in1=PM2[:, 0:512], op=ALU.mult),
                    reads=["ga_sb", "PM2"], writes=["actbuf"])
            for half in range(2):
                for fc in range(16):
                    for dbl in range(4):
                        db = half * 4 + dbl
                        P.emit("tensor", lambda t, fc=fc, db=db, dbl=dbl: t.matmul(
                            PDB[:, dbl * 512:(dbl + 1) * 512],
                            wo_all[:, fc * 1024 + db * 128:fc * 1024 + (db + 1) * 128],
                            actbuf[:, fc * 512:(fc + 1) * 512],
                            start=(fc == 0), stop=(fc == 15)),
                            reads=["wo_all", "actbuf"], writes=[f"PDB{dbl}"])
                for dbl in range(4):
                    db = half * 4 + dbl
                    sl = dbl % 2
                    P.emit("vector", lambda v, dbl=dbl, sl=sl: v.tensor_tensor(
                        out=mev[:, sl * 512:(sl + 1) * 512],
                        in0=PDB[:, dbl * 512:(dbl + 1) * 512], in1=coefbc[:],
                        op=ALU.mult),
                        reads=[f"PDB{dbl}", "coefbc"], writes=[f"mev{sl}"])
                    P.emit("gpsimd", lambda g, db=db, tb=tb, sl=sl: g.dma_start(
                        out=moe_d[db, :, tb * 512:(tb + 1) * 512],
                        in_=mev[:, sl * 512:(sl + 1) * 512], accum_op=ALU.add),
                        reads=[f"mev{sl}"], writes=["moe_dram"], dma=True,
                        stream="macc")

    P.base_waits = P.barrier_token()
    esP2.close()
    esM.close()

    w1_sb = sb("w1_sb", [128, 8 * 3072], BF16)
    wp_sb = sb("wp_sb", [128, 8 * 1024], BF16)
    bcol = sb("bcol", [128, 24], F32)
    bpcol = sb("bpcol", [128, 8], F32)
    brow = sb("brow", [1, 3072], F32)
    xnT1a = sb("xnT1a", [128, 8 * T], BF16)
    xa3 = xnT1a[:].rearrange("p (c t) -> p c t", c=8)
    qT2_sb = sb("qT2_sb", [H0, T], BF16)
    kT2_sb = sb("kT2_sb", [H0, T], BF16)
    vtm_sb = sb("vtm_sb", [128, 16 * 64], BF16)
    pbuf = sb("pbuf", [128, T], BF16)
    pT_sb = sb("pT_sb", [128, 16 * 512], BF16)
    avT_sb = sb("avT_sb", [128, 8 * T], BF16)
    lsum = sb("lsum", [128, 4], F32)
    negm = sb("negm", [128, 1], F32)
    rlrow = sb("rlrow", [1, 512], F32)
    rlbc = sb("rlbc", [H0, 512], F32)
    sum_fm2 = sb("sum_fm2", [128, 8 * 512], F32)
    moe_sb = sb("moe_sb", [128, 2 * 512], F32)
    xrow2 = sb("xrow2", [128, D], F32)
    outtok = sb("outtok", [128, D], F32)

    for c in range(8):
        P.emit("gpsimd", lambda g, c=c: g.dma_start(
            out=w1_sb[:, c * 3072:(c + 1) * 3072],
            in_=w1_d[c * 128:(c + 1) * 128, :]), writes=["w1_sb"], dma=True,
            stream="wi")
        P.emit("gpsimd", lambda g, c=c: g.dma_start(
            out=wp_sb[:, c * 1024:(c + 1) * 1024],
            in_=wp_d[c * 128:(c + 1) * 128, :]), writes=["wp_sb"], dma=True,
            stream="wi")
    P.emit("sync", lambda s: s.dma_start(
        out=brow[0:1, :], in_=b1_d[:].rearrange("(o f) -> o f", o=1)),
        writes=["brow"], dma=True, stream="coef2")
    for cb in range(24):
        P.emit("tensor", lambda t, cb=cb: t.transpose(
            PQK[0:128, 0:1], brow[0:1, cb * 128:(cb + 1) * 128], ident_f[0:1, 0:1]),
            reads=["brow", "ident_f"], writes=["PQK"])
        P.emit("vector", lambda v, cb=cb: v.tensor_copy(
            out=bcol[:, cb:cb + 1], in_=PQK[0:128, 0:1]),
            reads=["PQK"], writes=["bcol"])
    P.emit("sync", lambda s: s.dma_start(
        out=brow[0:1, 0:1024], in_=bp_d[:].rearrange("(o f) -> o f", o=1)),
        writes=["brow"], dma=True, stream="coef2")
    for cb in range(8):
        P.emit("tensor", lambda t, cb=cb: t.transpose(
            PQK[0:128, 0:1], brow[0:1, cb * 128:(cb + 1) * 128], ident_f[0:1, 0:1]),
            reads=["brow", "ident_f"], writes=["PQK"])
        P.emit("vector", lambda v, cb=cb: v.tensor_copy(
            out=bpcol[:, cb:cb + 1], in_=PQK[0:128, 0:1]),
            reads=["PQK"], writes=["bpcol"])

    PS = psum("PS", [128, 2048], F32)
    PEb2 = psum("PEb2", [128, 1024], BF16)
    PAV = psum("PAV", [128, 512], F32)
    PQK = psum("PQK", [128, 512], F32)

    for b in range(B_LOC):
        row0 = b * T
        for c in range(8):
            P.emit("sync", lambda s, b=b, c=c: s.dma_start(
                out=xa3[:, c, :],
                in_=xn1_d[row0:row0 + T, c * 128:(c + 1) * 128],
                transpose=True), writes=["xnT1a"], dma=True, stream="xt")
        for hh in range(NH):
            qcol = hh * 192
            kcol = hh * 192 + 64
            vcol = hh * 192 + 128
            for (col, dstbuf, nm) in ((qcol, qT2_sb, "qk_q"), (kcol, kT2_sb, "qk_k")):
                cb = col // 128
                off = col % 128
                for tg in range(4):
                    for c in range(8):
                        P.emit("tensor", lambda t, col=col, tg=tg, c=c: t.matmul(
                            PQK[0:64, 0:512],
                            w1_sb[:, c * 3072 + col:c * 3072 + col + 64],
                            xa3[:, c, tg * 512:(tg + 1) * 512],
                            start=(c == 0), stop=(c == 7)),
                            reads=["w1_sb", "xnT1a"], writes=["PQK"])
                    P.emit("scalar", lambda sc, dstbuf=dstbuf, tg=tg, cb=cb, off=off: sc.activation(
                        out=dstbuf[:, tg * 512:(tg + 1) * 512],
                        in_=PQK[0:64, 0:512], func=AF.Identity,
                        bias=bcol[off:off + 64, cb:cb + 1]),
                        reads=["PQK", "bcol"], writes=[nm])
            for tg in range(4):
                for c in range(8):
                    P.emit("tensor", lambda t, vcol=vcol, tg=tg, c=c: t.matmul(
                        PQK[0:64, 0:512],
                        w1_sb[:, c * 3072 + vcol:c * 3072 + vcol + 64],
                        xa3[:, c, tg * 512:(tg + 1) * 512],
                        start=(c == 0), stop=(c == 7)),
                        reads=["w1_sb", "xnT1a"], writes=["PQK"])
                vcb = (vcol) // 128
                voff = vcol % 128
                P.emit("scalar", lambda sc, tg=tg, vcb=vcb, voff=voff: sc.activation(
                    out=pbuf[0:64, tg * 512:(tg + 1) * 512], in_=PQK[0:64, 0:512],
                    func=AF.Identity, bias=bcol[voff:voff + 64, vcb:vcb + 1]),
                    reads=["PQK", "bcol"], writes=["pbuf"])
            for tt in range(16):
                P.emit("tensor", lambda t, tt=tt: t.transpose(
                    PEb2[:, 0:64], pbuf[0:64, tt * 128:(tt + 1) * 128],
                    ident_b[0:64, 0:64]),
                    reads=["pbuf", "ident_b", "vtm"], writes=["PEb2"])
                P.emit("vector", lambda v, tt=tt: v.tensor_copy(
                    out=vtm_sb[:, tt * 64:(tt + 1) * 64], in_=PEb2[0:128, 0:64]),
                    reads=["PEb2"], writes=["vtm"])
            for qi in range(16):
                kmax = (qi + 1) * 128
                nkb = ceil_div(kmax, 512)
                for kb in range(nkb):
                    klen = min(512, kmax - kb * 512)
                    P.emit("tensor", lambda t, qi=qi, kb=kb, klen=klen: t.matmul(
                        PS[:, kb * 512:kb * 512 + klen],
                        qT2_sb[:, qi * 128:(qi + 1) * 128],
                        kT2_sb[:, kb * 512:kb * 512 + klen],
                        start=True, stop=True),
                        reads=["qk_q", "qk_k"], writes=["PS"])
                P.emit("vector", lambda v, qi=qi: v.tensor_tensor(
                    out=PS[:, qi * 128:(qi + 1) * 128],
                    in0=PS[:, qi * 128:(qi + 1) * 128], in1=cmask[:], op=ALU.add),
                    reads=["PS", "cmask"], writes=["PS"])
                P.emit("vector", lambda v, qi=qi, kmax=kmax: v.reduce_max(
                    out=negm[:], in_=PS[:, 0:kmax], axis=AX.X),
                    reads=["PS"], writes=["negm"])
                P.emit("vector", lambda v: v.tensor_scalar_mul(
                    out=negm[:], in0=negm[:], scalar1=-0.125),
                    reads=["negm"], writes=["negm"])
                sl4 = qi % 4
                P.emit("vector", lambda v, sl4=sl4: v.memset(
                    lsum[:, sl4:sl4 + 1], 0.0), writes=["lsum"])
                P.emit("scalar", lambda sc, kmax=kmax, sl4=sl4: sc.activation(
                    out=pbuf[:, 0:kmax], in_=PS[:, 0:kmax], func=AF.Exp,
                    bias=negm[:, 0:1], scale=0.125, accum_out=lsum[:, sl4:sl4 + 1]),
                    reads=["PS", "negm", "lsum"], writes=["pbuf", "lsum"])
                for kt in range(qi + 1, (qi // 4) * 4 + 4):
                    P.emit("vector", lambda v, kt=kt, qi=qi: v.memset(
                        pT_sb[:, kt * 512 + (qi % 4) * 128:kt * 512 + (qi % 4) * 128 + 128],
                        0.0), writes=["pT"])
                for kt in range(qi + 1):
                    P.emit("tensor", lambda t, kt=kt: t.transpose(
                        PEb2[:, 0:128], pbuf[:, kt * 128:(kt + 1) * 128], ident_b[:]),
                        reads=["pbuf", "ident_b", "pT"], writes=["PEb2"])
                    P.emit("vector", lambda v, kt=kt, qi=qi: v.tensor_copy(
                        out=pT_sb[:, kt * 512 + (qi % 4) * 128:kt * 512 + (qi % 4) * 128 + 128],
                        in_=PEb2[:, 0:128]), reads=["PEb2"], writes=["pT"])
                if qi % 4 == 3:
                    qg = qi // 4
                    nkt = (qg + 1) * 4
                    for kt in range(nkt):
                        P.emit("tensor", lambda t, kt=kt, nkt=nkt: t.matmul(
                            PAV[0:64, 0:512], vtm_sb[:, kt * 64:(kt + 1) * 64],
                            pT_sb[:, kt * 512:(kt + 1) * 512],
                            start=(kt == 0), stop=(kt == nkt - 1)),
                            reads=["vtm", "pT"], writes=["PAV"])
                    P.emit("vector", lambda v: v.reciprocal(
                        out=lsum[:, 0:4], in_=lsum[:, 0:4]),
                        reads=["lsum"], writes=["lsum"])
                    for j in range(4):
                        P.emit("tensor", lambda t, j=j: t.transpose(
                            PQK[0:1, 0:128], lsum[:, j:j + 1], ident_f[:]),
                            reads=["lsum", "ident_f"], writes=["PQK"])
                        P.emit("vector", lambda v, j=j: v.tensor_copy(
                            out=rlrow[0:1, j * 128:(j + 1) * 128], in_=PQK[0:1, 0:128]),
                            reads=["PQK"], writes=["rlrow"])
                    P.emit("tensor", lambda t: t.matmul(
                        PQK[0:64, 0:512], ones_f[0:1, 0:64], rlrow[0:1, :],
                        start=True, stop=True),
                        reads=["ones_f", "rlrow"], writes=["PQK"])
                    P.emit("vector", lambda v: v.tensor_copy(
                        out=rlbc[:], in_=PQK[0:64, 0:512]),
                        reads=["PQK"], writes=["rlbc"])
                    hslot = hh % 2
                    P.emit("vector", lambda v, hh=hh, qg=qg, hslot=hslot: v.tensor_tensor(
                        out=avT_sb[hslot * 64:hslot * 64 + 64,
                                   (hh // 2) * T + qg * 512:(hh // 2) * T + (qg + 1) * 512],
                        in0=PAV[0:64, 0:512], in1=rlbc[:], op=ALU.mult),
                        reads=["PAV", "rlbc"], writes=["avT"])
        for qg in range(4):
            tslc = slice(qg * 512, (qg + 1) * 512)
            def emit_moeload(db):
                msl = db % 2
                P.emit("sync", lambda s, db=db, b=b, qg=qg, msl=msl: s.dma_start(
                    out=moe_sb[:, msl * 512:(msl + 1) * 512],
                    in_=moe_d[db, :, b * T + qg * 512:b * T + (qg + 1) * 512]),
                    writes=[f"moe_sb{msl}"], dma=True, stream="moin")
            emit_moeload(0)
            for db in range(8):
                if db + 1 < 8:
                    emit_moeload(db + 1)
                for hh in range(NH):
                    P.emit("tensor", lambda t, db=db, hh=hh, qg=qg: t.matmul(
                        PQK[:, 0:512],
                        wp_sb[(hh % 2) * 64:(hh % 2) * 64 + 64,
                              (hh // 4) * 2048 + 0:0 + 0] if False else
                        wp_sb[64 * (hh % 2):64 * (hh % 2) + 64,
                              (hh // 2) * 1024 + db * 128:(hh // 2) * 1024 + (db + 1) * 128],
                        avT_sb[(hh % 2) * 64:(hh % 2) * 64 + 64,
                               (hh // 2) * T + qg * 512:(hh // 2) * T + (qg + 1) * 512],
                        start=(hh == 0), stop=(hh == NH - 1)),
                        reads=["wp_sb", "avT"], writes=["PQK"])
                msl = db % 2
                P.emit("vector", lambda v, db=db: v.tensor_scalar(
                    out=sum_fm2[:, db * 512:(db + 1) * 512], in0=PQK[:, 0:512],
                    scalar1=bpcol[:, db:db + 1], scalar2=None, op0=ALU.add),
                    reads=["PQK", "bpcol"], writes=[f"sf{db}"])
                P.emit("vector", lambda v, db=db, msl=msl: v.tensor_tensor(
                    out=sum_fm2[:, db * 512:(db + 1) * 512],
                    in0=sum_fm2[:, db * 512:(db + 1) * 512],
                    in1=moe_sb[:, msl * 512:(msl + 1) * 512], op=ALU.add),
                    reads=[f"sf{db}", f"moe_sb{msl}"], writes=[f"sf{db}"])
            for ts in range(4):
                rows = slice(row0 + qg * 512 + ts * 128, row0 + qg * 512 + ts * 128 + 128)
                for dc in range(8):
                    P.emit("tensor", lambda t, dc=dc, ts=ts: t.transpose(
                        PS[:, dc * 128:(dc + 1) * 128],
                        sum_fm2[:, dc * 512 + ts * 128:dc * 512 + (ts + 1) * 128],
                        ident_f[:]),
                        reads=[f"sf{dc}", "ident_f", "outtok"], writes=["PS"])
                P.emit("gpsimd", lambda s, rows=rows: s.dma_start(
                    out=xrow2[:], in_=x1_d[rows, :]), writes=["xrow2"], dma=True,
                    stream="xin")
                P.emit("vector", lambda v: v.tensor_tensor(
                    out=outtok[:], in0=xrow2[:], in1=PS[:, 0:1024], op=ALU.add),
                    reads=["xrow2", "PS"], writes=["outtok"])
                P.emit("gpsimd", lambda g, rows=rows: g.dma_start(
                    out=out_d[rows, :], in_=outtok[:]), reads=["outtok"], dma=True,
                    stream="ost")

    toks = P.barrier_token()
    P.emit("vector", lambda v: v.engine_nop(), waits=toks)
    blk = es.enter_context(nc.Block())
    P.finalize(blk)
    es.close()
    return nc

_NC1 = None
_NC2 = None


def kernel(L0_attn_x, L0_attn_W_q, L0_attn_W_k, L0_attn_W_v, L0_attn_W_o,
           L0_ffn_w_up, L0_ffn_w_down, L1_attn_W1, L1_attn_b1, L1_attn_Wp,
           L1_attn_bp, L1_ffn_Wg, L1_ffn_W_in, L1_ffn_W_out):
    global _NC1, _NC2
    x = np.ascontiguousarray(L0_attn_x.reshape(NTOK_ALL, D))
    if _NC1 is None:
        _NC1 = build_launch1()
    in1 = []
    for c in range(N_CORES):
        in1.append({
            "x": x[c * TOK:(c + 1) * TOK],
            "wq": L0_attn_W_q, "wk": L0_attn_W_k, "wv": L0_attn_W_v,
            "wo": L0_attn_W_o, "wup": L0_ffn_w_up, "wdn": L0_ffn_w_down,
            "wg": L1_ffn_Wg,
        })
    r1 = run_bass_kernel_spmd(_NC1, in1, core_ids=list(range(N_CORES)))
    lgt_full = np.concatenate([r["lgt"] for r in r1.results], 0)
    if _NC2 is None:
        _NC2 = build_launch2()
    in2 = []
    for c in range(N_CORES):
        npown = TOK // 512
        sel = np.zeros((128, npown), np.float32)
        for j in range(npown):
            sel[c * npown + j, j] = 1.0
        in2.append({
            "x1": r1.results[c]["x1"], "xn1": r1.results[c]["xn1"],
            "lgt": lgt_full, "sel": sel,
            "w1": L1_attn_W1, "b1": L1_attn_b1, "wp": L1_attn_Wp,
            "bp": L1_attn_bp, "wi": L1_ffn_W_in, "wo": L1_ffn_W_out,
        })
    r2 = run_bass_kernel_spmd(_NC2, in2, core_ids=list(range(N_CORES)))
    out = np.concatenate([r["out"] for r in r2.results], 0)
    return out.reshape(B_ALL, T, D)


if __name__ == "__main__":
    nc = build_launch1()
    print("launch1 built ok; instructions:",
          sum(len(f.instructions) for f in []) if False else "n/a")
